# version 15
# baseline (speedup 1.0000x reference)
import math
from contextlib import ExitStack

import numpy as np
import ml_dtypes

import concourse.bass as bass
import concourse.mybir as mybir
from concourse.bass_utils import run_bass_kernel_spmd

F32 = mybir.dt.float32
BF16 = mybir.dt.bfloat16
AF = mybir.ActivationFunctionType
ALU = mybir.AluOpType
AX = mybir.AxisListType
bfdt = ml_dtypes.bfloat16

B, SKV, D, H, HD = 4, 4096, 1024, 16, 64
OWN = 2048
W1 = OWN + 1
QT = 512
NQT = OWN // QT
KC = SKV // 128
NG = SKV // 512
EPS = 1e-5

ENG_SEM = {"t": "sT", "v": "sV", "s": "sS", "y": "sY", "g": "sG"}
ENG_ATTR = {"t": "tensor", "v": "vector", "s": "scalar", "y": "sync", "g": "gpsimd"}
DMA_ENG = ("y", "g")


class Em:

    def __init__(self, sems, counters, lastw, gate=()):
        self.sems = sems
        self.c = counters
        self.lastw = lastw
        self.gate = list(gate)
        self.ops = {e: [] for e in ENG_SEM}

    def st(self, eng, fns, deps=()):
        if not isinstance(fns, (list, tuple)):
            fns = [fns]
        sem = ENG_SEM[eng]
        waits = []
        for d in list(deps) + self.gate:
            if d is None:
                continue
            dsem, dval = d
            if dsem == sem and eng not in DMA_ENG:
                continue
            key = (eng, dsem)
            if self.lastw.get(key, -1) >= dval:
                continue
            self.lastw[key] = dval
            waits.append((dsem, dval))
        pre = self.c[sem]
        if eng in DMA_ENG:
            if pre > 0 and self.lastw.get((eng, sem), -1) < pre:
                self.lastw[(eng, sem)] = pre
                waits.append((sem, pre))
            self.c[sem] += 16 * len(fns)
        else:
            self.c[sem] += len(fns)
        self.ops[eng].append((waits, list(fns), eng in DMA_ENG, pre))
        return (sem, self.c[sem])

    def raw(self, eng, waits, fns, inc):
        sem = ENG_SEM[eng]
        self.c[sem] += inc
        self.ops[eng].append((waits, list(fns), False, ("raw", inc)))
        return (sem, self.c[sem])

    def play(self, eng, engobj):
        sem = self.sems[ENG_SEM[eng]]
        own = ENG_SEM[eng]
        serial = eng in ("v", "s")
        for waits, fns, is_dma, pre in self.ops[eng]:
            for (s, v) in waits:
                engobj.wait_ge(self.sems[s], v)
            if is_dma:
                for f in fns:
                    f(engobj).then_inc(sem, 16)
            elif isinstance(pre, tuple):
                insts = [f(engobj) for f in fns]
                if insts and pre[1]:
                    insts[-1].then_inc(sem, pre[1])
            else:
                for i, f in enumerate(fns):
                    if serial and pre + i > 0:
                        engobj.wait_ge(sem, pre + i)
                    f(engobj).then_inc(sem, 1)

    def run(self, nc):
        with nc.Block() as blk:
            for eng in ("y", "g", "t", "v", "s"):
                if self.ops[eng]:
                    getattr(blk, ENG_ATTR[eng])(
                        lambda e, _eng=eng: self.play(_eng, e))


def build_program(dbg=False):
    nc = bass.Bass("TRN2", target_bir_lowering=False)

    def din(name, shape, dt=F32):
        return nc.dram_tensor(name, shape, dt, kind="ExternalInput").ap()

    xkv = din("xkv", [SKV, D])
    xq = din("xq", [W1, D])
    wq = din("wq", [D, D])
    wk = din("wk", [D, D])
    wv = din("wv", [D, D])
    wo = din("wo", [D, D])
    wr = din("wr", [D, D])
    wkf = din("wkf", [D, D])
    wvf = din("wvf", [D, D])
    mur_i = din("mur", [128, 8])
    muk_i = din("muk", [128, 8])
    trigk_i = din("trigk", [128, SKV], F32)
    trigq_i = din("trigq", [128, W1], F32)
    identb_i = din("identb", [128, 128], BF16)
    identf_i = din("identf", [128, 128], F32)
    u0m_i = din("u0m", [128, 2])
    bmask_i = din("bmask", [1, 1])

    out_d = nc.dram_tensor("out", [OWN, D], F32, kind="ExternalOutput").ap()
    skind = "ExternalOutput" if dbg else "Internal"
    x2_d = nc.dram_tensor("x2d", [OWN, D], F32, kind=skind).ap()
    ukhi_d = nc.dram_tensor("ukhi", [128, 4, SKV], BF16, kind="Internal").ap()
    vhi_d = nc.dram_tensor("vhi", [SKV, 512], BF16, kind="Internal").ap()
    if dbg:
        dctx_d = nc.dram_tensor("dctx", [128, 8, OWN], BF16,
                                kind="ExternalOutput").ap()
        dx2r0_d = nc.dram_tensor("dx2r0", [1, D], F32, kind="ExternalOutput").ap()
        dU0c_d = nc.dram_tensor("dU0c", [2, 128, 4, 8], BF16,
                                kind="ExternalOutput").ap()
        dP0_d = nc.dram_tensor("dP0", [2, 8, SKV], BF16,
                               kind="ExternalOutput").ap()
        dc0_d = nc.dram_tensor("dc0", [2, 1, 512], BF16,
                               kind="ExternalOutput").ap()
        dct_d = nc.dram_tensor("dct", [128, 8], BF16,
                               kind="ExternalOutput").ap()
        dat0_d = nc.dram_tensor("dat0", [1, D], F32,
                                kind="ExternalOutput").ap()
        dcolb_d = nc.dram_tensor("dcolb", [128, 8], F32,
                                 kind="ExternalOutput").ap()

    sems = {}
    counters = {s: 0 for s in ENG_SEM.values()}
    lastw = {}
    war = {}

    gate_hold = [[]]

    def em():
        return Em(sems, counters, lastw, gate_hold[0])

    def set_gate():
        gate_hold[0] = [(s, counters[s]) for s in counters if counters[s] > 0]

    with ExitStack() as top:
        for s in ("sT", "sV", "sS", "sY", "sG"):
            sems[s] = top.enter_context(nc.semaphore(s))

        def sb(name, shape, dt, ctx):
            return ctx.enter_context(nc.sbuf_tensor("s_" + name, shape, dt))

        def ps(name, shape, dt, ctx):
            return ctx.enter_context(nc.psum_tensor(name, shape, dt))

        identb = sb("identb", [128, 128], BF16, top)
        identf = sb("identf", [128, 128], F32, top)
        u0m = sb("u0m", [128, 2], F32, top)
        mur = sb("mur", [128, 8], F32, top)
        muk = sb("muk", [128, 8], F32, top)
        omur = sb("omur", [128, 8], F32, top)
        omuk = sb("omuk", [128, 8], F32, top)
        bmask = sb("bmask", [1, 1], F32, top)
        junk = sb("junk", [128, 1024], BF16, top)
        stat = sb("stat", [128, 80], F32, top)
        x2r0 = sb("x2r0", [1, D], F32, top)
        xq0 = sb("xq0", [1, D], F32, top)
        cbias = sb("cbias", [128, 2], F32, top)

        SX, SXX, MM, RS, M2 = 0, 16, 32, 48, 64

        def mcol(j, cb=0):
            return stat[:, MM + cb + j:MM + cb + j + 1]

        def rcol(j, cb=0):
            return stat[:, RS + cb + j:RS + cb + j + 1]

        def ln_stats(E, srcs, nsub, deps_each, n=1024.0, cb=0, pp=128):
            toks = []
            for j in range(nsub):
                t1 = E.st("v", lambda v, _j=j: v.tensor_scalar(
                    junk[0:pp, :], srcs(_j), 1.0, 0.0, ALU.mult, ALU.add,
                    accum_out=stat[0:pp, SX + cb + _j:SX + cb + _j + 1]),
                    deps=deps_each[j])
                t2 = E.st("v", lambda v, _j=j: v.scalar_tensor_tensor(
                    junk[0:pp, :], srcs(_j), 1.0, srcs(_j), ALU.mult, ALU.mult,
                    accum_out=stat[0:pp, SXX + cb + _j:SXX + cb + _j + 1]),
                    deps=deps_each[j])
                toks += [t1, t2]
            c0, c1 = cb, cb + nsub
            tm = E.st("v", [
                lambda v: v.tensor_scalar(stat[:, MM + c0:MM + c1],
                                          stat[:, SX + c0:SX + c1], 1.0 / n, 0.0,
                                          ALU.mult, ALU.add),
                lambda v: v.tensor_tensor(stat[:, M2 + c0:M2 + c1],
                                          stat[:, MM + c0:MM + c1],
                                          stat[:, MM + c0:MM + c1], ALU.mult),
                lambda v: v.tensor_scalar(stat[:, RS + c0:RS + c1],
                                          stat[:, SXX + c0:SXX + c1], 1.0 / n, 0.0,
                                          ALU.mult, ALU.add),
                lambda v: v.tensor_tensor(stat[:, RS + c0:RS + c1],
                                          stat[:, RS + c0:RS + c1],
                                          stat[:, M2 + c0:M2 + c1], ALU.subtract),
            ], deps=toks + [war.get("lnrs")])
            trs = E.st("s", [
                lambda s: s.activation(stat[:, RS + c0:RS + c1],
                                       stat[:, RS + c0:RS + c1], AF.Ln,
                                       bias=cbias[:, 0:1]),
                lambda s: s.activation(stat[:, RS + c0:RS + c1],
                                       stat[:, RS + c0:RS + c1], AF.Exp,
                                       scale=-0.5),
            ], deps=[tm])
            war["lnrs"] = trs
            return trs

        ap_scope = ExitStack()
        top.enter_context(ap_scope)
        uk_sb = sb("uk_sb", [128, 4, SKV], BF16, ap_scope)
        vbuf = sb("vbuf", [128, KC, 8, 65], BF16, ap_scope)
        a_scope = ExitStack()
        ap_scope.enter_context(a_scope)
        wk_sb = sb("wk_sb", [128, 8, 1024], BF16, a_scope)
        wv_sb = sb("wv_sb", [128, 8, 1024], BF16, a_scope)
        trigk = sb("trigk", [128, SKV], BF16, a_scope)
        xg = sb("xg", [128, 2, 4, 1024], F32, a_scope)
        h1g = sb("h1g", [128, 4, 1024], BF16, a_scope)
        h1gT = sb("h1gT", [128, 8, 512], BF16, a_scope)
        ukst = sb("ukst", [128, 2, 512], BF16, a_scope)
        vst = sb("vst", [128, 2, 512], BF16, a_scope)

        E = em()
        tok_const = E.st("y", [
            lambda y: y.dma_start(identb[:, :], identb_i),
            lambda y: y.dma_start(identf[:, :], identf_i),
            lambda y: y.dma_start(u0m[:, :], u0m_i),
            lambda y: y.dma_start(mur[:, :], mur_i),
            lambda y: y.dma_start(muk[:, :], muk_i),
            lambda y: y.dma_start(bmask[:, :], bmask_i),
            lambda y: y.dma_start(xq0[:, :], xq[0:1, :]),
        ])
        E.st("s", [
            lambda s: s.activation(omur[:, :], mur[:, :], AF.Identity,
                                   bias=1.0, scale=-1.0),
            lambda s: s.activation(omuk[:, :], muk[:, :], AF.Identity,
                                   bias=1.0, scale=-1.0),
        ], deps=[tok_const])
        _e1 = float(np.float32(EPS))
        _e2 = float(np.float32(EPS - _e1))
        tok_kn0 = E.st("v", [
            lambda v: v.memset(cbias[:, 0:1], _e1),
            lambda v: v.tensor_scalar(cbias[:, 0:1], cbias[:, 0:1], 1.0, _e2,
                                      ALU.mult, ALU.add),
            lambda v: v.memset(cbias[:, 1:2], 1e-30),
            lambda v: v.memset(stat[:, :], 0.0),
        ], deps=[])
        tok_vones = E.st("v", lambda v: v.memset(vbuf[:, :, :, 64:65], 1.0),
                         deps=[])
        tok_wkv = E.st("g", [
            lambda g: g.dma_start(
                wk_sb[:, :, :], wk.rearrange("(c p) o -> p c o", p=128)),
            lambda g: g.dma_start(
                wv_sb[:, :, :], wv.rearrange("(c p) o -> p c o", p=128)),
        ])
        tok_trigk = E.st("g", lambda g: g.dma_start(trigk[:, :], trigk_i))
        E.run(nc)

        E = em()
        a_ps = ExitStack()
        a_scope.enter_context(a_ps)
        ps_tp = [ps(f"ps_tpA{i}", [128, 1024], BF16, a_ps) for i in range(2)]
        ps_ko = [ps(f"ps_koA{i}", [128, 512], F32, a_ps) for i in range(2)]
        ps_v = ps("ps_vA", [128, 1024], F32, a_ps)

        def emit_xg_dma(g, dep):
            sl = g % 2
            return E.st("y", lambda y, _g=g, _sl=sl: y.dma_start(
                xg[:, _sl, :, :],
                xkv[512 * _g:512 * (_g + 1), :].rearrange("(a p) o -> p a o", p=128)),
                deps=[dep])

        tok_dma = {0: emit_xg_dma(0, tok_wkv), 1: emit_xg_dma(1, None)}
        for g in range(NG):
            sl = g % 2
            trs = ln_stats(E, lambda j, _sl=sl: xg[:, _sl, j, :], 4,
                           [[tok_dma[g]]] * 4)
            tok_norm = []
            for j in range(4):
                tok_norm.append(E.st("v", lambda v, _j=j, _sl=sl: v.tensor_scalar(
                    h1g[:, _j, :], xg[:, _sl, _j, :], mcol(_j), rcol(_j),
                    ALU.subtract, ALU.mult), deps=[trs]))
            if g + 2 < NG:
                tok_dma[g + 2] = emit_xg_dma(g + 2, tok_norm[-1])
            tok_ev = []
            for j in range(4):
                bk = j % 2
                tt = E.st("t", [
                    (lambda t, _j=j, _c=c, _bk=bk: t.matmul(
                        ps_tp[_bk][:, 128 * _c:128 * (_c + 1)],
                        h1g[:, _j, 128 * _c:128 * (_c + 1)], identb[:, :],
                        start=True, stop=True, is_transpose=True))
                    for c in range(8)
                ], deps=[tok_norm[j], war.get(("ps_tpA", bk)), tok_const])
                tev = E.st("v", lambda v, _j=j, _bk=bk: v.tensor_copy(
                    h1gT[:, :, 128 * _j:128 * (_j + 1)],
                    ps_tp[_bk][:, :].rearrange("p (c q) -> p c q", q=128)),
                    deps=[tt])
                war[("ps_tpA", bk)] = tev
                tok_ev.append(tev)
            tok_nm = None
            for o in range(8):
                bk = o % 2
                tmm = E.st("t", [
                    (lambda t, _o=o, _c=c, _bk=bk: t.matmul(
                        ps_ko[_bk][:, :], wk_sb[:, _c, 128 * _o:128 * (_o + 1)],
                        h1gT[:, _c, :], start=(_c == 0), stop=(_c == 7)))
                    for c in range(8)
                ], deps=tok_ev + [war.get(("ps_koA", bk)), tok_trigk])
                if o < 4:
                    dst = uk_sb[:, o, 512 * g:512 * (g + 1)]
                else:
                    dst = ukst[:, o % 2, :]
                tev = E.st("v", lambda v, _bk=bk, _g=g, _dst=dst: v.tensor_tensor(
                    _dst, ps_ko[_bk][:, :], trigk[:, 512 * _g:512 * (_g + 1)],
                    ALU.mult),
                    deps=[tmm, war.get(("ukst", o % 2)) if o >= 4 else None])
                war[("ps_koA", bk)] = tev
                if o >= 4:
                    tst = E.st("y", lambda y, _o=o, _g=g: y.dma_start(
                        ukhi_d[:, _o - 4, 512 * _g:512 * (_g + 1)],
                        ukst[:, _o % 2, :]), deps=[tev])
                    war[("ukst", o % 2)] = tst
            for j in range(4):
                kc = 4 * g + j
                fns = []
                for half in range(2):
                    for c in range(8):
                        fns.append(lambda t, _j=j, _c=c, _h=half: t.matmul(
                            ps_v[:, 512 * _h:512 * (_h + 1)],
                            h1gT[:, _c, 128 * _j:128 * (_j + 1)],
                            wv_sb[:, _c, 512 * _h:512 * (_h + 1)],
                            start=(_c == 0), stop=(_c == 7)))
                tvm = E.st("t", fns, deps=tok_ev + [war.get("ps_vA")])
                tlo = E.st("v", lambda v, _kc=kc: v.tensor_copy(
                    vbuf[:, _kc, :, 0:64],
                    ps_v[:, 0:512].rearrange("p (h d) -> p h d", d=64)),
                    deps=[tvm, tok_vones])
                thi = E.st("v", lambda v, _kc=kc: v.tensor_copy(
                    vst[:, _kc % 2, :], ps_v[:, 512:1024]),
                    deps=[tvm, war.get(("vst", kc % 2))])
                war["ps_vA"] = thi
                tvs = E.st("y", lambda y, _kc=kc: y.dma_start(
                    vhi_d[128 * _kc:128 * (_kc + 1), :], vst[:, _kc % 2, :]),
                    deps=[thi])
                war[("vst", kc % 2)] = tvs
        tok_A_ukst = (war[("ukst", 0)], war[("ukst", 1)])
        tok_A_vst = (war[("vst", 0)], war[("vst", 1)])
        E.run(nc)
        a_ps.close()
        a_scope.close()
        set_gate()

        p_scope = ExitStack()
        ap_scope.enter_context(p_scope)
        wo_sb = sb("wo_sb", [128, 8, 1024], BF16, p_scope)
        ctxF = sb("ctxF", [128, 8, OWN], BF16, p_scope)
        ctx0T = sb("ctx0T", [128, 16], BF16, p_scope)
        pa_scope = ExitStack()
        p_scope.enter_context(pa_scope)
        wq_sb = sb("wq_sb", [128, 8, 1024], BF16, pa_scope)
        trigq = sb("trigq", [128, W1], BF16, pa_scope)
        uqt = sb("uqt", [128, 4, 512], BF16, pa_scope)
        xqg = sb("xqg", [128, 4, 1024], F32, pa_scope)
        h1q = sb("h1q", [128, 4, 1024], BF16, pa_scope)
        h1qT = sb("h1qT", [128, 8, 512], BF16, pa_scope)
        pT = sb("pT", [128, 4, 1024], BF16, pa_scope)
        rl = sb("rl", [128, 1], F32, pa_scope)
        rden = sb("rden", [128, 4], F32, pa_scope)
        ctx_sb = sb("ctx_sb", [128, 4, 64], BF16, pa_scope)
        h1q0 = sb("h1q0", [1, 1024], BF16, pa_scope)
        h1q0T = sb("h1q0T", [128, 8], BF16, pa_scope)
        U0c = sb("U0c", [128, 4, 8], BF16, pa_scope)
        P0 = sb("P0", [8, SKV], BF16, pa_scope)
        P0T = sb("P0T", [128, KC, 8], BF16, pa_scope)
        vtmp = sb("vtmp", [128, 2, 512], BF16, pa_scope)
        c0row = sb("c0row", [1, 512], BF16, pa_scope)
        rc0 = sb("rc0", [8, 1], F32, pa_scope)

        pp_ps = ExitStack()
        pa_scope.enter_context(pp_ps)
        ps_s = ps("ps_s", [128, 2048], F32, pp_ps)
        ps_av = [ps(f"ps_av{i}", [128, 512], F32, pp_ps) for i in range(4)]

        for ppass in range(2):
            E = em()
            if ppass == 0:
                tok_w = E.st("g", lambda g: g.dma_start(
                    wq_sb[:, :, :], wq.rearrange("(c p) o -> p c o", p=128)))
                tok_trq = E.st("g", lambda g: g.dma_start(trigq[:, :], trigq_i))
                tok_ukrdy = tok_w
                tok_vrdy = tok_vones
            else:
                tok_w = E.st("y", [
                    (lambda y, _j=j: y.dma_start(uk_sb[:, _j, :], ukhi_d[:, _j, :]))
                    for j in range(4)
                ], deps=[tok_A_ukst[0], tok_A_ukst[1], war.get("uk_rd")])
                tok_ukrdy = tok_w
                vt = None
                for kc in range(KC):
                    tvd = E.st("y", lambda y, _kc=kc: y.dma_start(
                        vtmp[:, _kc % 2, :], vhi_d[128 * _kc:128 * (_kc + 1), :]),
                        deps=[tok_A_vst[0], tok_A_vst[1],
                              war.get(("vtmp", kc % 2))])
                    vt = E.st("v", lambda v, _kc=kc: v.tensor_copy(
                        vbuf[:, _kc, :, 0:64],
                        vtmp[:, _kc % 2, :].rearrange("p (h d) -> p h d", d=64)),
                        deps=[tvd, war.get("vbuf_rd")])
                    war[("vtmp", kc % 2)] = vt
                tok_vrdy = vt
                tok_trq = None

            for qt in range(NQT):
                r0 = 1 + QT * qt
                tq = E.st("y", lambda y, _r0=r0: y.dma_start(
                    xqg[:, :, :],
                    xq[_r0:_r0 + QT, :].rearrange("(a p) o -> p a o", p=128)),
                    deps=[war.get("xqg"), tok_trq])
                trs = ln_stats(E, lambda j: xqg[:, j, :], 4, [[tq]] * 4)
                tok_norm = []
                for j in range(4):
                    tok_norm.append(E.st("v", lambda v, _j=j: v.tensor_scalar(
                        h1q[:, _j, :], xqg[:, _j, :], mcol(_j), rcol(_j),
                        ALU.subtract, ALU.mult), deps=[trs]))
                war["xqg"] = tok_norm[-1]
                tok_ev = []
                for j in range(4):
                    hfj = j % 2
                    tt = E.st("t", [
                        (lambda t, _j=j, _c=c, _hf=hfj: t.matmul(
                            ps_s[:, 1024 * _hf + 128 * _c:
                                 1024 * _hf + 128 * (_c + 1)],
                            h1q[:, _j, 128 * _c:128 * (_c + 1)], identb[:, :],
                            start=True, stop=True))
                        for c in range(8)
                    ], deps=[tok_norm[j], war.get(("ps_s", hfj))])
                    tev = E.st("v", lambda v, _j=j, _hf=hfj: v.tensor_copy(
                        h1qT[:, :, 128 * _j:128 * (_j + 1)],
                        ps_s[:, 1024 * _hf:1024 * (_hf + 1)].rearrange(
                            "p (c q) -> p c q", q=128)),
                        deps=[tt])
                    war[("ps_s", hfj)] = tev
                    tok_ev.append(tev)
                tok_uq = []
                for oo in range(4):
                    o = 4 * ppass + oo
                    tmm = E.st("t", [
                        (lambda t, _o=o, _c=c, _oo=oo: t.matmul(
                            ps_av[_oo][:, :],
                            wq_sb[:, _c, 128 * _o:128 * (_o + 1)],
                            h1qT[:, _c, :], start=(_c == 0), stop=(_c == 7)))
                        for c in range(8)
                    ], deps=tok_ev + [war.get(("ps_av", oo)), tok_w])
                    tev = E.st("v", lambda v, _oo=oo, _r0=r0: v.tensor_tensor(
                        uqt[:, _oo, :], ps_av[_oo][:, :],
                        trigq[:, _r0:_r0 + QT],
                        ALU.mult), deps=[tmm, war.get("uqt")])
                    war[("ps_av", oo)] = tev
                    tok_uq.append(tev)

                for hh in range(8):
                    h = 8 * ppass + hh
                    oo, m = hh // 2, hh % 2
                    tok_exp = {}
                    tok_pv = {}
                    def _emit_pv(_gg):
                        _sl = _gg % 4
                        tok_pv[_gg] = E.st("t", [
                            (lambda t, _jj=jj, _qq=qq, _g=_gg, _hh=hh, _s=_sl:
                             t.matmul(
                                ps_av[_qq][:, 0:65],
                                pT[:, _s, 512 * _jj + 128 * _qq:
                                   512 * _jj + 128 * _qq + 128],
                                vbuf[:, 2 * _g + _jj, _hh, :],
                                start=(2 * _g + _jj == 0),
                                stop=(2 * _g + _jj == KC - 1)))
                            for jj in range(2) for qq in range(4)
                        ], deps=[tok_exp[_gg], tok_vrdy]
                           + ([war.get(("ps_av", q2)) for q2 in range(4)]
                              if _gg == 0 else []))
                        war[("pT", _sl)] = tok_pv[_gg]

                    for gg in range(16):
                        hf = gg % 2
                        sl = gg % 4
                        tqk = E.st("t", [
                            (lambda t, _jj=jj, _gg=gg, _oo=oo, _m=m, _hf=hf:
                             t.matmul(
                                ps_s[:, 1024 * _hf + 512 * _jj:
                                     1024 * _hf + 512 * (_jj + 1)],
                                uk_sb[64 * _m:64 * _m + 64, _oo,
                                      128 * (2 * _gg + _jj):
                                      128 * (2 * _gg + _jj) + 128],
                                uqt[64 * _m:64 * _m + 64, _oo, :],
                                start=True, stop=True,
                                tile_position=(64 * _m, 0)))
                            for jj in range(2)
                        ], deps=[tok_uq[oo], tok_ukrdy,
                                 war.get(("ps_s", hf))])
                        tok_exp[gg] = E.st("s", lambda s, _sl=sl, _h=h, _hf=hf:
                            s.activation(
                                pT[:, _sl, :],
                                ps_s[:, 1024 * _hf:1024 * (_hf + 1)], AF.Exp,
                                scale=0.125),
                            deps=[tqk, war.get(("pT", sl))])
                        war[("ps_s", hf)] = tok_exp[gg]
                        if gg >= 1:
                            _emit_pv(gg - 1)
                            E.st("t", [
                                (lambda t: t.ldweights(identb[:, :]))
                                for _ in range(4)
                            ], deps=[])
                    _emit_pv(15)
                    chunk = 4 * ppass + hh // 2
                    tok_dv = []
                    for qq in range(4):
                        tdv = E.st("v", [
                            (lambda v, _qq=qq: v.reciprocal(
                                rden[:, _qq:_qq + 1], ps_av[_qq][:, 64:65])),
                            (lambda v, _qq=qq: v.tensor_scalar(
                                ctx_sb[:, _qq, :], ps_av[_qq][:, 0:64],
                                rden[:, _qq:_qq + 1], 0.0, ALU.mult, ALU.add)),
                        ], deps=[tok_pv[15]])
                        war[("ps_av", qq)] = tdv
                        tok_dv.append(tdv)
                    ttx = E.st("t", [
                        (lambda t, _qq=qq, _m=m: t.matmul(
                            ps_av[_qq][64 * _m:64 * _m + 64, 128:256],
                            ctx_sb[:, _qq, :], identb[:, :],
                            start=True, stop=True))
                        for qq in range(4)
                    ], deps=tok_dv)
                    tcx = E.st("v", [
                        (lambda v, _ck=chunk, _qt=qt, _m=m, _qq=qq:
                         v.tensor_copy(
                             ctxF[64 * _m:64 * _m + 64, _ck,
                                  QT * _qt + 128 * _qq:
                                  QT * _qt + 128 * (_qq + 1)],
                             ps_av[_qq][64 * _m:64 * _m + 64, 128:256]))
                        for qq in range(4)
                    ], deps=[ttx])
                    for qq in range(4):
                        war[("ps_av", qq)] = tcx
                war["uqt"] = tok_pv[15]

            trs0 = ln_stats(E, lambda j: xq0[0:1, :], 1, [[tok_const]], cb=8, pp=1)
            tn0 = E.st("v", lambda v: v.tensor_scalar(
                h1q0[:, :], xq0[0:1, :], stat[0:1, MM + 8:MM + 9],
                stat[0:1, RS + 8:RS + 9], ALU.subtract, ALU.mult), deps=[trs0])
            tt0 = E.st("t", [
                (lambda t, _c=c: t.matmul(
                    ps_av[0][:, _c:_c + 1], h1q0[0:1, 128 * _c:128 * (_c + 1)],
                    identb[0:1, 0:1], start=True, stop=True))
                for c in range(8)
            ], deps=[tn0, war.get(("ps_av", 0))])
            tev0 = E.st("v", lambda v: v.tensor_copy(
                h1q0T[:, 0:8], ps_av[0][:, 0:8]),
                deps=[tt0])
            war[("ps_av", 0)] = tev0
            tu0m = E.st("v", lambda v: v.memset(U0c[:, :, :], 0.0),
                        deps=[war.get("U0c_rd"), war.get("dbg_g")])
            tok_u0 = []
            for oo in range(4):
                o = 4 * ppass + oo
                tmm = E.st("t", [
                    (lambda t, _o=o, _c=c: t.matmul(
                        ps_av[1][:, 0:1], wq_sb[:, _c, 128 * _o:128 * (_o + 1)],
                        h1q0T[:, _c:_c + 1], start=(_c == 0), stop=(_c == 7)))
                    for c in range(8)
                ], deps=[tev0, war.get(("ps_av", 1))])
                tum = E.st("v", lambda v: v.tensor_tensor(
                    rl[:, 0:1], ps_av[1][:, 0:1], trigq[:, 0:1], ALU.mult),
                    deps=[tmm])
                war[("ps_av", 1)] = tum
                for jj in range(2):
                    tok_u0.append(E.st("v", lambda v, _oo=oo, _jj=jj:
                                       v.tensor_scalar(
                                           U0c[:, _oo, 2 * _oo + _jj:2 * _oo + _jj + 1],
                                           rl[:, 0:1], u0m[:, _jj:_jj + 1], 0.0,
                                           ALU.mult, ALU.add),
                                       deps=[tum, tu0m]))
            if dbg:
                war["dbg_g"] = E.st("g", lambda g, _p=ppass: g.dma_start(
                    dU0c_d[_p], U0c[:, :, :]), deps=tok_u0)
            tok_p0 = []
            tsm = None
            for c8 in range(8):
                tsm = E.st("t", [
                    (lambda t, _oo=oo, _c8=c8: t.matmul(
                        ps_av[2][0:8, :], U0c[:, _oo, :],
                        uk_sb[:, _oo, 512 * _c8:512 * (_c8 + 1)],
                        start=(_oo == 0), stop=(_oo == 3)))
                    for oo in range(4)
                ], deps=tok_u0 + [war.get(("ps_av", 2)), tok_ukrdy])
                texp = E.st("s", lambda s, _c8=c8: s.activation(
                    P0[:, 512 * _c8:512 * (_c8 + 1)], ps_av[2][0:8, :], AF.Exp,
                    scale=0.125),
                    deps=[tsm, war.get("P0_rdv"), war.get("P0_rdt"),
                          war.get("dbg_g")])
                war[("ps_av", 2)] = texp
                tok_p0.append(texp)
            tnp0 = E.st("v", [
                lambda v: v.tensor_reduce(rc0[:, :], P0[:, :], AX.X, ALU.add),
                lambda v: v.reciprocal(rc0[:, :], rc0[:, :]),
                lambda v: v.tensor_scalar(P0[:, :], P0[:, :], rc0[:, :], 0.0,
                                          ALU.mult, ALU.add),
            ], deps=tok_p0)
            if dbg:
                war["dbg_g"] = E.st("g", lambda g, _p=ppass: g.dma_start(
                    dP0_d[_p], P0[:, :]), deps=[tnp0])
            war["P0_rdv"] = tnp0
            war["U0c_rd"] = tsm
            war["uk_rd"] = tsm
            ttr0 = E.st("t", [
                (lambda t, _kc=kc: t.matmul(
                    ps_av[3][:, 8 * _kc:8 * (_kc + 1)],
                    P0[:, 128 * _kc:128 * (_kc + 1)], identb[0:8, 0:8],
                    start=True, stop=True))
                for kc in range(KC)
            ], deps=[tnp0, war.get(("ps_av", 3))])
            war["P0_rdt"] = ttr0
            tp0e = E.st("v", lambda v: v.tensor_copy(
                P0T[:, :, :],
                ps_av[3][:, 0:256].rearrange("p (k e) -> p k e", e=8)),
                deps=[ttr0, war.get("P0T_rd")])
            war[("ps_av", 3)] = tp0e
            tpv = E.st("t", [
                (lambda t, _kc=kc, _j=j: t.matmul(
                    ps_av[0][0:1, 64 * _j:64 * _j + 64],
                    P0T[:, _kc, _j:_j + 1], vbuf[:, _kc, _j, 0:64],
                    start=(_kc == 0), stop=(_kc == KC - 1)))
                for j in range(8) for kc in range(KC)
            ], deps=[tp0e, tok_vrdy, war.get(("ps_av", 0))])
            war["P0T_rd"] = tpv
            war["vbuf_rd"] = tpv
            tc0 = E.st("v", lambda v: v.tensor_copy(
                c0row[0:1, :], ps_av[0][0:1, 0:512]),
                deps=[tpv, war.get("dbg_g")])
            war[("ps_av", 0)] = tc0
            if dbg:
                war["dbg_g"] = E.st("g", lambda g, _p=ppass: g.dma_start(
                    dc0_d[_p], c0row[0:1, :]), deps=[tc0])
            ttc = E.st("t", [
                (lambda t, _h=h, _half=half: t.matmul(
                    ps_av[1][64 * _half:64 * _half + 64,
                             8 * _half + _h:8 * _half + _h + 1],
                    c0row[0:1, 64 * _h:64 * _h + 64], identb[0:1, 0:1],
                    start=True, stop=True))
                for half in range(2) for h in range(8)
            ], deps=[tc0, war.get(("ps_av", 1))])
            tce = E.st("v", [
                lambda v, _p=ppass: v.tensor_copy(
                    ctx0T[0:64, 4 * _p:4 * _p + 4].rearrange(
                        "p (c e) -> p c e", e=1),
                    ps_av[1][0:64, 0:8].rearrange(
                        "p (c e) -> p c e", e=2)[:, :, 0:1]),
                lambda v, _p=ppass: v.tensor_copy(
                    ctx0T[64:128, 4 * _p:4 * _p + 4].rearrange(
                        "p (c e) -> p c e", e=1),
                    ps_av[1][64:128, 8:16].rearrange(
                        "p (c e) -> p c e", e=2)[:, :, 1:2]),
            ], deps=[ttc])
            war[("ps_av", 1)] = tce
            war["ctx0T"] = tce
            E.run(nc)

        pp_ps.close()
        pa_scope.close()
        set_gate()

        e_scope = ExitStack()
        p_scope.enter_context(e_scope)
        atF = sb("atF", [128, 8, 512], F32, e_scope)
        at_sb = sb("at_sb", [128, 1024], F32, e_scope)
        xqE = sb("xqE", [128, 4, 1024], F32, e_scope)
        x2st = sb("x2st", [128, 2, 1024], F32, e_scope)
        at0 = sb("at0", [1, 1024], F32, e_scope)
        e_ps = ExitStack()
        e_scope.enter_context(e_ps)
        ps_wo = [ps(f"ps_wo{i}", [128, 512], F32, e_ps) for i in range(2)]
        ps_tm = [ps(f"ps_tm{i}", [128, 1024], F32, e_ps) for i in range(2)]

        E = em()
        tok_wo = E.st("g", lambda g: g.dma_start(
            wo_sb[:, :, :], wo.rearrange("(c p) o -> p c o", p=128)))
        colb = sb("colb", [128, 8], F32, e_scope)
        tcl = []
        for o in range(8):
            bk = o % 2
            fns = []
            for c in range(8):
                fns.append(lambda t, _o=o, _c=c, _bk=bk: t.matmul(
                    ps_wo[_bk][:, 0:1],
                    wo_sb[:, _c, 128 * _o:128 * (_o + 1)],
                    ctx0T[:, _c:_c + 1],
                    start=(_c == 0), stop=(_c == 7)))
            tmo = E.st("t", fns, deps=[war["ctx0T"], tok_wo,
                                       war.get(("ps_wo", bk))])
            tce2 = E.st("v", lambda v, _o=o, _bk=bk: v.tensor_copy(
                colb[:, _o:_o + 1], ps_wo[_bk][:, 0:1]), deps=[tmo])
            war[("ps_wo", bk)] = tce2
            tcl.append(tce2)
        t0tr = E.st("t", [
            (lambda t, _o=o: t.matmul(
                ps_tm[0][0:1, 128 * _o:128 * (_o + 1)], colb[:, _o:_o + 1],
                identf[:, :], start=True, stop=True,
                is_transpose=True))
            for o in range(8)
        ], deps=tcl + [war.get(("ps_tm", 0))])
        t0e = E.st("v", lambda v: v.tensor_copy(at0[0:1, :], ps_tm[0][0:1, :]),
                   deps=[t0tr])
        war[("ps_tm", 0)] = t0e
        trs0 = ln_stats(E, lambda j: at0[0:1, :], 1, [[t0e]], cb=8, pp=1)
        tx2r0 = E.st("v", [
            lambda v: v.tensor_scalar(x2r0[:, :], at0[0:1, :],
                                      stat[0:1, MM + 8:MM + 9],
                                      stat[0:1, RS + 8:RS + 9],
                                      ALU.subtract, ALU.mult),
            lambda v: v.tensor_tensor(x2r0[:, :], x2r0[:, :], xq0[0:1, :],
                                      ALU.add),
            lambda v: v.tensor_scalar(x2r0[:, :], x2r0[:, :], bmask[0:1, 0:1],
                                      0.0, ALU.mult, ALU.add),
        ], deps=[trs0, tok_const])
        war["x2r0"] = tx2r0

        for qt in range(NQT):
            r0 = 1 + QT * qt
            tq = E.st("y", lambda y, _r0=r0: y.dma_start(
                xqE[:, :, :],
                xq[_r0:_r0 + QT, :].rearrange("(a p) o -> p a o", p=128)),
                deps=[war.get("xqE")])
            tok_wom = []
            for o in range(8):
                bk = o % 2
                fns = []
                for c in range(8):
                    fns.append(lambda t, _c=c, _o=o, _bk=bk, _qt=qt:
                               t.matmul(
                                   ps_wo[_bk][:, :],
                                   wo_sb[:, _c, 128 * _o:128 * (_o + 1)],
                                   ctxF[:, _c, QT * _qt:QT * (_qt + 1)],
                                   start=(_c == 0), stop=(_c == 7)))
                tm = E.st("t", fns, deps=[tok_wo, war.get(("ps_wo", bk))])
                tev = E.st("v", lambda v, _o=o, _bk=bk: v.tensor_copy(
                    atF[:, _o, :], ps_wo[_bk][:, :]), deps=[tm])
                war[("ps_wo", bk)] = tev
                tok_wom.append(tev)
            for j in range(4):
                bk = j % 2
                ttr = E.st("t", [
                    (lambda t, _j=j, _o=o, _bk=bk: t.matmul(
                        ps_tm[_bk][:, 128 * _o:128 * (_o + 1)],
                        atF[:, _o, 128 * _j:128 * (_j + 1)], identf[:, :],
                        start=True, stop=True, is_transpose=True))
                    for o in range(8)
                ], deps=tok_wom + [war.get(("ps_tm", bk))])
                tst = E.st("v", lambda v, _bk=bk, _j=j: v.tensor_scalar(
                    at_sb[:, :], ps_tm[_bk][:, :], 1.0, 0.0, ALU.mult, ALU.add,
                    accum_out=stat[:, SX + _j:SX + _j + 1]), deps=[ttr])
                tst2 = E.st("v", lambda v, _j=j: v.scalar_tensor_tensor(
                    junk[:, :], at_sb[:, :], 1.0, at_sb[:, :], ALU.mult,
                    ALU.mult, accum_out=stat[:, SXX + _j:SXX + _j + 1]),
                    deps=[tst])
                war[("ps_tm", bk)] = tst
                tfin = E.st("v", [
                    lambda v, _j=j: v.tensor_scalar(
                        mcol(_j), stat[:, SX + _j:SX + _j + 1], 1.0 / 1024.0,
                        0.0, ALU.mult, ALU.add),
                    lambda v, _j=j: v.tensor_tensor(
                        stat[:, M2 + _j:M2 + _j + 1], mcol(_j), mcol(_j), ALU.mult),
                    lambda v, _j=j: v.tensor_scalar(
                        rcol(_j), stat[:, SXX + _j:SXX + _j + 1], 1.0 / 1024.0,
                        0.0, ALU.mult, ALU.add),
                    lambda v, _j=j: v.tensor_tensor(
                        rcol(_j), rcol(_j), stat[:, M2 + _j:M2 + _j + 1], ALU.subtract),
                ], deps=[tst2, war.get("lnrs")])
                trsE = E.st("s", [
                    lambda s, _j=j: s.activation(rcol(_j), rcol(_j), AF.Ln,
                                                 bias=cbias[:, 0:1]),
                    lambda s, _j=j: s.activation(rcol(_j), rcol(_j), AF.Exp,
                                                 scale=-0.5),
                ], deps=[tfin])
                war["lnrs"] = trsE
                sl = j % 2
                tnr = E.st("v", [
                    lambda v, _j=j, _sl=sl: v.tensor_scalar(
                        x2st[:, _sl, :], at_sb[:, :], mcol(_j), rcol(_j),
                        ALU.subtract, ALU.mult),
                    lambda v, _j=j, _sl=sl: v.tensor_tensor(
                        x2st[:, _sl, :], x2st[:, _sl, :], xqE[:, _j, :],
                        ALU.add),
                ], deps=[trsE, tq, war.get(("x2st", sl))])
                tso = E.st("g", lambda g, _qt=qt, _j=j, _sl=sl: g.dma_start(
                    x2_d[QT * _qt + 128 * _j:QT * _qt + 128 * _j + 128, :],
                    x2st[:, _sl, :]), deps=[tnr])
                war[("x2st", sl)] = tso
                war["at_sb"] = tnr
            war["xqE"] = war["at_sb"]
        if dbg:
            E.st("g", [
                lambda g: g.dma_start(dctx_d[:, :, :], ctxF[:, :, :]),
                lambda g: g.dma_start(dx2r0_d, x2r0[:, :]),
                lambda g: g.dma_start(dct_d[:, :], ctx0T[:, 0:8]),
                lambda g: g.dma_start(dat0_d, at0[:, :]),
                lambda g: g.dma_start(dcolb_d, colb[:, :]),
            ], deps=[war["x2r0"]])
        tok_x2_done = [war[("x2st", 0)], war[("x2st", 1)]]
        E.run(nc)
        e_ps.close()
        e_scope.close()
        p_scope.close()
        ap_scope.close()
        set_gate()

        f_scope = ExitStack()
        top.enter_context(f_scope)
        wr_sb = sb("wr_sb", [128, 8, 1024], BF16, f_scope)
        wkf_sb = sb("wkf_sb", [128, 8, 1024], BF16, f_scope)
        wvf_sb = sb("wvf_sb", [128, 8, 1024], BF16, f_scope)
        h2T = sb("h2T", [128, 8, W1], BF16, f_scope)
        ffnout = sb("ffnout", [128, 16, 1024], BF16, f_scope)
        x2t = sb("x2t", [128, 2, 1024], F32, f_scope)
        h2b = sb("h2b", [128, 1024], BF16, f_scope)
        xrT = sb("xrT", [128, 2, 8, 512], BF16, f_scope)
        xkT = sb("xkT", [128, 2, 8, 512], BF16, f_scope)
        kactT = sb("kactT", [128, 8, 512], BF16, f_scope)
        krelu = sb("krelu", [128, 512], BF16, f_scope)
        rt = sb("rt", [128, 1024], F32, f_scope)
        tmp8 = sb("tmp8", [128, 512], BF16, f_scope)
        sx3 = sb("sx3", [128, 16], F32, f_scope)
        sxx3 = sb("sxx3", [128, 16], F32, f_scope)
        m3 = sb("m3", [128, 16], F32, f_scope)
        rstd3 = sb("rstd3", [128, 16], F32, f_scope)
        outst = sb("outst", [128, 2, 1024], F32, f_scope)

        fp_ps = ExitStack()
        f_scope.enter_context(fp_ps)
        ps_tpF = [ps(f"ps_tpF{i}", [128, 1024], BF16, fp_ps) for i in range(2)]
        E = em()
        tok_wf = E.st("g", [
            lambda g: g.dma_start(wr_sb[:, :, :],
                                  wr.rearrange("(c p) o -> p c o", p=128)),
            lambda g: g.dma_start(wkf_sb[:, :, :],
                                  wkf.rearrange("(c p) o -> p c o", p=128)),
            lambda g: g.dma_start(wvf_sb[:, :, :],
                                  wvf.rearrange("(c p) o -> p c o", p=128)),
        ])
        trs0 = ln_stats(E, lambda j: x2r0[0:1, :], 1, [[war["x2r0"]]], cb=8, pp=1)
        th20 = E.st("v", lambda v: v.tensor_scalar(
            h2b[0:1, :], x2r0[0:1, :], stat[0:1, MM + 8:MM + 9],
            stat[0:1, RS + 8:RS + 9], ALU.subtract, ALU.mult), deps=[trs0])
        tt0 = E.st("t", [
            (lambda t, _c=c: t.matmul(
                ps_tpF[0][:, 2 * _c:2 * _c + 1], h2b[0:1, 128 * _c:128 * (_c + 1)],
                identb[0:1, 0:1], start=True, stop=True,
                is_transpose=True))
            for c in range(8)
        ], deps=[th20])
        tev0 = E.st("v", lambda v: v.tensor_copy(
            h2T[:, :, 0:1],
            ps_tpF[0][:, 0:16].rearrange("p (c e) -> p c e", e=2)[:, :, 0:1]),
            deps=[tt0])
        war[("ps_tpF", 0)] = tev0
        for tt_i in range(16):
            sl = tt_i % 2
            tqd = E.st("y", lambda y, _tt=tt_i, _sl=sl: y.dma_start(
                x2t[:, _sl, :], x2_d[128 * _tt:128 * (_tt + 1), :]),
                deps=tok_x2_done + [war.get(("x2t", sl))])
            trs = ln_stats(E, lambda j, _sl=sl: x2t[:, _sl, :], 1, [[tqd]],
                           cb=tt_i % 4)
            tn = E.st("v", lambda v, _sl=sl, _tt=tt_i: v.tensor_scalar(
                h2b[:, :], x2t[:, _sl, :], mcol(_tt % 4), rcol(_tt % 4),
                ALU.subtract, ALU.mult), deps=[trs])
            war[("x2t", sl)] = tn
            bk = tt_i % 2
            ttr = E.st("t", [
                (lambda t, _c=c, _bk=bk: t.matmul(
                    ps_tpF[_bk][:, 128 * _c:128 * (_c + 1)],
                    h2b[:, 128 * _c:128 * (_c + 1)], identb[:, :],
                    start=True, stop=True, is_transpose=True))
                for c in range(8)
            ], deps=[tn, war.get(("ps_tpF", bk))])
            tev = E.st("v", lambda v, _tt=tt_i, _bk=bk: v.tensor_copy(
                h2T[:, :, 1 + 128 * _tt:1 + 128 * (_tt + 1)],
                ps_tpF[_bk][:, :].rearrange("p (c q) -> p c q", q=128)),
                deps=[ttr])
            war[("ps_tpF", bk)] = tev
            war["h2T"] = tev
        E.run(nc)
        fp_ps.close()
        set_gate()

        fa_ps = ExitStack()
        f_scope.enter_context(fa_ps)
        ps_r = ps("ps_r", [128, 1024], F32, fa_ps)
        ps_k = [ps(f"ps_kF{i}", [128, 512], F32, fa_ps) for i in range(2)]
        ps_vo = [ps(f"ps_voF{i}", [128, 1024], F32, fa_ps) for i in range(2)]
        E = em()

        def _ffn_build(_ft):
            _c0 = QT * _ft
            _sl = _ft % 2
            toks = []
            for c in range(8):
                for mu_t, omu_t, dstT in ((mur, omur, xrT), (muk, omuk, xkT)):
                    t1 = E.st("v", lambda v, _c=c, _omu=omu_t, _cc=_c0:
                              v.tensor_scalar(
                                  tmp8[:, :], h2T[:, _c, _cc:_cc + QT],
                                  _omu[:, _c:_c + 1], 0.0, ALU.mult, ALU.add),
                              deps=[war["h2T"], war.get(("xT_rd", _sl))])
                    t2 = E.st("v", lambda v, _c=c, _mu=mu_t, _dst=dstT,
                              _cc=_c0, _s=_sl:
                              v.scalar_tensor_tensor(
                                  _dst[:, _s, _c, :],
                                  h2T[:, _c, _cc + 1:_cc + 1 + QT],
                                  _mu[:, _c:_c + 1], tmp8[:, :], ALU.mult,
                                  ALU.add), deps=[t1])
                    toks.append(t2)
            return toks

        tok_builds = {0: _ffn_build(0)}
        for ft in range(4):
            sl = ft % 2
            tok_x = tok_builds[ft]
            tok_k = []
            for o in range(8):
                bk = o % 2
                tmm = E.st("t", [
                    (lambda t, _o=o, _c=c, _bk=bk, _s=sl: t.matmul(
                        ps_k[_bk][:, :], wkf_sb[:, _c, 128 * _o:128 * (_o + 1)],
                        xkT[:, _s, _c, :], start=(_c == 0), stop=(_c == 7)))
                    for c in range(8)
                ], deps=tok_x + [war.get(("ps_kF", bk)), tok_wf])
                trl = E.st("v", lambda v, _bk=bk: v.tensor_scalar_max(
                    krelu[:, :], ps_k[_bk][:, :], 0.0), deps=[tmm])
                war[("ps_kF", bk)] = trl
                tk2 = E.st("v", lambda v, _o=o: v.tensor_tensor(
                    kactT[:, _o, :], krelu[:, :], krelu[:, :], ALU.mult),
                    deps=[trl])
                tok_k.append(tk2)
            if ft + 1 < 4:
                tok_builds[ft + 1] = _ffn_build(ft + 1)
            for j in range(4):
                slot = 4 * ft + j
                fns = []
                for half in range(2):
                    for c in range(8):
                        fns.append(lambda t, _j=j, _c=c, _h=half, _s=sl:
                                   t.matmul(
                            ps_r[:, 512 * _h:512 * (_h + 1)],
                            xrT[:, _s, _c, 128 * _j:128 * (_j + 1)],
                            wr_sb[:, _c, 512 * _h:512 * (_h + 1)],
                            start=(_c == 0), stop=(_c == 7)))
                tmr = E.st("t", fns, deps=tok_x + [war.get("ps_r"), tok_wf])
                tth = E.st("s", lambda s: s.activation(
                    rt[:, :], ps_r[:, :], AF.Tanh, scale=0.5),
                    deps=[tmr, war.get("rt")])
                war["ps_r"] = tth
                tsg = E.st("v", lambda v: v.tensor_scalar(
                    rt[:, :], rt[:, :], 0.5, 0.5, ALU.mult, ALU.add),
                    deps=[tth])
                bk = j % 2
                fns = []
                for half in range(2):
                    for c in range(8):
                        fns.append(lambda t, _j=j, _c=c, _h=half, _bk=bk:
                                   t.matmul(
                                       ps_vo[_bk][:, 512 * _h:512 * (_h + 1)],
                                       kactT[:, _c, 128 * _j:128 * (_j + 1)],
                                       wvf_sb[:, _c, 512 * _h:512 * (_h + 1)],
                                       start=(_c == 0), stop=(_c == 7)))
                tmv = E.st("t", fns, deps=tok_k + [war.get(("ps_voF", bk)),
                                                   tok_wf])
                war[("xT_rd", sl)] = tmv
                tcf = E.st("v", lambda v, _slot=slot, _bk=bk:
                           v.scalar_tensor_tensor(
                               ffnout[:, _slot, :], ps_vo[_bk][:, :], 1.0,
                               rt[:, :], ALU.mult, ALU.mult,
                               accum_out=sx3[:, _slot:_slot + 1]),
                           deps=[tmv, tsg])
                war[("ps_voF", bk)] = tcf
                war["rt"] = tcf
                tcq = E.st("v", lambda v, _slot=slot: v.scalar_tensor_tensor(
                    junk[:, :], ffnout[:, _slot, :], 1.0, ffnout[:, _slot, :],
                    ALU.mult, ALU.mult,
                    accum_out=sxx3[:, _slot:_slot + 1]), deps=[tcf])
                war[("ffn", slot)] = tcq
        tok_ffn_done = [war[("ffn", s_)] for s_ in range(16)]
        E.run(nc)
        fa_ps.close()
        set_gate()

        E = em()
        tfin = E.st("v", [
            lambda v: v.tensor_scalar(m3[:, :], sx3[:, :], 1.0 / 1024.0, 0.0,
                                      ALU.mult, ALU.add),
            lambda v: v.tensor_tensor(stat[:, M2:M2 + 16], m3[:, :],
                                      m3[:, :], ALU.mult),
            lambda v: v.tensor_scalar(rstd3[:, :], sxx3[:, :], 1.0 / 1024.0,
                                      0.0, ALU.mult, ALU.add),
            lambda v: v.tensor_tensor(rstd3[:, :], rstd3[:, :],
                                      stat[:, M2:M2 + 16], ALU.subtract),
        ], deps=tok_ffn_done + [war.get("lnrs")])
        trs3 = E.st("s", [
            lambda s: s.activation(rstd3[:, :], rstd3[:, :], AF.Ln,
                                   bias=cbias[:, 0:1]),
            lambda s: s.activation(rstd3[:, :], rstd3[:, :], AF.Exp,
                                   scale=-0.5),
        ], deps=[tfin])
        for tt_i in range(16):
            sl = tt_i % 2
            tqd = E.st("y", lambda y, _tt=tt_i, _sl=sl: y.dma_start(
                x2t[:, _sl, :], x2_d[128 * _tt:128 * (_tt + 1), :]),
                deps=[war.get(("x2tF", sl))])
            tn = E.st("v", [
                lambda v, _tt=tt_i, _sl=sl: v.tensor_scalar(
                    outst[:, _sl, :], ffnout[:, _tt, :], m3[:, _tt:_tt + 1],
                    rstd3[:, _tt:_tt + 1], ALU.subtract, ALU.mult),
                lambda v, _tt=tt_i, _sl=sl: v.tensor_tensor(
                    outst[:, _sl, :], outst[:, _sl, :], x2t[:, _sl, :],
                    ALU.add),
            ], deps=[trs3, tqd, war.get(("outst", sl))])
            war[("x2tF", sl)] = tn
            tso = E.st("g", lambda g, _tt=tt_i, _sl=sl: g.dma_start(
                out_d[128 * _tt:128 * (_tt + 1), :], outst[:, _sl, :]),
                deps=[tn])
            war[("outst", sl)] = tso
        E.raw("g", [("sG", counters["sG"])], [lambda g: g.engine_nop()], 0)
        E.run(nc)

    return nc



def _trig(positions):
    import jax
    import jax.numpy as jnp
    cpu = jax.devices("cpu")[0]
    with jax.default_device(cpu):
        pos = jnp.asarray(np.asarray(positions, np.float32))
        freqs = jnp.exp(jnp.arange(0, HD, 2, dtype=jnp.float32)
                        * (-math.log(10000.0) / HD))
        ang = pos[:, None] * freqs[None, :]
        c = np.asarray(jnp.cos(ang)).T.astype(np.float32)
        s = np.asarray(jnp.sin(ang)).T.astype(np.float32)
    t = np.concatenate([c, s], axis=0)
    return np.vstack([t, t]).astype(np.float32)


def _dup_rows(W):
    Wd = np.empty_like(W)
    for h in range(H):
        blk = W[h * HD:(h + 1) * HD]
        Wd[h * HD:h * HD + HD // 2] = blk[0::2]
        Wd[h * HD + HD // 2:(h + 1) * HD] = blk[0::2]
    return np.ascontiguousarray(Wd.T)


def make_in_maps(L0_attn_x, L0_attn_Wq, L0_attn_Wk, L0_attn_Wv, L0_attn_Wo,
                 L0_ffn_W_r, L0_ffn_W_k, L0_ffn_W_v, L0_ffn_mu_r, L0_ffn_mu_k):
    f32 = np.float32
    x = np.ascontiguousarray(np.asarray(L0_attn_x, f32))
    wq_h = _dup_rows(np.asarray(L0_attn_Wq, f32))
    wk_h = _dup_rows(np.asarray(L0_attn_Wk, f32))
    wv_h = np.ascontiguousarray(np.asarray(L0_attn_Wv, f32).T)
    wo_h = np.ascontiguousarray(np.asarray(L0_attn_Wo, f32).T)
    wr_h = np.ascontiguousarray(np.asarray(L0_ffn_W_r, f32).T)
    wkf_h = np.ascontiguousarray(np.asarray(L0_ffn_W_k, f32).T)
    wvf_h = np.ascontiguousarray(np.asarray(L0_ffn_W_v, f32).T)
    mur_h = np.ascontiguousarray(np.asarray(L0_ffn_mu_r, f32).reshape(8, 128).T)
    muk_h = np.ascontiguousarray(np.asarray(L0_ffn_mu_k, f32).reshape(8, 128).T)
    trigk_h = _trig(np.arange(SKV))
    identb_h = np.eye(128, dtype=bfdt)
    identf_h = np.eye(128, dtype=np.float32)
    u0m_h = np.zeros((128, 2), np.float32)
    u0m_h[0:64, 0] = 1
    u0m_h[64:128, 1] = 1
    zrow = np.zeros((1, D), f32)
    trigq0 = np.ascontiguousarray(np.concatenate(
        [np.zeros((128, 1), np.float32), _trig(np.arange(OWN))], axis=1))
    trigq1 = np.ascontiguousarray(_trig(np.arange(OWN - 1, SKV)))

    in_maps = []
    for core in range(8):
        b, s = core // 2, core % 2
        if s == 0:
            xq_h = np.ascontiguousarray(np.concatenate([zrow, x[b, 0:OWN]], 0))
            trigq_h, bm = trigq0, np.zeros((1, 1), f32)
        else:
            xq_h = np.ascontiguousarray(x[b, OWN - 1:SKV])
            trigq_h, bm = trigq1, np.ones((1, 1), f32)
        in_maps.append({
            "xkv": np.ascontiguousarray(x[b]), "xq": xq_h,
            "wq": wq_h, "wk": wk_h, "wv": wv_h, "wo": wo_h,
            "wr": wr_h, "wkf": wkf_h, "wvf": wvf_h,
            "mur": mur_h, "muk": muk_h,
            "trigk": trigk_h, "trigq": trigq_h,
            "identb": identb_h, "identf": identf_h,
            "u0m": u0m_h, "bmask": bm,
        })
    return in_maps


def kernel(L0_attn_x, L0_attn_Wq, L0_attn_Wk, L0_attn_Wv, L0_attn_Wo,
           L0_ffn_W_r, L0_ffn_W_k, L0_ffn_W_v, L0_ffn_mu_r, L0_ffn_mu_k):
    in_maps = make_in_maps(L0_attn_x, L0_attn_Wq, L0_attn_Wk, L0_attn_Wv,
                           L0_attn_Wo, L0_ffn_W_r, L0_ffn_W_k, L0_ffn_W_v,
                           L0_ffn_mu_r, L0_ffn_mu_k)
    nc = build_program()
    res = run_bass_kernel_spmd(nc, in_maps, core_ids=list(range(8)))
    out = np.empty((B, SKV, D), np.asarray(res.results[0]["out"]).dtype)
    for core in range(8):
        b, s = core // 2, core % 2
        out[b, OWN * s:OWN * (s + 1)] = res.results[core]["out"]
    return out


if __name__ == "__main__":
    from reference import setup_inputs
    ins = setup_inputs(0)
    o = kernel(**{k: np.asarray(v) for k, v in ins.items()})
    print(o.shape, o.dtype, float(np.abs(o).mean()))


# version 16
# speedup vs baseline: 1.0473x; 1.0473x over previous
import math
from contextlib import ExitStack

import numpy as np
import ml_dtypes

import concourse.bass as bass
import concourse.mybir as mybir
from concourse.bass_utils import run_bass_kernel_spmd

F32 = mybir.dt.float32
BF16 = mybir.dt.bfloat16
AF = mybir.ActivationFunctionType
ALU = mybir.AluOpType
AX = mybir.AxisListType
bfdt = ml_dtypes.bfloat16

B, SKV, D, H, HD = 4, 4096, 1024, 16, 64
OWN = 2048
W1 = OWN + 1
QT = 512
NQT = OWN // QT
KC = SKV // 128
NG = SKV // 512
EPS = 1e-5

ENG_SEM = {"t": "sT", "v": "sV", "s": "sS", "y": "sY", "g": "sG"}
ENG_ATTR = {"t": "tensor", "v": "vector", "s": "scalar", "y": "sync", "g": "gpsimd"}
DMA_ENG = ("y", "g")


class Em:

    def __init__(self, sems, counters, lastw, gate=()):
        self.sems = sems
        self.c = counters
        self.lastw = lastw
        self.gate = list(gate)
        self.ops = {e: [] for e in ENG_SEM}

    def st(self, eng, fns, deps=()):
        if not isinstance(fns, (list, tuple)):
            fns = [fns]
        sem = ENG_SEM[eng]
        waits = []
        for d in list(deps) + self.gate:
            if d is None:
                continue
            dsem, dval = d
            if dsem == sem and eng not in DMA_ENG:
                continue
            key = (eng, dsem)
            if self.lastw.get(key, -1) >= dval:
                continue
            self.lastw[key] = dval
            waits.append((dsem, dval))
        pre = self.c[sem]
        if eng in DMA_ENG:
            if pre > 0 and self.lastw.get((eng, sem), -1) < pre:
                self.lastw[(eng, sem)] = pre
                waits.append((sem, pre))
            self.c[sem] += 16 * len(fns)
        else:
            self.c[sem] += len(fns)
        self.ops[eng].append((waits, list(fns), eng in DMA_ENG, pre))
        return (sem, self.c[sem])

    def raw(self, eng, waits, fns, inc):
        sem = ENG_SEM[eng]
        self.c[sem] += inc
        self.ops[eng].append((waits, list(fns), False, ("raw", inc)))
        return (sem, self.c[sem])

    def play(self, eng, engobj):
        sem = self.sems[ENG_SEM[eng]]
        own = ENG_SEM[eng]
        serial = eng in ("v", "s")
        for waits, fns, is_dma, pre in self.ops[eng]:
            for (s, v) in waits:
                engobj.wait_ge(self.sems[s], v)
            if is_dma:
                for f in fns:
                    f(engobj).then_inc(sem, 16)
            elif isinstance(pre, tuple):
                insts = [f(engobj) for f in fns]
                if insts and pre[1]:
                    insts[-1].then_inc(sem, pre[1])
            else:
                for i, f in enumerate(fns):
                    if serial and pre + i > 0:
                        engobj.wait_ge(sem, pre + i)
                    f(engobj).then_inc(sem, 1)

    def run(self, nc):
        with nc.Block() as blk:
            for eng in ("y", "g", "t", "v", "s"):
                if self.ops[eng]:
                    getattr(blk, ENG_ATTR[eng])(
                        lambda e, _eng=eng: self.play(_eng, e))


def build_program(dbg=False):
    nc = bass.Bass("TRN2", target_bir_lowering=False)

    def din(name, shape, dt=F32):
        return nc.dram_tensor(name, shape, dt, kind="ExternalInput").ap()

    xkv = din("xkv", [SKV, D])
    xq = din("xq", [W1, D])
    wq = din("wq", [D, D])
    wk = din("wk", [D, D])
    wv = din("wv", [D, D])
    wo = din("wo", [D, D])
    wr = din("wr", [D, D])
    wkf = din("wkf", [D, D])
    wvf = din("wvf", [D, D])
    mur_i = din("mur", [128, 8])
    muk_i = din("muk", [128, 8])
    trigk_i = din("trigk", [128, SKV], F32)
    trigq_i = din("trigq", [128, W1], F32)
    identb_i = din("identb", [128, 128], BF16)
    identf_i = din("identf", [128, 128], F32)
    u0m_i = din("u0m", [128, 2])
    bmask_i = din("bmask", [1, 1])

    out_d = nc.dram_tensor("out", [OWN, D], F32, kind="ExternalOutput").ap()
    skind = "ExternalOutput" if dbg else "Internal"
    x2_d = nc.dram_tensor("x2d", [OWN, D], F32, kind=skind).ap()
    ukhi_d = nc.dram_tensor("ukhi", [128, 4, SKV], BF16, kind="Internal").ap()
    vhi_d = nc.dram_tensor("vhi", [SKV, 512], BF16, kind="Internal").ap()
    if dbg:
        dctx_d = nc.dram_tensor("dctx", [128, 8, OWN], BF16,
                                kind="ExternalOutput").ap()
        dx2r0_d = nc.dram_tensor("dx2r0", [1, D], F32, kind="ExternalOutput").ap()
        dU0c_d = nc.dram_tensor("dU0c", [2, 128, 4, 8], BF16,
                                kind="ExternalOutput").ap()
        dP0_d = nc.dram_tensor("dP0", [2, 8, SKV], BF16,
                               kind="ExternalOutput").ap()
        dc0_d = nc.dram_tensor("dc0", [2, 1, 512], BF16,
                               kind="ExternalOutput").ap()
        dct_d = nc.dram_tensor("dct", [128, 8], BF16,
                               kind="ExternalOutput").ap()
        dat0_d = nc.dram_tensor("dat0", [1, D], F32,
                                kind="ExternalOutput").ap()
        dcolb_d = nc.dram_tensor("dcolb", [128, 8], F32,
                                 kind="ExternalOutput").ap()

    sems = {}
    counters = {s: 0 for s in ENG_SEM.values()}
    lastw = {}
    war = {}

    gate_hold = [[]]

    def em():
        return Em(sems, counters, lastw, gate_hold[0])

    def set_gate():
        gate_hold[0] = [(s, counters[s]) for s in counters if counters[s] > 0]

    with ExitStack() as top:
        for s in ("sT", "sV", "sS", "sY", "sG"):
            sems[s] = top.enter_context(nc.semaphore(s))

        def sb(name, shape, dt, ctx):
            return ctx.enter_context(nc.sbuf_tensor("s_" + name, shape, dt))

        def ps(name, shape, dt, ctx):
            return ctx.enter_context(nc.psum_tensor(name, shape, dt))

        identb = sb("identb", [128, 128], BF16, top)
        identf = sb("identf", [128, 128], F32, top)
        u0m = sb("u0m", [128, 2], F32, top)
        mur = sb("mur", [128, 8], F32, top)
        muk = sb("muk", [128, 8], F32, top)
        omur = sb("omur", [128, 8], F32, top)
        omuk = sb("omuk", [128, 8], F32, top)
        bmask = sb("bmask", [1, 1], F32, top)
        junk = sb("junk", [128, 1024], BF16, top)
        stat = sb("stat", [128, 80], F32, top)
        x2r0 = sb("x2r0", [1, D], F32, top)
        xq0 = sb("xq0", [1, D], F32, top)
        cbias = sb("cbias", [128, 2], F32, top)

        SX, SXX, MM, RS, M2 = 0, 16, 32, 48, 64

        def mcol(j, cb=0):
            return stat[:, MM + cb + j:MM + cb + j + 1]

        def rcol(j, cb=0):
            return stat[:, RS + cb + j:RS + cb + j + 1]

        def ln_stats(E, srcs, nsub, deps_each, n=1024.0, cb=0, pp=128):
            toks = []
            for j in range(nsub):
                t1 = E.st("v", lambda v, _j=j: v.tensor_scalar(
                    junk[0:pp, :], srcs(_j), 1.0, 0.0, ALU.mult, ALU.add,
                    accum_out=stat[0:pp, SX + cb + _j:SX + cb + _j + 1]),
                    deps=deps_each[j])
                t2 = E.st("v", lambda v, _j=j: v.scalar_tensor_tensor(
                    junk[0:pp, :], srcs(_j), 1.0, srcs(_j), ALU.mult, ALU.mult,
                    accum_out=stat[0:pp, SXX + cb + _j:SXX + cb + _j + 1]),
                    deps=deps_each[j])
                toks += [t1, t2]
            c0, c1 = cb, cb + nsub
            tm = E.st("v", [
                lambda v: v.tensor_scalar(stat[:, MM + c0:MM + c1],
                                          stat[:, SX + c0:SX + c1], 1.0 / n, 0.0,
                                          ALU.mult, ALU.add),
                lambda v: v.tensor_tensor(stat[:, M2 + c0:M2 + c1],
                                          stat[:, MM + c0:MM + c1],
                                          stat[:, MM + c0:MM + c1], ALU.mult),
                lambda v: v.tensor_scalar(stat[:, RS + c0:RS + c1],
                                          stat[:, SXX + c0:SXX + c1], 1.0 / n, 0.0,
                                          ALU.mult, ALU.add),
                lambda v: v.tensor_tensor(stat[:, RS + c0:RS + c1],
                                          stat[:, RS + c0:RS + c1],
                                          stat[:, M2 + c0:M2 + c1], ALU.subtract),
            ], deps=toks + [war.get("lnrs")])
            trs = E.st("s", [
                lambda s: s.activation(stat[:, RS + c0:RS + c1],
                                       stat[:, RS + c0:RS + c1], AF.Ln,
                                       bias=cbias[:, 0:1]),
                lambda s: s.activation(stat[:, RS + c0:RS + c1],
                                       stat[:, RS + c0:RS + c1], AF.Exp,
                                       scale=-0.5),
            ], deps=[tm])
            war["lnrs"] = trs
            return trs

        ap_scope = ExitStack()
        top.enter_context(ap_scope)
        uk_sb = sb("uk_sb", [128, 4, SKV], BF16, ap_scope)
        vbuf = sb("vbuf", [128, KC, 8, 65], BF16, ap_scope)
        a_scope = ExitStack()
        ap_scope.enter_context(a_scope)
        wk_sb = sb("wk_sb", [128, 8, 1024], BF16, a_scope)
        wv_sb = sb("wv_sb", [128, 8, 1024], BF16, a_scope)
        trigk = sb("trigk", [128, SKV], BF16, a_scope)
        xg = sb("xg", [128, 2, 4, 1024], F32, a_scope)
        h1g = sb("h1g", [128, 2, 4, 1024], BF16, a_scope)
        h1gT = sb("h1gT", [128, 8, 512], BF16, a_scope)
        ukst = sb("ukst", [128, 2, 512], BF16, a_scope)
        vst = sb("vst", [128, 2, 512], BF16, a_scope)

        E = em()
        tok_const = E.st("y", [
            lambda y: y.dma_start(identb[:, :], identb_i),
            lambda y: y.dma_start(identf[:, :], identf_i),
            lambda y: y.dma_start(u0m[:, :], u0m_i),
            lambda y: y.dma_start(mur[:, :], mur_i),
            lambda y: y.dma_start(muk[:, :], muk_i),
            lambda y: y.dma_start(bmask[:, :], bmask_i),
            lambda y: y.dma_start(xq0[:, :], xq[0:1, :]),
        ])
        E.st("s", [
            lambda s: s.activation(omur[:, :], mur[:, :], AF.Identity,
                                   bias=1.0, scale=-1.0),
            lambda s: s.activation(omuk[:, :], muk[:, :], AF.Identity,
                                   bias=1.0, scale=-1.0),
        ], deps=[tok_const])
        _e1 = float(np.float32(EPS))
        _e2 = float(np.float32(EPS - _e1))
        tok_kn0 = E.st("v", [
            lambda v: v.memset(cbias[:, 0:1], _e1),
            lambda v: v.tensor_scalar(cbias[:, 0:1], cbias[:, 0:1], 1.0, _e2,
                                      ALU.mult, ALU.add),
            lambda v: v.memset(cbias[:, 1:2], 1e-30),
            lambda v: v.memset(stat[:, :], 0.0),
        ], deps=[])
        tok_vones = E.st("v", lambda v: v.memset(vbuf[:, :, :, 64:65], 1.0),
                         deps=[])
        tok_wkv = E.st("g", [
            lambda g: g.dma_start(
                wk_sb[:, :, :], wk.rearrange("(c p) o -> p c o", p=128)),
            lambda g: g.dma_start(
                wv_sb[:, :, :], wv.rearrange("(c p) o -> p c o", p=128)),
        ])
        tok_trigk = E.st("g", lambda g: g.dma_start(trigk[:, :], trigk_i))
        E.run(nc)

        E = em()
        a_ps = ExitStack()
        a_scope.enter_context(a_ps)
        ps_tp = [ps(f"ps_tpA{i}", [128, 1024], BF16, a_ps) for i in range(2)]
        ps_ko = [ps(f"ps_koA{i}", [128, 512], F32, a_ps) for i in range(2)]
        ps_v = ps("ps_vA", [128, 1024], F32, a_ps)

        def emit_xg_dma(g, dep):
            sl = g % 2
            return E.st("y", lambda y, _g=g, _sl=sl: y.dma_start(
                xg[:, _sl, :, :],
                xkv[512 * _g:512 * (_g + 1), :].rearrange("(a p) o -> p a o", p=128)),
                deps=[dep])

        tok_dma = {0: emit_xg_dma(0, tok_wkv), 1: emit_xg_dma(1, None)}

        def a_accum(g, idx):
            sl, cb = g % 2, (g % 2) * 4
            j, kind = idx // 2, idx % 2
            if kind == 0:
                return E.st("v", lambda v, _j=j, _sl=sl, _cb=cb: v.tensor_scalar(
                    junk[:, :], xg[:, _sl, _j, :], 1.0, 0.0, ALU.mult, ALU.add,
                    accum_out=stat[:, SX + _cb + _j:SX + _cb + _j + 1]),
                    deps=[tok_dma[g]])
            return E.st("v", lambda v, _j=j, _sl=sl, _cb=cb:
                        v.scalar_tensor_tensor(
                junk[:, :], xg[:, _sl, _j, :], 1.0, xg[:, _sl, _j, :],
                ALU.mult, ALU.mult,
                accum_out=stat[:, SXX + _cb + _j:SXX + _cb + _j + 1]),
                deps=[tok_dma[g]])

        def a_finish(g, toks):
            cb = (g % 2) * 4
            c0, c1 = cb, cb + 4
            tm = E.st("v", [
                lambda v: v.tensor_scalar(stat[:, MM + c0:MM + c1],
                                          stat[:, SX + c0:SX + c1],
                                          1.0 / 1024.0, 0.0, ALU.mult, ALU.add),
                lambda v: v.tensor_tensor(stat[:, M2 + c0:M2 + c1],
                                          stat[:, MM + c0:MM + c1],
                                          stat[:, MM + c0:MM + c1], ALU.mult),
                lambda v: v.tensor_scalar(stat[:, RS + c0:RS + c1],
                                          stat[:, SXX + c0:SXX + c1],
                                          1.0 / 1024.0, 0.0, ALU.mult, ALU.add),
                lambda v: v.tensor_tensor(stat[:, RS + c0:RS + c1],
                                          stat[:, RS + c0:RS + c1],
                                          stat[:, M2 + c0:M2 + c1],
                                          ALU.subtract),
            ], deps=toks + [war.get("lnrs")])
            trs = E.st("s", [
                lambda s: s.activation(stat[:, RS + c0:RS + c1],
                                       stat[:, RS + c0:RS + c1], AF.Ln,
                                       bias=cbias[:, 0:1]),
                lambda s: s.activation(stat[:, RS + c0:RS + c1],
                                       stat[:, RS + c0:RS + c1], AF.Exp,
                                       scale=-0.5),
            ], deps=[tm])
            war["lnrs"] = trs
            return trs

        def a_norm(g, j, extra):
            sl, cb = g % 2, (g % 2) * 4
            return E.st("v", lambda v, _j=j, _sl=sl, _cb=cb: v.tensor_scalar(
                h1g[:, _sl, _j, :], xg[:, _sl, _j, :], mcol(_j, _cb),
                rcol(_j, _cb), ALU.subtract, ALU.mult), deps=extra)

        toks0 = [a_accum(0, i) for i in range(8)]
        trs0a = a_finish(0, toks0)
        tok_norm = {0: [a_norm(0, j, [trs0a]) for j in range(4)]}
        for g in range(NG):
            sl = g % 2
            if g + 2 < NG:
                tok_dma[g + 2] = emit_xg_dma(g + 2, tok_norm[g][-1])
            tok_ev = []
            for j in range(4):
                bk = j % 2
                tt = E.st("t", [
                    (lambda t, _j=j, _c=c, _bk=bk, _sl=sl: t.matmul(
                        ps_tp[_bk][:, 128 * _c:128 * (_c + 1)],
                        h1g[:, _sl, _j, 128 * _c:128 * (_c + 1)], identb[:, :],
                        start=True, stop=True, is_transpose=True))
                    for c in range(8)
                ], deps=[tok_norm[g][j], war.get(("ps_tpA", bk)), tok_const])
                tev = E.st("v", lambda v, _j=j, _bk=bk: v.tensor_copy(
                    h1gT[:, :, 128 * _j:128 * (_j + 1)],
                    ps_tp[_bk][:, :].rearrange("p (c q) -> p c q", q=128)),
                    deps=[tt])
                war[("ps_tpA", bk)] = tev
                tok_ev.append(tev)
            tok_nxt = []
            for o in range(8):
                bk = o % 2
                tmm = E.st("t", [
                    (lambda t, _o=o, _c=c, _bk=bk: t.matmul(
                        ps_ko[_bk][:, :], wk_sb[:, _c, 128 * _o:128 * (_o + 1)],
                        h1gT[:, _c, :], start=(_c == 0), stop=(_c == 7)))
                    for c in range(8)
                ], deps=tok_ev + [war.get(("ps_koA", bk)), tok_trigk])
                if o < 4:
                    dst = uk_sb[:, o, 512 * g:512 * (g + 1)]
                else:
                    dst = ukst[:, o % 2, :]
                tev = E.st("v", lambda v, _bk=bk, _g=g, _dst=dst: v.tensor_tensor(
                    _dst, ps_ko[_bk][:, :], trigk[:, 512 * _g:512 * (_g + 1)],
                    ALU.mult),
                    deps=[tmm, war.get(("ukst", o % 2)) if o >= 4 else None])
                war[("ps_koA", bk)] = tev
                if o >= 4:
                    tst = E.st("y", lambda y, _o=o, _g=g: y.dma_start(
                        ukhi_d[:, _o - 4, 512 * _g:512 * (_g + 1)],
                        ukst[:, _o % 2, :]), deps=[tev])
                    war[("ukst", o % 2)] = tst
                if g + 1 < NG:
                    tok_nxt.append(a_accum(g + 1, o))
            if g + 1 < NG:
                trs_n = a_finish(g + 1, tok_nxt)
            for j in range(4):
                kc = 4 * g + j
                fns = []
                for half in range(2):
                    for c in range(8):
                        fns.append(lambda t, _j=j, _c=c, _h=half: t.matmul(
                            ps_v[:, 512 * _h:512 * (_h + 1)],
                            h1gT[:, _c, 128 * _j:128 * (_j + 1)],
                            wv_sb[:, _c, 512 * _h:512 * (_h + 1)],
                            start=(_c == 0), stop=(_c == 7)))
                tvm = E.st("t", fns, deps=tok_ev + [war.get("ps_vA")])
                tlo = E.st("v", lambda v, _kc=kc: v.tensor_copy(
                    vbuf[:, _kc, :, 0:64],
                    ps_v[:, 0:512].rearrange("p (h d) -> p h d", d=64)),
                    deps=[tvm, tok_vones])
                thi = E.st("v", lambda v, _kc=kc: v.tensor_copy(
                    vst[:, _kc % 2, :], ps_v[:, 512:1024]),
                    deps=[tvm, war.get(("vst", kc % 2))])
                war["ps_vA"] = thi
                tvs = E.st("y", lambda y, _kc=kc: y.dma_start(
                    vhi_d[128 * _kc:128 * (_kc + 1), :], vst[:, _kc % 2, :]),
                    deps=[thi])
                war[("vst", kc % 2)] = tvs
                if g + 1 < NG:
                    tok_norm.setdefault(g + 1, []).append(
                        a_norm(g + 1, j, [trs_n, tok_ev[-1]]))
        tok_A_ukst = (war[("ukst", 0)], war[("ukst", 1)])
        tok_A_vst = (war[("vst", 0)], war[("vst", 1)])
        E.run(nc)
        a_ps.close()
        a_scope.close()
        set_gate()

        p_scope = ExitStack()
        ap_scope.enter_context(p_scope)
        wo_sb = sb("wo_sb", [128, 8, 1024], BF16, p_scope)
        ctxF = sb("ctxF", [128, 8, OWN], BF16, p_scope)
        ctx0T = sb("ctx0T", [128, 16], BF16, p_scope)
        pa_scope = ExitStack()
        p_scope.enter_context(pa_scope)
        wq_sb = sb("wq_sb", [128, 8, 1024], BF16, pa_scope)
        trigq = sb("trigq", [128, W1], BF16, pa_scope)
        uqt = sb("uqt", [128, 4, 512], BF16, pa_scope)
        xqg = sb("xqg", [128, 4, 1024], F32, pa_scope)
        h1q = sb("h1q", [128, 4, 1024], BF16, pa_scope)
        h1qT = sb("h1qT", [128, 8, 512], BF16, pa_scope)
        pT = sb("pT", [128, 4, 1024], BF16, pa_scope)
        rl = sb("rl", [128, 1], F32, pa_scope)
        rden = sb("rden", [128, 4], F32, pa_scope)
        ctx_sb = sb("ctx_sb", [128, 4, 64], BF16, pa_scope)
        h1q0 = sb("h1q0", [1, 1024], BF16, pa_scope)
        h1q0T = sb("h1q0T", [128, 8], BF16, pa_scope)
        U0c = sb("U0c", [128, 4, 8], BF16, pa_scope)
        P0 = sb("P0", [8, SKV], BF16, pa_scope)
        P0T = sb("P0T", [128, KC, 8], BF16, pa_scope)
        vtmp = sb("vtmp", [128, 2, 512], BF16, pa_scope)
        c0row = sb("c0row", [1, 512], BF16, pa_scope)
        rc0 = sb("rc0", [8, 1], F32, pa_scope)

        pp_ps = ExitStack()
        pa_scope.enter_context(pp_ps)
        ps_s = ps("ps_s", [128, 2048], F32, pp_ps)
        ps_av = [ps(f"ps_av{i}", [128, 512], F32, pp_ps) for i in range(4)]

        for ppass in range(2):
            E = em()
            if ppass == 0:
                tok_w = E.st("g", lambda g: g.dma_start(
                    wq_sb[:, :, :], wq.rearrange("(c p) o -> p c o", p=128)))
                tok_trq = E.st("g", lambda g: g.dma_start(trigq[:, :], trigq_i))
                tok_ukrdy = tok_w
                tok_vrdy = tok_vones
            else:
                tok_w = E.st("y", [
                    (lambda y, _j=j: y.dma_start(uk_sb[:, _j, :], ukhi_d[:, _j, :]))
                    for j in range(4)
                ], deps=[tok_A_ukst[0], tok_A_ukst[1], war.get("uk_rd")])
                tok_ukrdy = tok_w
                vt = None
                for kc in range(KC):
                    tvd = E.st("y", lambda y, _kc=kc: y.dma_start(
                        vtmp[:, _kc % 2, :], vhi_d[128 * _kc:128 * (_kc + 1), :]),
                        deps=[tok_A_vst[0], tok_A_vst[1],
                              war.get(("vtmp", kc % 2))])
                    vt = E.st("v", lambda v, _kc=kc: v.tensor_copy(
                        vbuf[:, _kc, :, 0:64],
                        vtmp[:, _kc % 2, :].rearrange("p (h d) -> p h d", d=64)),
                        deps=[tvd, war.get("vbuf_rd")])
                    war[("vtmp", kc % 2)] = vt
                tok_vrdy = vt
                tok_trq = None

            for qt in range(NQT):
                r0 = 1 + QT * qt
                tq = E.st("y", lambda y, _r0=r0: y.dma_start(
                    xqg[:, :, :],
                    xq[_r0:_r0 + QT, :].rearrange("(a p) o -> p a o", p=128)),
                    deps=[war.get("xqg"), tok_trq])
                trs = ln_stats(E, lambda j: xqg[:, j, :], 4, [[tq]] * 4)
                tok_norm = []
                for j in range(4):
                    tok_norm.append(E.st("v", lambda v, _j=j: v.tensor_scalar(
                        h1q[:, _j, :], xqg[:, _j, :], mcol(_j), rcol(_j),
                        ALU.subtract, ALU.mult), deps=[trs]))
                war["xqg"] = tok_norm[-1]
                tok_ev = []
                for j in range(4):
                    hfj = j % 2
                    tt = E.st("t", [
                        (lambda t, _j=j, _c=c, _hf=hfj: t.matmul(
                            ps_s[:, 1024 * _hf + 128 * _c:
                                 1024 * _hf + 128 * (_c + 1)],
                            h1q[:, _j, 128 * _c:128 * (_c + 1)], identb[:, :],
                            start=True, stop=True))
                        for c in range(8)
                    ], deps=[tok_norm[j], war.get(("ps_s", hfj))])
                    tev = E.st("v", lambda v, _j=j, _hf=hfj: v.tensor_copy(
                        h1qT[:, :, 128 * _j:128 * (_j + 1)],
                        ps_s[:, 1024 * _hf:1024 * (_hf + 1)].rearrange(
                            "p (c q) -> p c q", q=128)),
                        deps=[tt])
                    war[("ps_s", hfj)] = tev
                    tok_ev.append(tev)
                tok_uq = []
                for oo in range(4):
                    o = 4 * ppass + oo
                    tmm = E.st("t", [
                        (lambda t, _o=o, _c=c, _oo=oo: t.matmul(
                            ps_av[_oo][:, :],
                            wq_sb[:, _c, 128 * _o:128 * (_o + 1)],
                            h1qT[:, _c, :], start=(_c == 0), stop=(_c == 7)))
                        for c in range(8)
                    ], deps=tok_ev + [war.get(("ps_av", oo)), tok_w])
                    tev = E.st("v", lambda v, _oo=oo, _r0=r0: v.tensor_tensor(
                        uqt[:, _oo, :], ps_av[_oo][:, :],
                        trigq[:, _r0:_r0 + QT],
                        ALU.mult), deps=[tmm, war.get("uqt")])
                    war[("ps_av", oo)] = tev
                    tok_uq.append(tev)

                for hh in range(8):
                    h = 8 * ppass + hh
                    oo, m = hh // 2, hh % 2
                    tok_exp = {}
                    tok_pv = {}
                    def _emit_pv(_gg):
                        _sl = _gg % 4
                        tok_pv[_gg] = E.st("t", [
                            (lambda t, _jj=jj, _qq=qq, _g=_gg, _hh=hh, _s=_sl:
                             t.matmul(
                                ps_av[_qq][:, 0:65],
                                pT[:, _s, 512 * _jj + 128 * _qq:
                                   512 * _jj + 128 * _qq + 128],
                                vbuf[:, 2 * _g + _jj, _hh, :],
                                start=(2 * _g + _jj == 0),
                                stop=(2 * _g + _jj == KC - 1)))
                            for jj in range(2) for qq in range(4)
                        ], deps=[tok_exp[_gg], tok_vrdy]
                           + ([war.get(("ps_av", q2)) for q2 in range(4)]
                              if _gg == 0 else []))
                        war[("pT", _sl)] = tok_pv[_gg]

                    for gg in range(16):
                        hf = gg % 2
                        sl = gg % 4
                        tqk = E.st("t", [
                            (lambda t, _jj=jj, _gg=gg, _oo=oo, _m=m, _hf=hf:
                             t.matmul(
                                ps_s[:, 1024 * _hf + 512 * _jj:
                                     1024 * _hf + 512 * (_jj + 1)],
                                uk_sb[64 * _m:64 * _m + 64, _oo,
                                      128 * (2 * _gg + _jj):
                                      128 * (2 * _gg + _jj) + 128],
                                uqt[64 * _m:64 * _m + 64, _oo, :],
                                start=True, stop=True,
                                tile_position=(64 * _m, 0)))
                            for jj in range(2)
                        ], deps=[tok_uq[oo], tok_ukrdy,
                                 war.get(("ps_s", hf))])
                        tok_exp[gg] = E.st("s", lambda s, _sl=sl, _h=h, _hf=hf:
                            s.activation(
                                pT[:, _sl, :],
                                ps_s[:, 1024 * _hf:1024 * (_hf + 1)], AF.Exp,
                                scale=0.125),
                            deps=[tqk, war.get(("pT", sl))])
                        war[("ps_s", hf)] = tok_exp[gg]
                        if gg >= 1:
                            _emit_pv(gg - 1)
                    _emit_pv(15)
                    chunk = 4 * ppass + hh // 2
                    tok_dv = []
                    for qq in range(4):
                        tdv = E.st("v", [
                            (lambda v, _qq=qq: v.reciprocal(
                                rden[:, _qq:_qq + 1], ps_av[_qq][:, 64:65])),
                            (lambda v, _qq=qq: v.tensor_scalar(
                                ctx_sb[:, _qq, :], ps_av[_qq][:, 0:64],
                                rden[:, _qq:_qq + 1], 0.0, ALU.mult, ALU.add)),
                        ], deps=[tok_pv[15]])
                        war[("ps_av", qq)] = tdv
                        tok_dv.append(tdv)
                    ttx = E.st("t", [
                        (lambda t, _qq=qq, _m=m: t.matmul(
                            ps_av[_qq][64 * _m:64 * _m + 64, 128:256],
                            ctx_sb[:, _qq, :], identb[:, :],
                            start=True, stop=True))
                        for qq in range(4)
                    ], deps=tok_dv)
                    tcx = E.st("v", [
                        (lambda v, _ck=chunk, _qt=qt, _m=m, _qq=qq:
                         v.tensor_copy(
                             ctxF[64 * _m:64 * _m + 64, _ck,
                                  QT * _qt + 128 * _qq:
                                  QT * _qt + 128 * (_qq + 1)],
                             ps_av[_qq][64 * _m:64 * _m + 64, 128:256]))
                        for qq in range(4)
                    ], deps=[ttx])
                    for qq in range(4):
                        war[("ps_av", qq)] = tcx
                war["uqt"] = tok_pv[15]

            trs0 = ln_stats(E, lambda j: xq0[0:1, :], 1, [[tok_const]], cb=8, pp=1)
            tn0 = E.st("v", lambda v: v.tensor_scalar(
                h1q0[:, :], xq0[0:1, :], stat[0:1, MM + 8:MM + 9],
                stat[0:1, RS + 8:RS + 9], ALU.subtract, ALU.mult), deps=[trs0])
            tt0 = E.st("t", [
                (lambda t, _c=c: t.matmul(
                    ps_av[0][:, _c:_c + 1], h1q0[0:1, 128 * _c:128 * (_c + 1)],
                    identb[0:1, 0:1], start=True, stop=True))
                for c in range(8)
            ], deps=[tn0, war.get(("ps_av", 0))])
            tev0 = E.st("v", lambda v: v.tensor_copy(
                h1q0T[:, 0:8], ps_av[0][:, 0:8]),
                deps=[tt0])
            war[("ps_av", 0)] = tev0
            tu0m = E.st("v", lambda v: v.memset(U0c[:, :, :], 0.0),
                        deps=[war.get("U0c_rd"), war.get("dbg_g")])
            tok_u0 = []
            for oo in range(4):
                o = 4 * ppass + oo
                tmm = E.st("t", [
                    (lambda t, _o=o, _c=c: t.matmul(
                        ps_av[1][:, 0:1], wq_sb[:, _c, 128 * _o:128 * (_o + 1)],
                        h1q0T[:, _c:_c + 1], start=(_c == 0), stop=(_c == 7)))
                    for c in range(8)
                ], deps=[tev0, war.get(("ps_av", 1))])
                tum = E.st("v", lambda v: v.tensor_tensor(
                    rl[:, 0:1], ps_av[1][:, 0:1], trigq[:, 0:1], ALU.mult),
                    deps=[tmm])
                war[("ps_av", 1)] = tum
                for jj in range(2):
                    tok_u0.append(E.st("v", lambda v, _oo=oo, _jj=jj:
                                       v.tensor_scalar(
                                           U0c[:, _oo, 2 * _oo + _jj:2 * _oo + _jj + 1],
                                           rl[:, 0:1], u0m[:, _jj:_jj + 1], 0.0,
                                           ALU.mult, ALU.add),
                                       deps=[tum, tu0m]))
            if dbg:
                war["dbg_g"] = E.st("g", lambda g, _p=ppass: g.dma_start(
                    dU0c_d[_p], U0c[:, :, :]), deps=tok_u0)
            tok_p0 = []
            tsm = None
            for c8 in range(8):
                tsm = E.st("t", [
                    (lambda t, _oo=oo, _c8=c8: t.matmul(
                        ps_av[2][0:8, :], U0c[:, _oo, :],
                        uk_sb[:, _oo, 512 * _c8:512 * (_c8 + 1)],
                        start=(_oo == 0), stop=(_oo == 3)))
                    for oo in range(4)
                ], deps=tok_u0 + [war.get(("ps_av", 2)), tok_ukrdy])
                texp = E.st("s", lambda s, _c8=c8: s.activation(
                    P0[:, 512 * _c8:512 * (_c8 + 1)], ps_av[2][0:8, :], AF.Exp,
                    scale=0.125),
                    deps=[tsm, war.get("P0_rdv"), war.get("P0_rdt"),
                          war.get("dbg_g")])
                war[("ps_av", 2)] = texp
                tok_p0.append(texp)
            tnp0 = E.st("v", [
                lambda v: v.tensor_reduce(rc0[:, :], P0[:, :], AX.X, ALU.add),
                lambda v: v.reciprocal(rc0[:, :], rc0[:, :]),
                lambda v: v.tensor_scalar(P0[:, :], P0[:, :], rc0[:, :], 0.0,
                                          ALU.mult, ALU.add),
            ], deps=tok_p0)
            if dbg:
                war["dbg_g"] = E.st("g", lambda g, _p=ppass: g.dma_start(
                    dP0_d[_p], P0[:, :]), deps=[tnp0])
            war["P0_rdv"] = tnp0
            war["U0c_rd"] = tsm
            war["uk_rd"] = tsm
            ttr0 = E.st("t", [
                (lambda t, _kc=kc: t.matmul(
                    ps_av[3][:, 8 * _kc:8 * (_kc + 1)],
                    P0[:, 128 * _kc:128 * (_kc + 1)], identb[0:8, 0:8],
                    start=True, stop=True))
                for kc in range(KC)
            ], deps=[tnp0, war.get(("ps_av", 3))])
            war["P0_rdt"] = ttr0
            tp0e = E.st("v", lambda v: v.tensor_copy(
                P0T[:, :, :],
                ps_av[3][:, 0:256].rearrange("p (k e) -> p k e", e=8)),
                deps=[ttr0, war.get("P0T_rd")])
            war[("ps_av", 3)] = tp0e
            tpv = E.st("t", [
                (lambda t, _kc=kc, _j=j: t.matmul(
                    ps_av[0][0:1, 64 * _j:64 * _j + 64],
                    P0T[:, _kc, _j:_j + 1], vbuf[:, _kc, _j, 0:64],
                    start=(_kc == 0), stop=(_kc == KC - 1)))
                for j in range(8) for kc in range(KC)
            ], deps=[tp0e, tok_vrdy, war.get(("ps_av", 0))])
            war["P0T_rd"] = tpv
            war["vbuf_rd"] = tpv
            tc0 = E.st("v", lambda v: v.tensor_copy(
                c0row[0:1, :], ps_av[0][0:1, 0:512]),
                deps=[tpv, war.get("dbg_g")])
            war[("ps_av", 0)] = tc0
            if dbg:
                war["dbg_g"] = E.st("g", lambda g, _p=ppass: g.dma_start(
                    dc0_d[_p], c0row[0:1, :]), deps=[tc0])
            ttc = E.st("t", [
                (lambda t, _h=h, _half=half: t.matmul(
                    ps_av[1][64 * _half:64 * _half + 64,
                             8 * _half + _h:8 * _half + _h + 1],
                    c0row[0:1, 64 * _h:64 * _h + 64], identb[0:1, 0:1],
                    start=True, stop=True))
                for half in range(2) for h in range(8)
            ], deps=[tc0, war.get(("ps_av", 1))])
            tce = E.st("v", [
                lambda v, _p=ppass: v.tensor_copy(
                    ctx0T[0:64, 4 * _p:4 * _p + 4].rearrange(
                        "p (c e) -> p c e", e=1),
                    ps_av[1][0:64, 0:8].rearrange(
                        "p (c e) -> p c e", e=2)[:, :, 0:1]),
                lambda v, _p=ppass: v.tensor_copy(
                    ctx0T[64:128, 4 * _p:4 * _p + 4].rearrange(
                        "p (c e) -> p c e", e=1),
                    ps_av[1][64:128, 8:16].rearrange(
                        "p (c e) -> p c e", e=2)[:, :, 1:2]),
            ], deps=[ttc])
            war[("ps_av", 1)] = tce
            war["ctx0T"] = tce
            E.run(nc)

        pp_ps.close()
        pa_scope.close()
        set_gate()

        e_scope = ExitStack()
        p_scope.enter_context(e_scope)
        atF = sb("atF", [128, 8, 512], F32, e_scope)
        at_sb = sb("at_sb", [128, 1024], F32, e_scope)
        xqE = sb("xqE", [128, 4, 1024], F32, e_scope)
        x2st = sb("x2st", [128, 2, 1024], F32, e_scope)
        at0 = sb("at0", [1, 1024], F32, e_scope)
        e_ps = ExitStack()
        e_scope.enter_context(e_ps)
        ps_wo = [ps(f"ps_wo{i}", [128, 512], F32, e_ps) for i in range(2)]
        ps_tm = [ps(f"ps_tm{i}", [128, 1024], F32, e_ps) for i in range(2)]

        E = em()
        tok_wo = E.st("g", lambda g: g.dma_start(
            wo_sb[:, :, :], wo.rearrange("(c p) o -> p c o", p=128)))
        colb = sb("colb", [128, 8], F32, e_scope)
        tcl = []
        for o in range(8):
            bk = o % 2
            fns = []
            for c in range(8):
                fns.append(lambda t, _o=o, _c=c, _bk=bk: t.matmul(
                    ps_wo[_bk][:, 0:1],
                    wo_sb[:, _c, 128 * _o:128 * (_o + 1)],
                    ctx0T[:, _c:_c + 1],
                    start=(_c == 0), stop=(_c == 7)))
            tmo = E.st("t", fns, deps=[war["ctx0T"], tok_wo,
                                       war.get(("ps_wo", bk))])
            tce2 = E.st("v", lambda v, _o=o, _bk=bk: v.tensor_copy(
                colb[:, _o:_o + 1], ps_wo[_bk][:, 0:1]), deps=[tmo])
            war[("ps_wo", bk)] = tce2
            tcl.append(tce2)
        t0tr = E.st("t", [
            (lambda t, _o=o: t.matmul(
                ps_tm[0][0:1, 128 * _o:128 * (_o + 1)], colb[:, _o:_o + 1],
                identf[:, :], start=True, stop=True,
                is_transpose=True))
            for o in range(8)
        ], deps=tcl + [war.get(("ps_tm", 0))])
        t0e = E.st("v", lambda v: v.tensor_copy(at0[0:1, :], ps_tm[0][0:1, :]),
                   deps=[t0tr])
        war[("ps_tm", 0)] = t0e
        trs0 = ln_stats(E, lambda j: at0[0:1, :], 1, [[t0e]], cb=8, pp=1)
        tx2r0 = E.st("v", [
            lambda v: v.tensor_scalar(x2r0[:, :], at0[0:1, :],
                                      stat[0:1, MM + 8:MM + 9],
                                      stat[0:1, RS + 8:RS + 9],
                                      ALU.subtract, ALU.mult),
            lambda v: v.tensor_tensor(x2r0[:, :], x2r0[:, :], xq0[0:1, :],
                                      ALU.add),
            lambda v: v.tensor_scalar(x2r0[:, :], x2r0[:, :], bmask[0:1, 0:1],
                                      0.0, ALU.mult, ALU.add),
        ], deps=[trs0, tok_const])
        war["x2r0"] = tx2r0

        for qt in range(NQT):
            r0 = 1 + QT * qt
            tq = E.st("y", lambda y, _r0=r0: y.dma_start(
                xqE[:, :, :],
                xq[_r0:_r0 + QT, :].rearrange("(a p) o -> p a o", p=128)),
                deps=[war.get("xqE")])
            tok_wom = []
            for o in range(8):
                bk = o % 2
                fns = []
                for c in range(8):
                    fns.append(lambda t, _c=c, _o=o, _bk=bk, _qt=qt:
                               t.matmul(
                                   ps_wo[_bk][:, :],
                                   wo_sb[:, _c, 128 * _o:128 * (_o + 1)],
                                   ctxF[:, _c, QT * _qt:QT * (_qt + 1)],
                                   start=(_c == 0), stop=(_c == 7)))
                tm = E.st("t", fns, deps=[tok_wo, war.get(("ps_wo", bk))])
                tev = E.st("v", lambda v, _o=o, _bk=bk: v.tensor_copy(
                    atF[:, _o, :], ps_wo[_bk][:, :]), deps=[tm])
                war[("ps_wo", bk)] = tev
                tok_wom.append(tev)
            for j in range(4):
                bk = j % 2
                ttr = E.st("t", [
                    (lambda t, _j=j, _o=o, _bk=bk: t.matmul(
                        ps_tm[_bk][:, 128 * _o:128 * (_o + 1)],
                        atF[:, _o, 128 * _j:128 * (_j + 1)], identf[:, :],
                        start=True, stop=True, is_transpose=True))
                    for o in range(8)
                ], deps=tok_wom + [war.get(("ps_tm", bk))])
                tst = E.st("v", lambda v, _bk=bk, _j=j: v.tensor_scalar(
                    at_sb[:, :], ps_tm[_bk][:, :], 1.0, 0.0, ALU.mult, ALU.add,
                    accum_out=stat[:, SX + _j:SX + _j + 1]), deps=[ttr])
                tst2 = E.st("v", lambda v, _j=j: v.scalar_tensor_tensor(
                    junk[:, :], at_sb[:, :], 1.0, at_sb[:, :], ALU.mult,
                    ALU.mult, accum_out=stat[:, SXX + _j:SXX + _j + 1]),
                    deps=[tst])
                war[("ps_tm", bk)] = tst
                tfin = E.st("v", [
                    lambda v, _j=j: v.tensor_scalar(
                        mcol(_j), stat[:, SX + _j:SX + _j + 1], 1.0 / 1024.0,
                        0.0, ALU.mult, ALU.add),
                    lambda v, _j=j: v.tensor_tensor(
                        stat[:, M2 + _j:M2 + _j + 1], mcol(_j), mcol(_j), ALU.mult),
                    lambda v, _j=j: v.tensor_scalar(
                        rcol(_j), stat[:, SXX + _j:SXX + _j + 1], 1.0 / 1024.0,
                        0.0, ALU.mult, ALU.add),
                    lambda v, _j=j: v.tensor_tensor(
                        rcol(_j), rcol(_j), stat[:, M2 + _j:M2 + _j + 1], ALU.subtract),
                ], deps=[tst2, war.get("lnrs")])
                trsE = E.st("s", [
                    lambda s, _j=j: s.activation(rcol(_j), rcol(_j), AF.Ln,
                                                 bias=cbias[:, 0:1]),
                    lambda s, _j=j: s.activation(rcol(_j), rcol(_j), AF.Exp,
                                                 scale=-0.5),
                ], deps=[tfin])
                war["lnrs"] = trsE
                sl = j % 2
                tnr = E.st("v", [
                    lambda v, _j=j, _sl=sl: v.tensor_scalar(
                        x2st[:, _sl, :], at_sb[:, :], mcol(_j), rcol(_j),
                        ALU.subtract, ALU.mult),
                    lambda v, _j=j, _sl=sl: v.tensor_tensor(
                        x2st[:, _sl, :], x2st[:, _sl, :], xqE[:, _j, :],
                        ALU.add),
                ], deps=[trsE, tq, war.get(("x2st", sl))])
                tso = E.st("g", lambda g, _qt=qt, _j=j, _sl=sl: g.dma_start(
                    x2_d[QT * _qt + 128 * _j:QT * _qt + 128 * _j + 128, :],
                    x2st[:, _sl, :]), deps=[tnr])
                war[("x2st", sl)] = tso
                war["at_sb"] = tnr
            war["xqE"] = war["at_sb"]
        if dbg:
            E.st("g", [
                lambda g: g.dma_start(dctx_d[:, :, :], ctxF[:, :, :]),
                lambda g: g.dma_start(dx2r0_d, x2r0[:, :]),
                lambda g: g.dma_start(dct_d[:, :], ctx0T[:, 0:8]),
                lambda g: g.dma_start(dat0_d, at0[:, :]),
                lambda g: g.dma_start(dcolb_d, colb[:, :]),
            ], deps=[war["x2r0"]])
        tok_x2_done = [war[("x2st", 0)], war[("x2st", 1)]]
        E.run(nc)
        e_ps.close()
        e_scope.close()
        p_scope.close()
        ap_scope.close()
        set_gate()

        f_scope = ExitStack()
        top.enter_context(f_scope)
        wr_sb = sb("wr_sb", [128, 8, 1024], BF16, f_scope)
        wkf_sb = sb("wkf_sb", [128, 8, 1024], BF16, f_scope)
        wvf_sb = sb("wvf_sb", [128, 8, 1024], BF16, f_scope)
        h2T = sb("h2T", [128, 8, W1], BF16, f_scope)
        ffnout = sb("ffnout", [128, 16, 1024], BF16, f_scope)
        x2t = sb("x2t", [128, 2, 1024], F32, f_scope)
        h2b = sb("h2b", [128, 1024], BF16, f_scope)
        xrT = sb("xrT", [128, 2, 8, 512], BF16, f_scope)
        xkT = sb("xkT", [128, 2, 8, 512], BF16, f_scope)
        kactT = sb("kactT", [128, 8, 512], BF16, f_scope)
        krelu = sb("krelu", [128, 512], BF16, f_scope)
        rt = sb("rt", [128, 1024], F32, f_scope)
        tmp8 = sb("tmp8", [128, 512], BF16, f_scope)
        sx3 = sb("sx3", [128, 16], F32, f_scope)
        sxx3 = sb("sxx3", [128, 16], F32, f_scope)
        m3 = sb("m3", [128, 16], F32, f_scope)
        rstd3 = sb("rstd3", [128, 16], F32, f_scope)
        outst = sb("outst", [128, 2, 1024], F32, f_scope)

        fp_ps = ExitStack()
        f_scope.enter_context(fp_ps)
        ps_tpF = [ps(f"ps_tpF{i}", [128, 1024], BF16, fp_ps) for i in range(2)]
        E = em()
        tok_wf = E.st("g", [
            lambda g: g.dma_start(wr_sb[:, :, :],
                                  wr.rearrange("(c p) o -> p c o", p=128)),
            lambda g: g.dma_start(wkf_sb[:, :, :],
                                  wkf.rearrange("(c p) o -> p c o", p=128)),
            lambda g: g.dma_start(wvf_sb[:, :, :],
                                  wvf.rearrange("(c p) o -> p c o", p=128)),
        ])
        trs0 = ln_stats(E, lambda j: x2r0[0:1, :], 1, [[war["x2r0"]]], cb=8, pp=1)
        th20 = E.st("v", lambda v: v.tensor_scalar(
            h2b[0:1, :], x2r0[0:1, :], stat[0:1, MM + 8:MM + 9],
            stat[0:1, RS + 8:RS + 9], ALU.subtract, ALU.mult), deps=[trs0])
        tt0 = E.st("t", [
            (lambda t, _c=c: t.matmul(
                ps_tpF[0][:, 2 * _c:2 * _c + 1], h2b[0:1, 128 * _c:128 * (_c + 1)],
                identb[0:1, 0:1], start=True, stop=True,
                is_transpose=True))
            for c in range(8)
        ], deps=[th20])
        tev0 = E.st("v", lambda v: v.tensor_copy(
            h2T[:, :, 0:1],
            ps_tpF[0][:, 0:16].rearrange("p (c e) -> p c e", e=2)[:, :, 0:1]),
            deps=[tt0])
        war[("ps_tpF", 0)] = tev0
        for tt_i in range(16):
            sl = tt_i % 2
            tqd = E.st("y", lambda y, _tt=tt_i, _sl=sl: y.dma_start(
                x2t[:, _sl, :], x2_d[128 * _tt:128 * (_tt + 1), :]),
                deps=tok_x2_done + [war.get(("x2t", sl))])
            trs = ln_stats(E, lambda j, _sl=sl: x2t[:, _sl, :], 1, [[tqd]],
                           cb=tt_i % 4)
            tn = E.st("v", lambda v, _sl=sl, _tt=tt_i: v.tensor_scalar(
                h2b[:, :], x2t[:, _sl, :], mcol(_tt % 4), rcol(_tt % 4),
                ALU.subtract, ALU.mult), deps=[trs])
            war[("x2t", sl)] = tn
            bk = tt_i % 2
            ttr = E.st("t", [
                (lambda t, _c=c, _bk=bk: t.matmul(
                    ps_tpF[_bk][:, 128 * _c:128 * (_c + 1)],
                    h2b[:, 128 * _c:128 * (_c + 1)], identb[:, :],
                    start=True, stop=True, is_transpose=True))
                for c in range(8)
            ], deps=[tn, war.get(("ps_tpF", bk))])
            tev = E.st("v", lambda v, _tt=tt_i, _bk=bk: v.tensor_copy(
                h2T[:, :, 1 + 128 * _tt:1 + 128 * (_tt + 1)],
                ps_tpF[_bk][:, :].rearrange("p (c q) -> p c q", q=128)),
                deps=[ttr])
            war[("ps_tpF", bk)] = tev
            war["h2T"] = tev
        E.run(nc)
        fp_ps.close()
        set_gate()

        fa_ps = ExitStack()
        f_scope.enter_context(fa_ps)
        ps_r = ps("ps_r", [128, 1024], F32, fa_ps)
        ps_k = [ps(f"ps_kF{i}", [128, 512], F32, fa_ps) for i in range(2)]
        ps_vo = [ps(f"ps_voF{i}", [128, 1024], F32, fa_ps) for i in range(2)]
        E = em()

        def _ffn_build(_ft):
            _c0 = QT * _ft
            _sl = _ft % 2
            toks = []
            for c in range(8):
                for mu_t, omu_t, dstT in ((mur, omur, xrT), (muk, omuk, xkT)):
                    t1 = E.st("v", lambda v, _c=c, _omu=omu_t, _cc=_c0:
                              v.tensor_scalar(
                                  tmp8[:, :], h2T[:, _c, _cc:_cc + QT],
                                  _omu[:, _c:_c + 1], 0.0, ALU.mult, ALU.add),
                              deps=[war["h2T"], war.get(("xT_rd", _sl))])
                    t2 = E.st("v", lambda v, _c=c, _mu=mu_t, _dst=dstT,
                              _cc=_c0, _s=_sl:
                              v.scalar_tensor_tensor(
                                  _dst[:, _s, _c, :],
                                  h2T[:, _c, _cc + 1:_cc + 1 + QT],
                                  _mu[:, _c:_c + 1], tmp8[:, :], ALU.mult,
                                  ALU.add), deps=[t1])
                    toks.append(t2)
            return toks

        tok_builds = {0: _ffn_build(0)}
        for ft in range(4):
            sl = ft % 2
            tok_x = tok_builds[ft]
            tok_k = []
            for o in range(8):
                bk = o % 2
                tmm = E.st("t", [
                    (lambda t, _o=o, _c=c, _bk=bk, _s=sl: t.matmul(
                        ps_k[_bk][:, :], wkf_sb[:, _c, 128 * _o:128 * (_o + 1)],
                        xkT[:, _s, _c, :], start=(_c == 0), stop=(_c == 7)))
                    for c in range(8)
                ], deps=tok_x + [war.get(("ps_kF", bk)), tok_wf])
                trl = E.st("v", lambda v, _bk=bk: v.tensor_scalar_max(
                    krelu[:, :], ps_k[_bk][:, :], 0.0), deps=[tmm])
                war[("ps_kF", bk)] = trl
                tk2 = E.st("v", lambda v, _o=o: v.tensor_tensor(
                    kactT[:, _o, :], krelu[:, :], krelu[:, :], ALU.mult),
                    deps=[trl])
                tok_k.append(tk2)
            if ft + 1 < 4:
                tok_builds[ft + 1] = _ffn_build(ft + 1)
            for j in range(4):
                slot = 4 * ft + j
                fns = []
                for half in range(2):
                    for c in range(8):
                        fns.append(lambda t, _j=j, _c=c, _h=half, _s=sl:
                                   t.matmul(
                            ps_r[:, 512 * _h:512 * (_h + 1)],
                            xrT[:, _s, _c, 128 * _j:128 * (_j + 1)],
                            wr_sb[:, _c, 512 * _h:512 * (_h + 1)],
                            start=(_c == 0), stop=(_c == 7)))
                tmr = E.st("t", fns, deps=tok_x + [war.get("ps_r"), tok_wf])
                tth = E.st("s", lambda s: s.activation(
                    rt[:, :], ps_r[:, :], AF.Tanh, scale=0.5),
                    deps=[tmr, war.get("rt")])
                war["ps_r"] = tth
                tsg = E.st("v", lambda v: v.tensor_scalar(
                    rt[:, :], rt[:, :], 0.5, 0.5, ALU.mult, ALU.add),
                    deps=[tth])
                bk = j % 2
                fns = []
                for half in range(2):
                    for c in range(8):
                        fns.append(lambda t, _j=j, _c=c, _h=half, _bk=bk:
                                   t.matmul(
                                       ps_vo[_bk][:, 512 * _h:512 * (_h + 1)],
                                       kactT[:, _c, 128 * _j:128 * (_j + 1)],
                                       wvf_sb[:, _c, 512 * _h:512 * (_h + 1)],
                                       start=(_c == 0), stop=(_c == 7)))
                tmv = E.st("t", fns, deps=tok_k + [war.get(("ps_voF", bk)),
                                                   tok_wf])
                war[("xT_rd", sl)] = tmv
                tcf = E.st("v", lambda v, _slot=slot, _bk=bk:
                           v.scalar_tensor_tensor(
                               ffnout[:, _slot, :], ps_vo[_bk][:, :], 1.0,
                               rt[:, :], ALU.mult, ALU.mult,
                               accum_out=sx3[:, _slot:_slot + 1]),
                           deps=[tmv, tsg])
                war[("ps_voF", bk)] = tcf
                war["rt"] = tcf
                tcq = E.st("v", lambda v, _slot=slot: v.scalar_tensor_tensor(
                    junk[:, :], ffnout[:, _slot, :], 1.0, ffnout[:, _slot, :],
                    ALU.mult, ALU.mult,
                    accum_out=sxx3[:, _slot:_slot + 1]), deps=[tcf])
                war[("ffn", slot)] = tcq
        tok_ffn_done = [war[("ffn", s_)] for s_ in range(16)]
        E.run(nc)
        fa_ps.close()
        set_gate()

        E = em()
        tfin = E.st("v", [
            lambda v: v.tensor_scalar(m3[:, :], sx3[:, :], 1.0 / 1024.0, 0.0,
                                      ALU.mult, ALU.add),
            lambda v: v.tensor_tensor(stat[:, M2:M2 + 16], m3[:, :],
                                      m3[:, :], ALU.mult),
            lambda v: v.tensor_scalar(rstd3[:, :], sxx3[:, :], 1.0 / 1024.0,
                                      0.0, ALU.mult, ALU.add),
            lambda v: v.tensor_tensor(rstd3[:, :], rstd3[:, :],
                                      stat[:, M2:M2 + 16], ALU.subtract),
        ], deps=tok_ffn_done + [war.get("lnrs")])
        trs3 = E.st("s", [
            lambda s: s.activation(rstd3[:, :], rstd3[:, :], AF.Ln,
                                   bias=cbias[:, 0:1]),
            lambda s: s.activation(rstd3[:, :], rstd3[:, :], AF.Exp,
                                   scale=-0.5),
        ], deps=[tfin])
        for tt_i in range(16):
            sl = tt_i % 2
            tqd = E.st("y", lambda y, _tt=tt_i, _sl=sl: y.dma_start(
                x2t[:, _sl, :], x2_d[128 * _tt:128 * (_tt + 1), :]),
                deps=[war.get(("x2tF", sl))])
            tn = E.st("v", [
                lambda v, _tt=tt_i, _sl=sl: v.tensor_scalar(
                    outst[:, _sl, :], ffnout[:, _tt, :], m3[:, _tt:_tt + 1],
                    rstd3[:, _tt:_tt + 1], ALU.subtract, ALU.mult),
                lambda v, _tt=tt_i, _sl=sl: v.tensor_tensor(
                    outst[:, _sl, :], outst[:, _sl, :], x2t[:, _sl, :],
                    ALU.add),
            ], deps=[trs3, tqd, war.get(("outst", sl))])
            war[("x2tF", sl)] = tn
            tso = E.st("g", lambda g, _tt=tt_i, _sl=sl: g.dma_start(
                out_d[128 * _tt:128 * (_tt + 1), :], outst[:, _sl, :]),
                deps=[tn])
            war[("outst", sl)] = tso
        E.raw("g", [("sG", counters["sG"])], [lambda g: g.engine_nop()], 0)
        E.run(nc)

    return nc



def _trig(positions):
    import jax
    import jax.numpy as jnp
    cpu = jax.devices("cpu")[0]
    with jax.default_device(cpu):
        pos = jnp.asarray(np.asarray(positions, np.float32))
        freqs = jnp.exp(jnp.arange(0, HD, 2, dtype=jnp.float32)
                        * (-math.log(10000.0) / HD))
        ang = pos[:, None] * freqs[None, :]
        c = np.asarray(jnp.cos(ang)).T.astype(np.float32)
        s = np.asarray(jnp.sin(ang)).T.astype(np.float32)
    t = np.concatenate([c, s], axis=0)
    return np.vstack([t, t]).astype(np.float32)


def _dup_rows(W):
    Wd = np.empty_like(W)
    for h in range(H):
        blk = W[h * HD:(h + 1) * HD]
        Wd[h * HD:h * HD + HD // 2] = blk[0::2]
        Wd[h * HD + HD // 2:(h + 1) * HD] = blk[0::2]
    return np.ascontiguousarray(Wd.T)


def make_in_maps(L0_attn_x, L0_attn_Wq, L0_attn_Wk, L0_attn_Wv, L0_attn_Wo,
                 L0_ffn_W_r, L0_ffn_W_k, L0_ffn_W_v, L0_ffn_mu_r, L0_ffn_mu_k):
    f32 = np.float32
    x = np.ascontiguousarray(np.asarray(L0_attn_x, f32))
    wq_h = _dup_rows(np.asarray(L0_attn_Wq, f32))
    wk_h = _dup_rows(np.asarray(L0_attn_Wk, f32))
    wv_h = np.ascontiguousarray(np.asarray(L0_attn_Wv, f32).T)
    wo_h = np.ascontiguousarray(np.asarray(L0_attn_Wo, f32).T)
    wr_h = np.ascontiguousarray(np.asarray(L0_ffn_W_r, f32).T)
    wkf_h = np.ascontiguousarray(np.asarray(L0_ffn_W_k, f32).T)
    wvf_h = np.ascontiguousarray(np.asarray(L0_ffn_W_v, f32).T)
    mur_h = np.ascontiguousarray(np.asarray(L0_ffn_mu_r, f32).reshape(8, 128).T)
    muk_h = np.ascontiguousarray(np.asarray(L0_ffn_mu_k, f32).reshape(8, 128).T)
    trigk_h = _trig(np.arange(SKV))
    identb_h = np.eye(128, dtype=bfdt)
    identf_h = np.eye(128, dtype=np.float32)
    u0m_h = np.zeros((128, 2), np.float32)
    u0m_h[0:64, 0] = 1
    u0m_h[64:128, 1] = 1
    zrow = np.zeros((1, D), f32)
    trigq0 = np.ascontiguousarray(np.concatenate(
        [np.zeros((128, 1), np.float32), _trig(np.arange(OWN))], axis=1))
    trigq1 = np.ascontiguousarray(_trig(np.arange(OWN - 1, SKV)))

    in_maps = []
    for core in range(8):
        b, s = core // 2, core % 2
        if s == 0:
            xq_h = np.ascontiguousarray(np.concatenate([zrow, x[b, 0:OWN]], 0))
            trigq_h, bm = trigq0, np.zeros((1, 1), f32)
        else:
            xq_h = np.ascontiguousarray(x[b, OWN - 1:SKV])
            trigq_h, bm = trigq1, np.ones((1, 1), f32)
        in_maps.append({
            "xkv": np.ascontiguousarray(x[b]), "xq": xq_h,
            "wq": wq_h, "wk": wk_h, "wv": wv_h, "wo": wo_h,
            "wr": wr_h, "wkf": wkf_h, "wvf": wvf_h,
            "mur": mur_h, "muk": muk_h,
            "trigk": trigk_h, "trigq": trigq_h,
            "identb": identb_h, "identf": identf_h,
            "u0m": u0m_h, "bmask": bm,
        })
    return in_maps


def kernel(L0_attn_x, L0_attn_Wq, L0_attn_Wk, L0_attn_Wv, L0_attn_Wo,
           L0_ffn_W_r, L0_ffn_W_k, L0_ffn_W_v, L0_ffn_mu_r, L0_ffn_mu_k):
    in_maps = make_in_maps(L0_attn_x, L0_attn_Wq, L0_attn_Wk, L0_attn_Wv,
                           L0_attn_Wo, L0_ffn_W_r, L0_ffn_W_k, L0_ffn_W_v,
                           L0_ffn_mu_r, L0_ffn_mu_k)
    nc = build_program()
    res = run_bass_kernel_spmd(nc, in_maps, core_ids=list(range(8)))
    out = np.empty((B, SKV, D), np.asarray(res.results[0]["out"]).dtype)
    for core in range(8):
        b, s = core // 2, core % 2
        out[b, OWN * s:OWN * (s + 1)] = res.results[core]["out"]
    return out


if __name__ == "__main__":
    from reference import setup_inputs
    ins = setup_inputs(0)
    o = kernel(**{k: np.asarray(v) for k, v in ins.items()})
    print(o.shape, o.dtype, float(np.abs(o).mean()))


# version 17
# speedup vs baseline: 1.0558x; 1.0081x over previous
import math
from contextlib import ExitStack

import numpy as np
import ml_dtypes

import concourse.bass as bass
import concourse.mybir as mybir
from concourse.bass_utils import run_bass_kernel_spmd

F32 = mybir.dt.float32
BF16 = mybir.dt.bfloat16
AF = mybir.ActivationFunctionType
ALU = mybir.AluOpType
AX = mybir.AxisListType
bfdt = ml_dtypes.bfloat16

B, SKV, D, H, HD = 4, 4096, 1024, 16, 64
OWN = 2048
W1 = OWN + 1
QT = 512
NQT = OWN // QT
KC = SKV // 128
NG = SKV // 512
EPS = 1e-5

ENG_SEM = {"t": "sT", "v": "sV", "s": "sS", "y": "sY", "g": "sG"}
ENG_ATTR = {"t": "tensor", "v": "vector", "s": "scalar", "y": "sync", "g": "gpsimd"}
DMA_ENG = ("y", "g")


class Em:

    def __init__(self, sems, counters, lastw, gate=()):
        self.sems = sems
        self.c = counters
        self.lastw = lastw
        self.gate = list(gate)
        self.ops = {e: [] for e in ENG_SEM}

    def st(self, eng, fns, deps=()):
        if not isinstance(fns, (list, tuple)):
            fns = [fns]
        sem = ENG_SEM[eng]
        waits = []
        for d in list(deps) + self.gate:
            if d is None:
                continue
            dsem, dval = d
            if dsem == sem and eng not in DMA_ENG:
                continue
            key = (eng, dsem)
            if self.lastw.get(key, -1) >= dval:
                continue
            self.lastw[key] = dval
            waits.append((dsem, dval))
        pre = self.c[sem]
        if eng in DMA_ENG:
            if pre > 0 and self.lastw.get((eng, sem), -1) < pre:
                self.lastw[(eng, sem)] = pre
                waits.append((sem, pre))
            self.c[sem] += 16 * len(fns)
        else:
            self.c[sem] += len(fns)
        self.ops[eng].append((waits, list(fns), eng in DMA_ENG, pre))
        return (sem, self.c[sem])

    def raw(self, eng, waits, fns, inc):
        sem = ENG_SEM[eng]
        self.c[sem] += inc
        self.ops[eng].append((waits, list(fns), False, ("raw", inc)))
        return (sem, self.c[sem])

    def play(self, eng, engobj):
        sem = self.sems[ENG_SEM[eng]]
        own = ENG_SEM[eng]
        serial = eng in ("v", "s")
        for waits, fns, is_dma, pre in self.ops[eng]:
            for (s, v) in waits:
                engobj.wait_ge(self.sems[s], v)
            if is_dma:
                for f in fns:
                    f(engobj).then_inc(sem, 16)
            elif isinstance(pre, tuple):
                insts = [f(engobj) for f in fns]
                if insts and pre[1]:
                    insts[-1].then_inc(sem, pre[1])
            else:
                for i, f in enumerate(fns):
                    if serial and pre + i > 0:
                        engobj.wait_ge(sem, pre + i)
                    f(engobj).then_inc(sem, 1)

    def run(self, nc):
        with nc.Block() as blk:
            for eng in ("y", "g", "t", "v", "s"):
                if self.ops[eng]:
                    getattr(blk, ENG_ATTR[eng])(
                        lambda e, _eng=eng: self.play(_eng, e))


def build_program(dbg=False):
    nc = bass.Bass("TRN2", target_bir_lowering=False)

    def din(name, shape, dt=F32):
        return nc.dram_tensor(name, shape, dt, kind="ExternalInput").ap()

    xkv = din("xkv", [SKV, D])
    xq = din("xq", [W1, D])
    wq = din("wq", [D, D])
    wk = din("wk", [D, D])
    wv = din("wv", [D, D])
    wo = din("wo", [D, D])
    wr = din("wr", [D, D])
    wkf = din("wkf", [D, D])
    wvf = din("wvf", [D, D])
    mur_i = din("mur", [128, 8])
    muk_i = din("muk", [128, 8])
    trigk_i = din("trigk", [128, SKV], F32)
    trigq_i = din("trigq", [128, W1], F32)
    identb_i = din("identb", [128, 128], BF16)
    identf_i = din("identf", [128, 128], F32)
    u0m_i = din("u0m", [128, 2])
    bmask_i = din("bmask", [1, 1])

    out_d = nc.dram_tensor("out", [OWN, D], F32, kind="ExternalOutput").ap()
    skind = "ExternalOutput" if dbg else "Internal"
    x2_d = nc.dram_tensor("x2d", [OWN, D], F32, kind=skind).ap()
    ukhi_d = nc.dram_tensor("ukhi", [128, 4, SKV], BF16, kind="Internal").ap()
    vhi_d = nc.dram_tensor("vhi", [SKV, 512], BF16, kind="Internal").ap()
    if dbg:
        dctx_d = nc.dram_tensor("dctx", [128, 8, OWN], BF16,
                                kind="ExternalOutput").ap()
        dx2r0_d = nc.dram_tensor("dx2r0", [1, D], F32, kind="ExternalOutput").ap()
        dU0c_d = nc.dram_tensor("dU0c", [2, 128, 4, 8], BF16,
                                kind="ExternalOutput").ap()
        dP0_d = nc.dram_tensor("dP0", [2, 8, SKV], BF16,
                               kind="ExternalOutput").ap()
        dc0_d = nc.dram_tensor("dc0", [2, 1, 512], BF16,
                               kind="ExternalOutput").ap()
        dct_d = nc.dram_tensor("dct", [128, 8], BF16,
                               kind="ExternalOutput").ap()
        dat0_d = nc.dram_tensor("dat0", [1, D], F32,
                                kind="ExternalOutput").ap()
        dcolb_d = nc.dram_tensor("dcolb", [128, 8], F32,
                                 kind="ExternalOutput").ap()

    sems = {}
    counters = {s: 0 for s in ENG_SEM.values()}
    lastw = {}
    war = {}

    gate_hold = [[]]

    def em():
        return Em(sems, counters, lastw, gate_hold[0])

    def set_gate():
        gate_hold[0] = [(s, counters[s]) for s in counters if counters[s] > 0]

    with ExitStack() as top:
        for s in ("sT", "sV", "sS", "sY", "sG"):
            sems[s] = top.enter_context(nc.semaphore(s))

        def sb(name, shape, dt, ctx):
            return ctx.enter_context(nc.sbuf_tensor("s_" + name, shape, dt))

        def ps(name, shape, dt, ctx):
            return ctx.enter_context(nc.psum_tensor(name, shape, dt))

        identb = sb("identb", [128, 128], BF16, top)
        identf = sb("identf", [128, 128], F32, top)
        u0m = sb("u0m", [128, 2], F32, top)
        mur = sb("mur", [128, 8], F32, top)
        muk = sb("muk", [128, 8], F32, top)
        omur = sb("omur", [128, 8], F32, top)
        omuk = sb("omuk", [128, 8], F32, top)
        bmask = sb("bmask", [1, 1], F32, top)
        junk = sb("junk", [128, 1024], BF16, top)
        stat = sb("stat", [128, 80], F32, top)
        x2r0 = sb("x2r0", [1, D], F32, top)
        xq0 = sb("xq0", [1, D], F32, top)
        cbias = sb("cbias", [128, 2], F32, top)

        SX, SXX, MM, RS, M2 = 0, 16, 32, 48, 64

        def mcol(j, cb=0):
            return stat[:, MM + cb + j:MM + cb + j + 1]

        def rcol(j, cb=0):
            return stat[:, RS + cb + j:RS + cb + j + 1]

        def ln_stats(E, srcs, nsub, deps_each, n=1024.0, cb=0, pp=128):
            toks = []
            for j in range(nsub):
                t1 = E.st("v", lambda v, _j=j: v.tensor_scalar(
                    junk[0:pp, :], srcs(_j), 1.0, 0.0, ALU.mult, ALU.add,
                    accum_out=stat[0:pp, SX + cb + _j:SX + cb + _j + 1]),
                    deps=deps_each[j])
                t2 = E.st("v", lambda v, _j=j: v.scalar_tensor_tensor(
                    junk[0:pp, :], srcs(_j), 1.0, srcs(_j), ALU.mult, ALU.mult,
                    accum_out=stat[0:pp, SXX + cb + _j:SXX + cb + _j + 1]),
                    deps=deps_each[j])
                toks += [t1, t2]
            c0, c1 = cb, cb + nsub
            tm = E.st("v", [
                lambda v: v.tensor_scalar(stat[:, MM + c0:MM + c1],
                                          stat[:, SX + c0:SX + c1], 1.0 / n, 0.0,
                                          ALU.mult, ALU.add),
                lambda v: v.tensor_tensor(stat[:, M2 + c0:M2 + c1],
                                          stat[:, MM + c0:MM + c1],
                                          stat[:, MM + c0:MM + c1], ALU.mult),
                lambda v: v.tensor_scalar(stat[:, RS + c0:RS + c1],
                                          stat[:, SXX + c0:SXX + c1], 1.0 / n, 0.0,
                                          ALU.mult, ALU.add),
                lambda v: v.tensor_tensor(stat[:, RS + c0:RS + c1],
                                          stat[:, RS + c0:RS + c1],
                                          stat[:, M2 + c0:M2 + c1], ALU.subtract),
            ], deps=toks + [war.get("lnrs")])
            trs = E.st("s", [
                lambda s: s.activation(stat[:, RS + c0:RS + c1],
                                       stat[:, RS + c0:RS + c1], AF.Ln,
                                       bias=cbias[:, 0:1]),
                lambda s: s.activation(stat[:, RS + c0:RS + c1],
                                       stat[:, RS + c0:RS + c1], AF.Exp,
                                       scale=-0.5),
            ], deps=[tm])
            war["lnrs"] = trs
            return trs

        ap_scope = ExitStack()
        top.enter_context(ap_scope)
        uk_sb = sb("uk_sb", [128, 4, SKV], BF16, ap_scope)
        vbuf = sb("vbuf", [128, KC, 8, 65], BF16, ap_scope)
        a_scope = ExitStack()
        ap_scope.enter_context(a_scope)
        wk_sb = sb("wk_sb", [128, 8, 1024], BF16, a_scope)
        wv_sb = sb("wv_sb", [128, 8, 1024], BF16, a_scope)
        trigk = sb("trigk", [128, SKV], BF16, a_scope)
        xg = sb("xg", [128, 2, 4, 1024], F32, a_scope)
        h1g = sb("h1g", [128, 2, 4, 1024], BF16, a_scope)
        h1gT = sb("h1gT", [128, 8, 512], BF16, a_scope)
        ukst = sb("ukst", [128, 2, 512], BF16, a_scope)
        vst = sb("vst", [128, 2, 512], BF16, a_scope)

        E = em()
        tok_const = E.st("y", [
            lambda y: y.dma_start(identb[:, :], identb_i),
            lambda y: y.dma_start(identf[:, :], identf_i),
            lambda y: y.dma_start(u0m[:, :], u0m_i),
            lambda y: y.dma_start(mur[:, :], mur_i),
            lambda y: y.dma_start(muk[:, :], muk_i),
            lambda y: y.dma_start(bmask[:, :], bmask_i),
            lambda y: y.dma_start(xq0[:, :], xq[0:1, :]),
        ])
        E.st("s", [
            lambda s: s.activation(omur[:, :], mur[:, :], AF.Identity,
                                   bias=1.0, scale=-1.0),
            lambda s: s.activation(omuk[:, :], muk[:, :], AF.Identity,
                                   bias=1.0, scale=-1.0),
        ], deps=[tok_const])
        _e1 = float(np.float32(EPS))
        _e2 = float(np.float32(EPS - _e1))
        tok_kn0 = E.st("v", [
            lambda v: v.memset(cbias[:, 0:1], _e1),
            lambda v: v.tensor_scalar(cbias[:, 0:1], cbias[:, 0:1], 1.0, _e2,
                                      ALU.mult, ALU.add),
            lambda v: v.memset(cbias[:, 1:2], 1e-30),
            lambda v: v.memset(stat[:, :], 0.0),
        ], deps=[])
        tok_vones = E.st("v", lambda v: v.memset(vbuf[:, :, :, 64:65], 1.0),
                         deps=[])
        tok_wkv = E.st("g", [
            lambda g: g.dma_start(
                wk_sb[:, :, :], wk.rearrange("(c p) o -> p c o", p=128)),
            lambda g: g.dma_start(
                wv_sb[:, :, :], wv.rearrange("(c p) o -> p c o", p=128)),
        ])
        tok_trigk = E.st("g", lambda g: g.dma_start(trigk[:, :], trigk_i))
        E.run(nc)

        E = em()
        a_ps = ExitStack()
        a_scope.enter_context(a_ps)
        ps_tp = [ps(f"ps_tpA{i}", [128, 1024], BF16, a_ps) for i in range(2)]
        ps_ko = [ps(f"ps_koA{i}", [128, 512], F32, a_ps) for i in range(2)]
        ps_v = ps("ps_vA", [128, 1024], F32, a_ps)

        def emit_xg_dma(g, dep):
            sl = g % 2
            return E.st("y", lambda y, _g=g, _sl=sl: y.dma_start(
                xg[:, _sl, :, :],
                xkv[512 * _g:512 * (_g + 1), :].rearrange("(a p) o -> p a o", p=128)),
                deps=[dep])

        tok_dma = {0: emit_xg_dma(0, tok_wkv), 1: emit_xg_dma(1, None)}

        def a_accum(g, idx):
            sl, cb = g % 2, (g % 2) * 4
            j, kind = idx // 2, idx % 2
            if kind == 0:
                return E.st("v", lambda v, _j=j, _sl=sl, _cb=cb: v.tensor_scalar(
                    junk[:, :], xg[:, _sl, _j, :], 1.0, 0.0, ALU.mult, ALU.add,
                    accum_out=stat[:, SX + _cb + _j:SX + _cb + _j + 1]),
                    deps=[tok_dma[g]])
            return E.st("v", lambda v, _j=j, _sl=sl, _cb=cb:
                        v.scalar_tensor_tensor(
                junk[:, :], xg[:, _sl, _j, :], 1.0, xg[:, _sl, _j, :],
                ALU.mult, ALU.mult,
                accum_out=stat[:, SXX + _cb + _j:SXX + _cb + _j + 1]),
                deps=[tok_dma[g]])

        def a_finish(g, toks):
            cb = (g % 2) * 4
            c0, c1 = cb, cb + 4
            tm = E.st("v", [
                lambda v: v.tensor_scalar(stat[:, MM + c0:MM + c1],
                                          stat[:, SX + c0:SX + c1],
                                          1.0 / 1024.0, 0.0, ALU.mult, ALU.add),
                lambda v: v.tensor_tensor(stat[:, M2 + c0:M2 + c1],
                                          stat[:, MM + c0:MM + c1],
                                          stat[:, MM + c0:MM + c1], ALU.mult),
                lambda v: v.tensor_scalar(stat[:, RS + c0:RS + c1],
                                          stat[:, SXX + c0:SXX + c1],
                                          1.0 / 1024.0, 0.0, ALU.mult, ALU.add),
                lambda v: v.tensor_tensor(stat[:, RS + c0:RS + c1],
                                          stat[:, RS + c0:RS + c1],
                                          stat[:, M2 + c0:M2 + c1],
                                          ALU.subtract),
            ], deps=toks + [war.get("lnrs")])
            trs = E.st("s", [
                lambda s: s.activation(stat[:, RS + c0:RS + c1],
                                       stat[:, RS + c0:RS + c1], AF.Ln,
                                       bias=cbias[:, 0:1]),
                lambda s: s.activation(stat[:, RS + c0:RS + c1],
                                       stat[:, RS + c0:RS + c1], AF.Exp,
                                       scale=-0.5),
            ], deps=[tm])
            war["lnrs"] = trs
            return trs

        def a_norm(g, j, extra):
            sl, cb = g % 2, (g % 2) * 4
            return E.st("v", lambda v, _j=j, _sl=sl, _cb=cb: v.tensor_scalar(
                h1g[:, _sl, _j, :], xg[:, _sl, _j, :], mcol(_j, _cb),
                rcol(_j, _cb), ALU.subtract, ALU.mult), deps=extra)

        toks0 = [a_accum(0, i) for i in range(8)]
        trs0a = a_finish(0, toks0)
        tok_norm = {0: [a_norm(0, j, [trs0a]) for j in range(4)]}
        for g in range(NG):
            sl = g % 2
            if g + 2 < NG:
                tok_dma[g + 2] = emit_xg_dma(g + 2, tok_norm[g][-1])
            tok_ev = []
            for j in range(4):
                bk = j % 2
                tt = E.st("t", [
                    (lambda t, _j=j, _c=c, _bk=bk, _sl=sl: t.matmul(
                        ps_tp[_bk][:, 128 * _c:128 * (_c + 1)],
                        h1g[:, _sl, _j, 128 * _c:128 * (_c + 1)], identb[:, :],
                        start=True, stop=True, is_transpose=True))
                    for c in range(8)
                ], deps=[tok_norm[g][j], war.get(("ps_tpA", bk)), tok_const])
                tev = E.st("v", lambda v, _j=j, _bk=bk: v.tensor_copy(
                    h1gT[:, :, 128 * _j:128 * (_j + 1)],
                    ps_tp[_bk][:, :].rearrange("p (c q) -> p c q", q=128)),
                    deps=[tt])
                war[("ps_tpA", bk)] = tev
                tok_ev.append(tev)
            tok_nxt = []
            for o in range(8):
                bk = o % 2
                tmm = E.st("t", [
                    (lambda t, _o=o, _c=c, _bk=bk: t.matmul(
                        ps_ko[_bk][:, :], wk_sb[:, _c, 128 * _o:128 * (_o + 1)],
                        h1gT[:, _c, :], start=(_c == 0), stop=(_c == 7)))
                    for c in range(8)
                ], deps=tok_ev + [war.get(("ps_koA", bk)), tok_trigk])
                if o < 4:
                    dst = uk_sb[:, o, 512 * g:512 * (g + 1)]
                else:
                    dst = ukst[:, o % 2, :]
                tev = E.st("v", lambda v, _bk=bk, _g=g, _dst=dst: v.tensor_tensor(
                    _dst, ps_ko[_bk][:, :], trigk[:, 512 * _g:512 * (_g + 1)],
                    ALU.mult),
                    deps=[tmm, war.get(("ukst", o % 2)) if o >= 4 else None])
                war[("ps_koA", bk)] = tev
                if o >= 4:
                    tst = E.st("y", lambda y, _o=o, _g=g: y.dma_start(
                        ukhi_d[:, _o - 4, 512 * _g:512 * (_g + 1)],
                        ukst[:, _o % 2, :]), deps=[tev])
                    war[("ukst", o % 2)] = tst
                if g + 1 < NG:
                    tok_nxt.append(a_accum(g + 1, o))
            if g + 1 < NG:
                trs_n = a_finish(g + 1, tok_nxt)
            for j in range(4):
                kc = 4 * g + j
                fns = []
                for half in range(2):
                    for c in range(8):
                        fns.append(lambda t, _j=j, _c=c, _h=half: t.matmul(
                            ps_v[:, 512 * _h:512 * (_h + 1)],
                            h1gT[:, _c, 128 * _j:128 * (_j + 1)],
                            wv_sb[:, _c, 512 * _h:512 * (_h + 1)],
                            start=(_c == 0), stop=(_c == 7)))
                tvm = E.st("t", fns, deps=tok_ev + [war.get("ps_vA")])
                tlo = E.st("v", lambda v, _kc=kc: v.tensor_copy(
                    vbuf[:, _kc, :, 0:64],
                    ps_v[:, 0:512].rearrange("p (h d) -> p h d", d=64)),
                    deps=[tvm, tok_vones])
                thi = E.st("v", lambda v, _kc=kc: v.tensor_copy(
                    vst[:, _kc % 2, :], ps_v[:, 512:1024]),
                    deps=[tvm, war.get(("vst", kc % 2))])
                war["ps_vA"] = thi
                tvs = E.st("y", lambda y, _kc=kc: y.dma_start(
                    vhi_d[128 * _kc:128 * (_kc + 1), :], vst[:, _kc % 2, :]),
                    deps=[thi])
                war[("vst", kc % 2)] = tvs
                if g + 1 < NG:
                    tok_norm.setdefault(g + 1, []).append(
                        a_norm(g + 1, j, [trs_n, tok_ev[-1]]))
        tok_A_ukst = (war[("ukst", 0)], war[("ukst", 1)])
        tok_A_vst = (war[("vst", 0)], war[("vst", 1)])
        E.run(nc)
        a_ps.close()
        a_scope.close()
        set_gate()

        p_scope = ExitStack()
        ap_scope.enter_context(p_scope)
        wo_sb = sb("wo_sb", [128, 8, 1024], BF16, p_scope)
        ctxF = sb("ctxF", [128, 8, OWN], BF16, p_scope)
        ctx0T = sb("ctx0T", [128, 16], BF16, p_scope)
        pa_scope = ExitStack()
        p_scope.enter_context(pa_scope)
        wq_sb = sb("wq_sb", [128, 8, 1024], BF16, pa_scope)
        trigq = sb("trigq", [128, W1], BF16, pa_scope)
        uqt = sb("uqt", [128, 4, 512], BF16, pa_scope)
        xqg = sb("xqg", [128, 4, 1024], F32, pa_scope)
        h1q = sb("h1q", [128, 4, 1024], BF16, pa_scope)
        h1qT = sb("h1qT", [128, 8, 512], BF16, pa_scope)
        pT = sb("pT", [128, 4, 1024], BF16, pa_scope)
        rl = sb("rl", [128, 1], F32, pa_scope)
        rden = sb("rden", [128, 4], F32, pa_scope)
        ctx_sb = sb("ctx_sb", [128, 4, 64], BF16, pa_scope)
        h1q0 = sb("h1q0", [1, 1024], BF16, pa_scope)
        h1q0T = sb("h1q0T", [128, 8], BF16, pa_scope)
        U0c = sb("U0c", [128, 4, 8], BF16, pa_scope)
        P0 = sb("P0", [8, SKV], BF16, pa_scope)
        P0T = sb("P0T", [128, KC, 8], BF16, pa_scope)
        vtmp = sb("vtmp", [128, 2, 512], BF16, pa_scope)
        c0row = sb("c0row", [1, 512], BF16, pa_scope)
        rc0 = sb("rc0", [8, 1], F32, pa_scope)

        pp_ps = ExitStack()
        pa_scope.enter_context(pp_ps)
        ps_s = ps("ps_s", [128, 2048], F32, pp_ps)
        ps_av = [ps(f"ps_av{i}", [128, 512], F32, pp_ps) for i in range(4)]

        for ppass in range(2):
            E = em()
            if ppass == 0:
                tok_w = E.st("g", lambda g: g.dma_start(
                    wq_sb[:, :, :], wq.rearrange("(c p) o -> p c o", p=128)))
                tok_trq = E.st("g", lambda g: g.dma_start(trigq[:, :], trigq_i))
                tok_ukrdy = tok_w
                tok_vrdy = tok_vones
            else:
                tok_w = E.st("y", [
                    (lambda y, _j=j: y.dma_start(uk_sb[:, _j, :], ukhi_d[:, _j, :]))
                    for j in range(4)
                ], deps=[tok_A_ukst[0], tok_A_ukst[1], war.get("uk_rd")])
                tok_ukrdy = tok_w
                vt = None
                for kc in range(KC):
                    tvd = E.st("y", lambda y, _kc=kc: y.dma_start(
                        vtmp[:, _kc % 2, :], vhi_d[128 * _kc:128 * (_kc + 1), :]),
                        deps=[tok_A_vst[0], tok_A_vst[1],
                              war.get(("vtmp", kc % 2))])
                    vt = E.st("v", lambda v, _kc=kc: v.tensor_copy(
                        vbuf[:, _kc, :, 0:64],
                        vtmp[:, _kc % 2, :].rearrange("p (h d) -> p h d", d=64)),
                        deps=[tvd, war.get("vbuf_rd")])
                    war[("vtmp", kc % 2)] = vt
                tok_vrdy = vt
                tok_trq = None

            for qt in range(NQT):
                r0 = 1 + QT * qt
                tq = E.st("y", lambda y, _r0=r0: y.dma_start(
                    xqg[:, :, :],
                    xq[_r0:_r0 + QT, :].rearrange("(a p) o -> p a o", p=128)),
                    deps=[war.get("xqg"), tok_trq])
                trs = ln_stats(E, lambda j: xqg[:, j, :], 4, [[tq]] * 4)
                tok_norm = []
                for j in range(4):
                    tok_norm.append(E.st("v", lambda v, _j=j: v.tensor_scalar(
                        h1q[:, _j, :], xqg[:, _j, :], mcol(_j), rcol(_j),
                        ALU.subtract, ALU.mult), deps=[trs]))
                war["xqg"] = tok_norm[-1]
                tok_ev = []
                for j in range(4):
                    hfj = j % 2
                    tt = E.st("t", [
                        (lambda t, _j=j, _c=c, _hf=hfj: t.matmul(
                            ps_s[:, 1024 * _hf + 128 * _c:
                                 1024 * _hf + 128 * (_c + 1)],
                            h1q[:, _j, 128 * _c:128 * (_c + 1)], identb[:, :],
                            start=True, stop=True))
                        for c in range(8)
                    ], deps=[tok_norm[j], war.get(("ps_s", hfj))])
                    tev = E.st("v", lambda v, _j=j, _hf=hfj: v.tensor_copy(
                        h1qT[:, :, 128 * _j:128 * (_j + 1)],
                        ps_s[:, 1024 * _hf:1024 * (_hf + 1)].rearrange(
                            "p (c q) -> p c q", q=128)),
                        deps=[tt])
                    war[("ps_s", hfj)] = tev
                    tok_ev.append(tev)
                tok_uq = []
                for oo in range(4):
                    o = 4 * ppass + oo
                    tmm = E.st("t", [
                        (lambda t, _o=o, _c=c, _oo=oo: t.matmul(
                            ps_av[_oo][:, :],
                            wq_sb[:, _c, 128 * _o:128 * (_o + 1)],
                            h1qT[:, _c, :], start=(_c == 0), stop=(_c == 7)))
                        for c in range(8)
                    ], deps=tok_ev + [war.get(("ps_av", oo)), tok_w])
                    tev = E.st("v", lambda v, _oo=oo, _r0=r0: v.tensor_tensor(
                        uqt[:, _oo, :], ps_av[_oo][:, :],
                        trigq[:, _r0:_r0 + QT],
                        ALU.mult), deps=[tmm, war.get("uqt")])
                    war[("ps_av", oo)] = tev
                    tok_uq.append(tev)

                for hh in range(8):
                    h = 8 * ppass + hh
                    oo, m = hh // 2, hh % 2
                    tok_exp = {}
                    tok_pv = {}
                    def _emit_pv(_gg):
                        _sl = _gg % 4
                        tok_pv[_gg] = E.st("t", [
                            (lambda t, _jj=jj, _qq=qq, _g=_gg, _hh=hh, _s=_sl:
                             t.matmul(
                                ps_av[_qq][:, 0:65],
                                pT[:, _s, 512 * _jj + 128 * _qq:
                                   512 * _jj + 128 * _qq + 128],
                                vbuf[:, 2 * _g + _jj, _hh, :],
                                start=(2 * _g + _jj == 0),
                                stop=(2 * _g + _jj == KC - 1)))
                            for jj in range(2) for qq in range(4)
                        ], deps=[tok_exp[_gg], tok_vrdy]
                           + ([war.get(("ps_av", q2)) for q2 in range(4)]
                              if _gg == 0 else []))
                        war[("pT", _sl)] = tok_pv[_gg]

                    for gg in range(16):
                        hf = gg % 2
                        sl = gg % 4
                        tqk = E.st("t", [
                            (lambda t, _jj=jj, _gg=gg, _oo=oo, _m=m, _hf=hf:
                             t.matmul(
                                ps_s[:, 1024 * _hf + 512 * _jj:
                                     1024 * _hf + 512 * (_jj + 1)],
                                uk_sb[64 * _m:64 * _m + 64, _oo,
                                      128 * (2 * _gg + _jj):
                                      128 * (2 * _gg + _jj) + 128],
                                uqt[64 * _m:64 * _m + 64, _oo, :],
                                start=True, stop=True,
                                tile_position=(64 * _m, 0)))
                            for jj in range(2)
                        ], deps=[tok_uq[oo], tok_ukrdy,
                                 war.get(("ps_s", hf))])
                        tok_exp[gg] = E.st("s", lambda s, _sl=sl, _h=h, _hf=hf:
                            s.activation(
                                pT[:, _sl, :],
                                ps_s[:, 1024 * _hf:1024 * (_hf + 1)], AF.Exp,
                                scale=0.125),
                            deps=[tqk, war.get(("pT", sl))])
                        war[("ps_s", hf)] = tok_exp[gg]
                        if gg >= 1:
                            _emit_pv(gg - 1)
                    _emit_pv(15)
                    chunk = 4 * ppass + hh // 2
                    tok_dv = []
                    for qq in range(4):
                        tdv = E.st("v", [
                            (lambda v, _qq=qq: v.reciprocal(
                                rden[:, _qq:_qq + 1], ps_av[_qq][:, 64:65])),
                            (lambda v, _qq=qq: v.tensor_scalar(
                                ctx_sb[:, _qq, :], ps_av[_qq][:, 0:64],
                                rden[:, _qq:_qq + 1], 0.0, ALU.mult, ALU.add)),
                        ], deps=[tok_pv[15]])
                        war[("ps_av", qq)] = tdv
                        tok_dv.append(tdv)
                    ttx = E.st("t", [
                        (lambda t, _qq=qq, _m=m: t.matmul(
                            ps_av[_qq][64 * _m:64 * _m + 64, 128:256],
                            ctx_sb[:, _qq, :], identb[:, :],
                            start=True, stop=True))
                        for qq in range(4)
                    ], deps=tok_dv)
                    tcx = E.st("v", [
                        (lambda v, _ck=chunk, _qt=qt, _m=m, _qq=qq:
                         v.tensor_copy(
                             ctxF[64 * _m:64 * _m + 64, _ck,
                                  QT * _qt + 128 * _qq:
                                  QT * _qt + 128 * (_qq + 1)],
                             ps_av[_qq][64 * _m:64 * _m + 64, 128:256]))
                        for qq in range(4)
                    ], deps=[ttx])
                    for qq in range(4):
                        war[("ps_av", qq)] = tcx
                war["uqt"] = tok_pv[15]

            trs0 = ln_stats(E, lambda j: xq0[0:1, :], 1, [[tok_const]], cb=8, pp=1)
            tn0 = E.st("v", lambda v: v.tensor_scalar(
                h1q0[:, :], xq0[0:1, :], stat[0:1, MM + 8:MM + 9],
                stat[0:1, RS + 8:RS + 9], ALU.subtract, ALU.mult), deps=[trs0])
            tt0 = E.st("t", [
                (lambda t, _c=c: t.matmul(
                    ps_av[0][:, _c:_c + 1], h1q0[0:1, 128 * _c:128 * (_c + 1)],
                    identb[0:1, 0:1], start=True, stop=True))
                for c in range(8)
            ], deps=[tn0, war.get(("ps_av", 0))])
            tev0 = E.st("v", lambda v: v.tensor_copy(
                h1q0T[:, 0:8], ps_av[0][:, 0:8]),
                deps=[tt0])
            war[("ps_av", 0)] = tev0
            tu0m = E.st("v", lambda v: v.memset(U0c[:, :, :], 0.0),
                        deps=[war.get("U0c_rd"), war.get("dbg_g")])
            tok_u0 = []
            for oo in range(4):
                o = 4 * ppass + oo
                tmm = E.st("t", [
                    (lambda t, _o=o, _c=c: t.matmul(
                        ps_av[1][:, 0:1], wq_sb[:, _c, 128 * _o:128 * (_o + 1)],
                        h1q0T[:, _c:_c + 1], start=(_c == 0), stop=(_c == 7)))
                    for c in range(8)
                ], deps=[tev0, war.get(("ps_av", 1))])
                tum = E.st("v", lambda v: v.tensor_tensor(
                    rl[:, 0:1], ps_av[1][:, 0:1], trigq[:, 0:1], ALU.mult),
                    deps=[tmm])
                war[("ps_av", 1)] = tum
                for jj in range(2):
                    tok_u0.append(E.st("v", lambda v, _oo=oo, _jj=jj:
                                       v.tensor_scalar(
                                           U0c[:, _oo, 2 * _oo + _jj:2 * _oo + _jj + 1],
                                           rl[:, 0:1], u0m[:, _jj:_jj + 1], 0.0,
                                           ALU.mult, ALU.add),
                                       deps=[tum, tu0m]))
            if dbg:
                war["dbg_g"] = E.st("g", lambda g, _p=ppass: g.dma_start(
                    dU0c_d[_p], U0c[:, :, :]), deps=tok_u0)
            tok_p0 = []
            tsm = None
            for c8 in range(8):
                tsm = E.st("t", [
                    (lambda t, _oo=oo, _c8=c8: t.matmul(
                        ps_av[2][0:8, :], U0c[:, _oo, :],
                        uk_sb[:, _oo, 512 * _c8:512 * (_c8 + 1)],
                        start=(_oo == 0), stop=(_oo == 3)))
                    for oo in range(4)
                ], deps=tok_u0 + [war.get(("ps_av", 2)), tok_ukrdy])
                texp = E.st("s", lambda s, _c8=c8: s.activation(
                    P0[:, 512 * _c8:512 * (_c8 + 1)], ps_av[2][0:8, :], AF.Exp,
                    scale=0.125),
                    deps=[tsm, war.get("P0_rdv"), war.get("P0_rdt"),
                          war.get("dbg_g")])
                war[("ps_av", 2)] = texp
                tok_p0.append(texp)
            tnp0 = E.st("v", [
                lambda v: v.tensor_reduce(rc0[:, :], P0[:, :], AX.X, ALU.add),
                lambda v: v.reciprocal(rc0[:, :], rc0[:, :]),
                lambda v: v.tensor_scalar(P0[:, :], P0[:, :], rc0[:, :], 0.0,
                                          ALU.mult, ALU.add),
            ], deps=tok_p0)
            if dbg:
                war["dbg_g"] = E.st("g", lambda g, _p=ppass: g.dma_start(
                    dP0_d[_p], P0[:, :]), deps=[tnp0])
            war["P0_rdv"] = tnp0
            war["U0c_rd"] = tsm
            war["uk_rd"] = tsm
            ttr0 = E.st("t", [
                (lambda t, _kc=kc: t.matmul(
                    ps_av[3][:, 8 * _kc:8 * (_kc + 1)],
                    P0[:, 128 * _kc:128 * (_kc + 1)], identb[0:8, 0:8],
                    start=True, stop=True))
                for kc in range(KC)
            ], deps=[tnp0, war.get(("ps_av", 3))])
            war["P0_rdt"] = ttr0
            tp0e = E.st("v", lambda v: v.tensor_copy(
                P0T[:, :, :],
                ps_av[3][:, 0:256].rearrange("p (k e) -> p k e", e=8)),
                deps=[ttr0, war.get("P0T_rd")])
            war[("ps_av", 3)] = tp0e
            tpv = E.st("t", [
                (lambda t, _kc=kc, _j=j: t.matmul(
                    ps_av[0][0:1, 64 * _j:64 * _j + 64],
                    P0T[:, _kc, _j:_j + 1], vbuf[:, _kc, _j, 0:64],
                    start=(_kc == 0), stop=(_kc == KC - 1)))
                for j in range(8) for kc in range(KC)
            ], deps=[tp0e, tok_vrdy, war.get(("ps_av", 0))])
            war["P0T_rd"] = tpv
            war["vbuf_rd"] = tpv
            tc0 = E.st("v", lambda v: v.tensor_copy(
                c0row[0:1, :], ps_av[0][0:1, 0:512]),
                deps=[tpv, war.get("dbg_g")])
            war[("ps_av", 0)] = tc0
            if dbg:
                war["dbg_g"] = E.st("g", lambda g, _p=ppass: g.dma_start(
                    dc0_d[_p], c0row[0:1, :]), deps=[tc0])
            ttc = E.st("t", [
                (lambda t, _h=h, _half=half: t.matmul(
                    ps_av[1][64 * _half:64 * _half + 64,
                             8 * _half + _h:8 * _half + _h + 1],
                    c0row[0:1, 64 * _h:64 * _h + 64], identb[0:1, 0:1],
                    start=True, stop=True))
                for half in range(2) for h in range(8)
            ], deps=[tc0, war.get(("ps_av", 1))])
            tce = E.st("v", [
                lambda v, _p=ppass: v.tensor_copy(
                    ctx0T[0:64, 4 * _p:4 * _p + 4].rearrange(
                        "p (c e) -> p c e", e=1),
                    ps_av[1][0:64, 0:8].rearrange(
                        "p (c e) -> p c e", e=2)[:, :, 0:1]),
                lambda v, _p=ppass: v.tensor_copy(
                    ctx0T[64:128, 4 * _p:4 * _p + 4].rearrange(
                        "p (c e) -> p c e", e=1),
                    ps_av[1][64:128, 8:16].rearrange(
                        "p (c e) -> p c e", e=2)[:, :, 1:2]),
            ], deps=[ttc])
            war[("ps_av", 1)] = tce
            war["ctx0T"] = tce
            E.run(nc)

        pp_ps.close()
        pa_scope.close()
        set_gate()

        e_scope = ExitStack()
        p_scope.enter_context(e_scope)
        atF = sb("atF", [128, 8, 512], F32, e_scope)
        at_sb = sb("at_sb", [128, 1024], F32, e_scope)
        xqE = sb("xqE", [128, 4, 1024], F32, e_scope)
        x2st = sb("x2st", [128, 2, 4, 1024], F32, e_scope)
        at0 = sb("at0", [1, 1024], F32, e_scope)
        e_ps = ExitStack()
        e_scope.enter_context(e_ps)
        ps_wo = [ps(f"ps_wo{i}", [128, 512], F32, e_ps) for i in range(2)]
        ps_tm = [ps(f"ps_tm{i}", [128, 1024], F32, e_ps) for i in range(2)]

        E = em()
        tok_wo = E.st("g", lambda g: g.dma_start(
            wo_sb[:, :, :], wo.rearrange("(c p) o -> p c o", p=128)))
        colb = sb("colb", [128, 8], F32, e_scope)
        tcl = []
        for o in range(8):
            bk = o % 2
            fns = []
            for c in range(8):
                fns.append(lambda t, _o=o, _c=c, _bk=bk: t.matmul(
                    ps_wo[_bk][:, 0:1],
                    wo_sb[:, _c, 128 * _o:128 * (_o + 1)],
                    ctx0T[:, _c:_c + 1],
                    start=(_c == 0), stop=(_c == 7)))
            tmo = E.st("t", fns, deps=[war["ctx0T"], tok_wo,
                                       war.get(("ps_wo", bk))])
            tce2 = E.st("v", lambda v, _o=o, _bk=bk: v.tensor_copy(
                colb[:, _o:_o + 1], ps_wo[_bk][:, 0:1]), deps=[tmo])
            war[("ps_wo", bk)] = tce2
            tcl.append(tce2)
        t0tr = E.st("t", [
            (lambda t, _o=o: t.matmul(
                ps_tm[0][0:1, 128 * _o:128 * (_o + 1)], colb[:, _o:_o + 1],
                identf[:, :], start=True, stop=True,
                is_transpose=True))
            for o in range(8)
        ], deps=tcl + [war.get(("ps_tm", 0))])
        t0e = E.st("v", lambda v: v.tensor_copy(at0[0:1, :], ps_tm[0][0:1, :]),
                   deps=[t0tr])
        war[("ps_tm", 0)] = t0e
        trs0 = ln_stats(E, lambda j: at0[0:1, :], 1, [[t0e]], cb=8, pp=1)
        tx2r0 = E.st("v", [
            lambda v: v.tensor_scalar(x2r0[:, :], at0[0:1, :],
                                      stat[0:1, MM + 8:MM + 9],
                                      stat[0:1, RS + 8:RS + 9],
                                      ALU.subtract, ALU.mult),
            lambda v: v.tensor_tensor(x2r0[:, :], x2r0[:, :], xq0[0:1, :],
                                      ALU.add),
            lambda v: v.tensor_scalar(x2r0[:, :], x2r0[:, :], bmask[0:1, 0:1],
                                      0.0, ALU.mult, ALU.add),
        ], deps=[trs0, tok_const])
        war["x2r0"] = tx2r0

        for qt in range(NQT):
            r0 = 1 + QT * qt
            tok_nrs = []
            tq = E.st("y", lambda y, _r0=r0: y.dma_start(
                xqE[:, :, :],
                xq[_r0:_r0 + QT, :].rearrange("(a p) o -> p a o", p=128)),
                deps=[war.get("xqE")])
            tok_wom = []
            for o in range(8):
                bk = o % 2
                fns = []
                for c in range(8):
                    fns.append(lambda t, _c=c, _o=o, _bk=bk, _qt=qt:
                               t.matmul(
                                   ps_wo[_bk][:, :],
                                   wo_sb[:, _c, 128 * _o:128 * (_o + 1)],
                                   ctxF[:, _c, QT * _qt:QT * (_qt + 1)],
                                   start=(_c == 0), stop=(_c == 7)))
                tm = E.st("t", fns, deps=[tok_wo, war.get(("ps_wo", bk))])
                tev = E.st("v", lambda v, _o=o, _bk=bk: v.tensor_copy(
                    atF[:, _o, :], ps_wo[_bk][:, :]), deps=[tm])
                war[("ps_wo", bk)] = tev
                tok_wom.append(tev)
            for j in range(4):
                bk = j % 2
                ttr = E.st("t", [
                    (lambda t, _j=j, _o=o, _bk=bk: t.matmul(
                        ps_tm[_bk][:, 128 * _o:128 * (_o + 1)],
                        atF[:, _o, 128 * _j:128 * (_j + 1)], identf[:, :],
                        start=True, stop=True, is_transpose=True))
                    for o in range(8)
                ], deps=tok_wom + [war.get(("ps_tm", bk))])
                tst = E.st("v", lambda v, _bk=bk, _j=j: v.tensor_scalar(
                    at_sb[:, :], ps_tm[_bk][:, :], 1.0, 0.0, ALU.mult, ALU.add,
                    accum_out=stat[:, SX + _j:SX + _j + 1]), deps=[ttr])
                tst2 = E.st("v", lambda v, _j=j: v.scalar_tensor_tensor(
                    junk[:, :], at_sb[:, :], 1.0, at_sb[:, :], ALU.mult,
                    ALU.mult, accum_out=stat[:, SXX + _j:SXX + _j + 1]),
                    deps=[tst])
                war[("ps_tm", bk)] = tst
                tfin = E.st("v", [
                    lambda v, _j=j: v.tensor_scalar(
                        mcol(_j), stat[:, SX + _j:SX + _j + 1], 1.0 / 1024.0,
                        0.0, ALU.mult, ALU.add),
                    lambda v, _j=j: v.tensor_tensor(
                        stat[:, M2 + _j:M2 + _j + 1], mcol(_j), mcol(_j), ALU.mult),
                    lambda v, _j=j: v.tensor_scalar(
                        rcol(_j), stat[:, SXX + _j:SXX + _j + 1], 1.0 / 1024.0,
                        0.0, ALU.mult, ALU.add),
                    lambda v, _j=j: v.tensor_tensor(
                        rcol(_j), rcol(_j), stat[:, M2 + _j:M2 + _j + 1], ALU.subtract),
                ], deps=[tst2, war.get("lnrs")])
                trsE = E.st("s", [
                    lambda s, _j=j: s.activation(rcol(_j), rcol(_j), AF.Ln,
                                                 bias=cbias[:, 0:1]),
                    lambda s, _j=j: s.activation(rcol(_j), rcol(_j), AF.Exp,
                                                 scale=-0.5),
                ], deps=[tfin])
                war["lnrs"] = trsE
                sl = qt % 2
                tnr = E.st("v", [
                    lambda v, _j=j, _sl=sl: v.tensor_scalar(
                        x2st[:, _sl, _j, :], at_sb[:, :], mcol(_j), rcol(_j),
                        ALU.subtract, ALU.mult),
                    lambda v, _j=j, _sl=sl: v.tensor_tensor(
                        x2st[:, _sl, _j, :], x2st[:, _sl, _j, :], xqE[:, _j, :],
                        ALU.add),
                ], deps=[trsE, tq, war.get(("x2st", sl))])
                tok_nrs.append(tnr)
                war["at_sb"] = tnr
            tso = E.st("g", lambda g, _qt=qt, _sl=qt % 2: g.dma_start(
                x2_d[QT * _qt:QT * (_qt + 1), :].rearrange(
                    "(a p) o -> p a o", p=128),
                x2st[:, _sl, :, :]), deps=list(tok_nrs))
            war[("x2st", qt % 2)] = tso
            war["xqE"] = war["at_sb"]
        if dbg:
            E.st("g", [
                lambda g: g.dma_start(dctx_d[:, :, :], ctxF[:, :, :]),
                lambda g: g.dma_start(dx2r0_d, x2r0[:, :]),
                lambda g: g.dma_start(dct_d[:, :], ctx0T[:, 0:8]),
                lambda g: g.dma_start(dat0_d, at0[:, :]),
                lambda g: g.dma_start(dcolb_d, colb[:, :]),
            ], deps=[war["x2r0"]])
        tok_x2_done = [war[("x2st", 0)], war[("x2st", 1)]]
        E.run(nc)
        e_ps.close()
        e_scope.close()
        p_scope.close()
        ap_scope.close()
        set_gate()

        f_scope = ExitStack()
        top.enter_context(f_scope)
        wr_sb = sb("wr_sb", [128, 8, 1024], BF16, f_scope)
        wkf_sb = sb("wkf_sb", [128, 8, 1024], BF16, f_scope)
        wvf_sb = sb("wvf_sb", [128, 8, 1024], BF16, f_scope)
        h2T = sb("h2T", [128, 8, W1], BF16, f_scope)
        ffnout = sb("ffnout", [128, 16, 1024], BF16, f_scope)
        x2t = sb("x2t", [128, 2, 2, 1024], F32, f_scope)
        h2b = sb("h2b", [128, 1024], BF16, f_scope)
        xrT = sb("xrT", [128, 2, 8, 512], BF16, f_scope)
        xkT = sb("xkT", [128, 2, 8, 512], BF16, f_scope)
        kactT = sb("kactT", [128, 8, 512], BF16, f_scope)
        krelu = sb("krelu", [128, 512], BF16, f_scope)
        rt = sb("rt", [128, 1024], F32, f_scope)
        tmp8 = sb("tmp8", [128, 512], BF16, f_scope)
        sx3 = sb("sx3", [128, 16], F32, f_scope)
        sxx3 = sb("sxx3", [128, 16], F32, f_scope)
        m3 = sb("m3", [128, 16], F32, f_scope)
        rstd3 = sb("rstd3", [128, 16], F32, f_scope)
        outst = sb("outst", [128, 2, 2, 1024], F32, f_scope)

        fp_ps = ExitStack()
        f_scope.enter_context(fp_ps)
        ps_tpF = [ps(f"ps_tpF{i}", [128, 1024], BF16, fp_ps) for i in range(2)]
        E = em()
        tok_wf = E.st("g", [
            lambda g: g.dma_start(wr_sb[:, :, :],
                                  wr.rearrange("(c p) o -> p c o", p=128)),
            lambda g: g.dma_start(wkf_sb[:, :, :],
                                  wkf.rearrange("(c p) o -> p c o", p=128)),
            lambda g: g.dma_start(wvf_sb[:, :, :],
                                  wvf.rearrange("(c p) o -> p c o", p=128)),
        ])
        trs0 = ln_stats(E, lambda j: x2r0[0:1, :], 1, [[war["x2r0"]]], cb=8, pp=1)
        th20 = E.st("v", lambda v: v.tensor_scalar(
            h2b[0:1, :], x2r0[0:1, :], stat[0:1, MM + 8:MM + 9],
            stat[0:1, RS + 8:RS + 9], ALU.subtract, ALU.mult), deps=[trs0])
        tt0 = E.st("t", [
            (lambda t, _c=c: t.matmul(
                ps_tpF[0][:, 2 * _c:2 * _c + 1], h2b[0:1, 128 * _c:128 * (_c + 1)],
                identb[0:1, 0:1], start=True, stop=True,
                is_transpose=True))
            for c in range(8)
        ], deps=[th20])
        tev0 = E.st("v", lambda v: v.tensor_copy(
            h2T[:, :, 0:1],
            ps_tpF[0][:, 0:16].rearrange("p (c e) -> p c e", e=2)[:, :, 0:1]),
            deps=[tt0])
        war[("ps_tpF", 0)] = tev0
        for tt_i in range(16):
            b2, sub = tt_i // 2, tt_i % 2
            sl = b2 % 2
            if sub == 0:
                tqd = E.st("y", lambda y, _b2=b2, _sl=sl: y.dma_start(
                    x2t[:, _sl, :, :],
                    x2_d[256 * _b2:256 * (_b2 + 1), :].rearrange(
                        "(a p) o -> p a o", p=128)),
                    deps=tok_x2_done + [war.get(("x2t", sl))])
            trs = ln_stats(E, lambda j, _sl=sl, _su=sub: x2t[:, _sl, _su, :],
                           1, [[tqd]], cb=tt_i % 4)
            tn = E.st("v", lambda v, _sl=sl, _su=sub, _tt=tt_i: v.tensor_scalar(
                h2b[:, :], x2t[:, _sl, _su, :], mcol(_tt % 4), rcol(_tt % 4),
                ALU.subtract, ALU.mult), deps=[trs])
            war[("x2t", sl)] = tn
            bk = tt_i % 2
            ttr = E.st("t", [
                (lambda t, _c=c, _bk=bk: t.matmul(
                    ps_tpF[_bk][:, 128 * _c:128 * (_c + 1)],
                    h2b[:, 128 * _c:128 * (_c + 1)], identb[:, :],
                    start=True, stop=True, is_transpose=True))
                for c in range(8)
            ], deps=[tn, war.get(("ps_tpF", bk))])
            tev = E.st("v", lambda v, _tt=tt_i, _bk=bk: v.tensor_copy(
                h2T[:, :, 1 + 128 * _tt:1 + 128 * (_tt + 1)],
                ps_tpF[_bk][:, :].rearrange("p (c q) -> p c q", q=128)),
                deps=[ttr])
            war[("ps_tpF", bk)] = tev
            war["h2T"] = tev
        E.run(nc)
        fp_ps.close()
        set_gate()

        fa_ps = ExitStack()
        f_scope.enter_context(fa_ps)
        ps_r = ps("ps_r", [128, 1024], F32, fa_ps)
        ps_k = [ps(f"ps_kF{i}", [128, 512], F32, fa_ps) for i in range(2)]
        ps_vo = [ps(f"ps_voF{i}", [128, 1024], F32, fa_ps) for i in range(2)]
        E = em()

        def _ffn_build(_ft):
            _c0 = QT * _ft
            _sl = _ft % 2
            toks = []
            for c in range(8):
                for mu_t, omu_t, dstT in ((mur, omur, xrT), (muk, omuk, xkT)):
                    t1 = E.st("v", lambda v, _c=c, _omu=omu_t, _cc=_c0:
                              v.tensor_scalar(
                                  tmp8[:, :], h2T[:, _c, _cc:_cc + QT],
                                  _omu[:, _c:_c + 1], 0.0, ALU.mult, ALU.add),
                              deps=[war["h2T"], war.get(("xT_rd", _sl))])
                    t2 = E.st("v", lambda v, _c=c, _mu=mu_t, _dst=dstT,
                              _cc=_c0, _s=_sl:
                              v.scalar_tensor_tensor(
                                  _dst[:, _s, _c, :],
                                  h2T[:, _c, _cc + 1:_cc + 1 + QT],
                                  _mu[:, _c:_c + 1], tmp8[:, :], ALU.mult,
                                  ALU.add), deps=[t1])
                    toks.append(t2)
            return toks

        tok_builds = {0: _ffn_build(0)}
        for ft in range(4):
            sl = ft % 2
            tok_x = tok_builds[ft]
            tok_k = []
            for o in range(8):
                bk = o % 2
                tmm = E.st("t", [
                    (lambda t, _o=o, _c=c, _bk=bk, _s=sl: t.matmul(
                        ps_k[_bk][:, :], wkf_sb[:, _c, 128 * _o:128 * (_o + 1)],
                        xkT[:, _s, _c, :], start=(_c == 0), stop=(_c == 7)))
                    for c in range(8)
                ], deps=tok_x + [war.get(("ps_kF", bk)), tok_wf])
                trl = E.st("v", lambda v, _bk=bk: v.tensor_scalar_max(
                    krelu[:, :], ps_k[_bk][:, :], 0.0), deps=[tmm])
                war[("ps_kF", bk)] = trl
                tk2 = E.st("v", lambda v, _o=o: v.tensor_tensor(
                    kactT[:, _o, :], krelu[:, :], krelu[:, :], ALU.mult),
                    deps=[trl])
                tok_k.append(tk2)
            if ft + 1 < 4:
                tok_builds[ft + 1] = _ffn_build(ft + 1)
            for j in range(4):
                slot = 4 * ft + j
                fns = []
                for half in range(2):
                    for c in range(8):
                        fns.append(lambda t, _j=j, _c=c, _h=half, _s=sl:
                                   t.matmul(
                            ps_r[:, 512 * _h:512 * (_h + 1)],
                            xrT[:, _s, _c, 128 * _j:128 * (_j + 1)],
                            wr_sb[:, _c, 512 * _h:512 * (_h + 1)],
                            start=(_c == 0), stop=(_c == 7)))
                tmr = E.st("t", fns, deps=tok_x + [war.get("ps_r"), tok_wf])
                tth = E.st("s", lambda s: s.activation(
                    rt[:, :], ps_r[:, :], AF.Tanh, scale=0.5),
                    deps=[tmr, war.get("rt")])
                war["ps_r"] = tth
                tsg = E.st("v", lambda v: v.tensor_scalar(
                    rt[:, :], rt[:, :], 0.5, 0.5, ALU.mult, ALU.add),
                    deps=[tth])
                bk = j % 2
                fns = []
                for half in range(2):
                    for c in range(8):
                        fns.append(lambda t, _j=j, _c=c, _h=half, _bk=bk:
                                   t.matmul(
                                       ps_vo[_bk][:, 512 * _h:512 * (_h + 1)],
                                       kactT[:, _c, 128 * _j:128 * (_j + 1)],
                                       wvf_sb[:, _c, 512 * _h:512 * (_h + 1)],
                                       start=(_c == 0), stop=(_c == 7)))
                tmv = E.st("t", fns, deps=tok_k + [war.get(("ps_voF", bk)),
                                                   tok_wf])
                war[("xT_rd", sl)] = tmv
                tcf = E.st("v", lambda v, _slot=slot, _bk=bk:
                           v.scalar_tensor_tensor(
                               ffnout[:, _slot, :], ps_vo[_bk][:, :], 1.0,
                               rt[:, :], ALU.mult, ALU.mult,
                               accum_out=sx3[:, _slot:_slot + 1]),
                           deps=[tmv, tsg])
                war[("ps_voF", bk)] = tcf
                war["rt"] = tcf
                tcq = E.st("v", lambda v, _slot=slot: v.scalar_tensor_tensor(
                    junk[:, :], ffnout[:, _slot, :], 1.0, ffnout[:, _slot, :],
                    ALU.mult, ALU.mult,
                    accum_out=sxx3[:, _slot:_slot + 1]), deps=[tcf])
                war[("ffn", slot)] = tcq
        tok_ffn_done = [war[("ffn", s_)] for s_ in range(16)]
        E.run(nc)
        fa_ps.close()
        set_gate()

        E = em()
        tfin = E.st("v", [
            lambda v: v.tensor_scalar(m3[:, :], sx3[:, :], 1.0 / 1024.0, 0.0,
                                      ALU.mult, ALU.add),
            lambda v: v.tensor_tensor(stat[:, M2:M2 + 16], m3[:, :],
                                      m3[:, :], ALU.mult),
            lambda v: v.tensor_scalar(rstd3[:, :], sxx3[:, :], 1.0 / 1024.0,
                                      0.0, ALU.mult, ALU.add),
            lambda v: v.tensor_tensor(rstd3[:, :], rstd3[:, :],
                                      stat[:, M2:M2 + 16], ALU.subtract),
        ], deps=tok_ffn_done + [war.get("lnrs")])
        trs3 = E.st("s", [
            lambda s: s.activation(rstd3[:, :], rstd3[:, :], AF.Ln,
                                   bias=cbias[:, 0:1]),
            lambda s: s.activation(rstd3[:, :], rstd3[:, :], AF.Exp,
                                   scale=-0.5),
        ], deps=[tfin])
        for tt_i in range(16):
            b2, sub = tt_i // 2, tt_i % 2
            sl = b2 % 2
            if sub == 0:
                tqd = E.st("y", lambda y, _b2=b2, _sl=sl: y.dma_start(
                    x2t[:, _sl, :, :],
                    x2_d[256 * _b2:256 * (_b2 + 1), :].rearrange(
                        "(a p) o -> p a o", p=128)),
                    deps=[war.get(("x2tF", sl))])
                tok_fin = []
            tn = E.st("v", [
                lambda v, _tt=tt_i, _sl=sl, _su=sub: v.tensor_scalar(
                    outst[:, _sl, _su, :], ffnout[:, _tt, :],
                    m3[:, _tt:_tt + 1],
                    rstd3[:, _tt:_tt + 1], ALU.subtract, ALU.mult),
                lambda v, _tt=tt_i, _sl=sl, _su=sub: v.tensor_tensor(
                    outst[:, _sl, _su, :], outst[:, _sl, _su, :],
                    x2t[:, _sl, _su, :],
                    ALU.add),
            ], deps=[trs3, tqd, war.get(("outst", sl))])
            war[("x2tF", sl)] = tn
            tok_fin.append(tn)
            if sub == 1:
                tso = E.st("g", lambda g, _b2=b2, _sl=sl: g.dma_start(
                    out_d[256 * _b2:256 * (_b2 + 1), :].rearrange(
                        "(a p) o -> p a o", p=128),
                    outst[:, _sl, :, :]), deps=list(tok_fin))
                war[("outst", sl)] = tso
        E.raw("g", [("sG", counters["sG"])], [lambda g: g.engine_nop()], 0)
        E.run(nc)

    return nc



def _trig(positions):
    import jax
    import jax.numpy as jnp
    cpu = jax.devices("cpu")[0]
    with jax.default_device(cpu):
        pos = jnp.asarray(np.asarray(positions, np.float32))
        freqs = jnp.exp(jnp.arange(0, HD, 2, dtype=jnp.float32)
                        * (-math.log(10000.0) / HD))
        ang = pos[:, None] * freqs[None, :]
        c = np.asarray(jnp.cos(ang)).T.astype(np.float32)
        s = np.asarray(jnp.sin(ang)).T.astype(np.float32)
    t = np.concatenate([c, s], axis=0)
    return np.vstack([t, t]).astype(np.float32)


def _dup_rows(W):
    Wd = np.empty_like(W)
    for h in range(H):
        blk = W[h * HD:(h + 1) * HD]
        Wd[h * HD:h * HD + HD // 2] = blk[0::2]
        Wd[h * HD + HD // 2:(h + 1) * HD] = blk[0::2]
    return np.ascontiguousarray(Wd.T)


def make_in_maps(L0_attn_x, L0_attn_Wq, L0_attn_Wk, L0_attn_Wv, L0_attn_Wo,
                 L0_ffn_W_r, L0_ffn_W_k, L0_ffn_W_v, L0_ffn_mu_r, L0_ffn_mu_k):
    f32 = np.float32
    x = np.ascontiguousarray(np.asarray(L0_attn_x, f32))
    wq_h = _dup_rows(np.asarray(L0_attn_Wq, f32))
    wk_h = _dup_rows(np.asarray(L0_attn_Wk, f32))
    wv_h = np.ascontiguousarray(np.asarray(L0_attn_Wv, f32).T)
    wo_h = np.ascontiguousarray(np.asarray(L0_attn_Wo, f32).T)
    wr_h = np.ascontiguousarray(np.asarray(L0_ffn_W_r, f32).T)
    wkf_h = np.ascontiguousarray(np.asarray(L0_ffn_W_k, f32).T)
    wvf_h = np.ascontiguousarray(np.asarray(L0_ffn_W_v, f32).T)
    mur_h = np.ascontiguousarray(np.asarray(L0_ffn_mu_r, f32).reshape(8, 128).T)
    muk_h = np.ascontiguousarray(np.asarray(L0_ffn_mu_k, f32).reshape(8, 128).T)
    trigk_h = _trig(np.arange(SKV))
    identb_h = np.eye(128, dtype=bfdt)
    identf_h = np.eye(128, dtype=np.float32)
    u0m_h = np.zeros((128, 2), np.float32)
    u0m_h[0:64, 0] = 1
    u0m_h[64:128, 1] = 1
    zrow = np.zeros((1, D), f32)
    trigq0 = np.ascontiguousarray(np.concatenate(
        [np.zeros((128, 1), np.float32), _trig(np.arange(OWN))], axis=1))
    trigq1 = np.ascontiguousarray(_trig(np.arange(OWN - 1, SKV)))

    in_maps = []
    for core in range(8):
        b, s = core // 2, core % 2
        if s == 0:
            xq_h = np.ascontiguousarray(np.concatenate([zrow, x[b, 0:OWN]], 0))
            trigq_h, bm = trigq0, np.zeros((1, 1), f32)
        else:
            xq_h = np.ascontiguousarray(x[b, OWN - 1:SKV])
            trigq_h, bm = trigq1, np.ones((1, 1), f32)
        in_maps.append({
            "xkv": np.ascontiguousarray(x[b]), "xq": xq_h,
            "wq": wq_h, "wk": wk_h, "wv": wv_h, "wo": wo_h,
            "wr": wr_h, "wkf": wkf_h, "wvf": wvf_h,
            "mur": mur_h, "muk": muk_h,
            "trigk": trigk_h, "trigq": trigq_h,
            "identb": identb_h, "identf": identf_h,
            "u0m": u0m_h, "bmask": bm,
        })
    return in_maps


def kernel(L0_attn_x, L0_attn_Wq, L0_attn_Wk, L0_attn_Wv, L0_attn_Wo,
           L0_ffn_W_r, L0_ffn_W_k, L0_ffn_W_v, L0_ffn_mu_r, L0_ffn_mu_k):
    in_maps = make_in_maps(L0_attn_x, L0_attn_Wq, L0_attn_Wk, L0_attn_Wv,
                           L0_attn_Wo, L0_ffn_W_r, L0_ffn_W_k, L0_ffn_W_v,
                           L0_ffn_mu_r, L0_ffn_mu_k)
    nc = build_program()
    res = run_bass_kernel_spmd(nc, in_maps, core_ids=list(range(8)))
    out = np.empty((B, SKV, D), np.asarray(res.results[0]["out"]).dtype)
    for core in range(8):
        b, s = core // 2, core % 2
        out[b, OWN * s:OWN * (s + 1)] = res.results[core]["out"]
    return out


if __name__ == "__main__":
    from reference import setup_inputs
    ins = setup_inputs(0)
    o = kernel(**{k: np.asarray(v) for k, v in ins.items()})
    print(o.shape, o.dtype, float(np.abs(o).mean()))


# version 18
# speedup vs baseline: 1.0576x; 1.0017x over previous
import math
from contextlib import ExitStack

import numpy as np
import ml_dtypes

import concourse.bass as bass
import concourse.mybir as mybir
from concourse.bass_utils import run_bass_kernel_spmd

F32 = mybir.dt.float32
BF16 = mybir.dt.bfloat16
AF = mybir.ActivationFunctionType
ALU = mybir.AluOpType
AX = mybir.AxisListType
bfdt = ml_dtypes.bfloat16

B, SKV, D, H, HD = 4, 4096, 1024, 16, 64
OWN = 2048
W1 = OWN + 1
QT = 512
NQT = OWN // QT
KC = SKV // 128
NG = SKV // 512
EPS = 1e-5

ENG_SEM = {"t": "sT", "v": "sV", "s": "sS", "y": "sY", "g": "sG"}
ENG_ATTR = {"t": "tensor", "v": "vector", "s": "scalar", "y": "sync", "g": "gpsimd"}
DMA_ENG = ("y", "g")


class Em:

    def __init__(self, sems, counters, lastw, gate=()):
        self.sems = sems
        self.c = counters
        self.lastw = lastw
        self.gate = list(gate)
        self.ops = {e: [] for e in ENG_SEM}

    def st(self, eng, fns, deps=()):
        if not isinstance(fns, (list, tuple)):
            fns = [fns]
        sem = ENG_SEM[eng]
        waits = []
        for d in list(deps) + self.gate:
            if d is None:
                continue
            dsem, dval = d
            if dsem == sem and eng not in DMA_ENG:
                continue
            key = (eng, dsem)
            if self.lastw.get(key, -1) >= dval:
                continue
            self.lastw[key] = dval
            waits.append((dsem, dval))
        pre = self.c[sem]
        if eng in DMA_ENG:
            if pre > 0 and self.lastw.get((eng, sem), -1) < pre:
                self.lastw[(eng, sem)] = pre
                waits.append((sem, pre))
            self.c[sem] += 16 * len(fns)
        else:
            self.c[sem] += len(fns)
        self.ops[eng].append((waits, list(fns), eng in DMA_ENG, pre))
        return (sem, self.c[sem])

    def raw(self, eng, waits, fns, inc):
        sem = ENG_SEM[eng]
        self.c[sem] += inc
        self.ops[eng].append((waits, list(fns), False, ("raw", inc)))
        return (sem, self.c[sem])

    def play(self, eng, engobj):
        sem = self.sems[ENG_SEM[eng]]
        own = ENG_SEM[eng]
        serial = eng in ("v", "s")
        for waits, fns, is_dma, pre in self.ops[eng]:
            for (s, v) in waits:
                engobj.wait_ge(self.sems[s], v)
            if is_dma:
                for f in fns:
                    f(engobj).then_inc(sem, 16)
            elif isinstance(pre, tuple):
                insts = [f(engobj) for f in fns]
                if insts and pre[1]:
                    insts[-1].then_inc(sem, pre[1])
            else:
                for i, f in enumerate(fns):
                    if serial and pre + i > 0:
                        engobj.wait_ge(sem, pre + i)
                    f(engobj).then_inc(sem, 1)

    def run(self, nc):
        with nc.Block() as blk:
            for eng in ("y", "g", "t", "v", "s"):
                if self.ops[eng]:
                    getattr(blk, ENG_ATTR[eng])(
                        lambda e, _eng=eng: self.play(_eng, e))


def build_program(dbg=False):
    nc = bass.Bass("TRN2", target_bir_lowering=False)

    def din(name, shape, dt=F32):
        return nc.dram_tensor(name, shape, dt, kind="ExternalInput").ap()

    xkv = din("xkv", [SKV, D])
    xq = din("xq", [W1, D])
    wq = din("wq", [D, D])
    wk = din("wk", [D, D])
    wv = din("wv", [D, D])
    wo = din("wo", [D, D])
    wr = din("wr", [D, D])
    wkf = din("wkf", [D, D])
    wvf = din("wvf", [D, D])
    mur_i = din("mur", [128, 8])
    muk_i = din("muk", [128, 8])
    trigk_i = din("trigk", [128, SKV], F32)
    trigq_i = din("trigq", [128, W1], F32)
    identb_i = din("identb", [128, 128], BF16)
    identf_i = din("identf", [128, 128], F32)
    u0m_i = din("u0m", [128, 2])
    bmask_i = din("bmask", [1, 1])

    out_d = nc.dram_tensor("out", [OWN, D], F32, kind="ExternalOutput").ap()
    skind = "ExternalOutput" if dbg else "Internal"
    x2_d = nc.dram_tensor("x2d", [OWN, D], F32, kind=skind).ap()
    ukhi_d = nc.dram_tensor("ukhi", [128, 4, SKV], BF16, kind="Internal").ap()
    vhi_d = nc.dram_tensor("vhi", [SKV, 512], BF16, kind="Internal").ap()
    if dbg:
        dctx_d = nc.dram_tensor("dctx", [128, 8, OWN], BF16,
                                kind="ExternalOutput").ap()
        dx2r0_d = nc.dram_tensor("dx2r0", [1, D], F32, kind="ExternalOutput").ap()
        dU0c_d = nc.dram_tensor("dU0c", [2, 128, 4, 8], BF16,
                                kind="ExternalOutput").ap()
        dP0_d = nc.dram_tensor("dP0", [2, 8, SKV], BF16,
                               kind="ExternalOutput").ap()
        dc0_d = nc.dram_tensor("dc0", [2, 1, 512], BF16,
                               kind="ExternalOutput").ap()
        dct_d = nc.dram_tensor("dct", [128, 8], BF16,
                               kind="ExternalOutput").ap()
        dat0_d = nc.dram_tensor("dat0", [1, D], F32,
                                kind="ExternalOutput").ap()
        dcolb_d = nc.dram_tensor("dcolb", [128, 8], F32,
                                 kind="ExternalOutput").ap()

    sems = {}
    counters = {s: 0 for s in ENG_SEM.values()}
    lastw = {}
    war = {}

    gate_hold = [[]]

    def em():
        return Em(sems, counters, lastw, gate_hold[0])

    def set_gate():
        gate_hold[0] = [(s, counters[s]) for s in counters if counters[s] > 0]

    with ExitStack() as top:
        for s in ("sT", "sV", "sS", "sY", "sG"):
            sems[s] = top.enter_context(nc.semaphore(s))

        def sb(name, shape, dt, ctx):
            return ctx.enter_context(nc.sbuf_tensor("s_" + name, shape, dt))

        def ps(name, shape, dt, ctx):
            return ctx.enter_context(nc.psum_tensor(name, shape, dt))

        identb = sb("identb", [128, 128], BF16, top)
        identf = sb("identf", [128, 128], F32, top)
        u0m = sb("u0m", [128, 2], F32, top)
        mur = sb("mur", [128, 8], F32, top)
        muk = sb("muk", [128, 8], F32, top)
        omur = sb("omur", [128, 8], F32, top)
        omuk = sb("omuk", [128, 8], F32, top)
        bmask = sb("bmask", [1, 1], F32, top)
        junk = sb("junk", [128, 1024], BF16, top)
        stat = sb("stat", [128, 80], F32, top)
        x2r0 = sb("x2r0", [1, D], F32, top)
        xq0 = sb("xq0", [1, D], F32, top)
        cbias = sb("cbias", [128, 2], F32, top)

        SX, SXX, MM, RS, M2 = 0, 16, 32, 48, 64

        def mcol(j, cb=0):
            return stat[:, MM + cb + j:MM + cb + j + 1]

        def rcol(j, cb=0):
            return stat[:, RS + cb + j:RS + cb + j + 1]

        def ln_stats(E, srcs, nsub, deps_each, n=1024.0, cb=0, pp=128):
            toks = []
            for j in range(nsub):
                t1 = E.st("v", lambda v, _j=j: v.tensor_scalar(
                    junk[0:pp, :], srcs(_j), 1.0, 0.0, ALU.mult, ALU.add,
                    accum_out=stat[0:pp, SX + cb + _j:SX + cb + _j + 1]),
                    deps=deps_each[j])
                t2 = E.st("v", lambda v, _j=j: v.scalar_tensor_tensor(
                    junk[0:pp, :], srcs(_j), 1.0, srcs(_j), ALU.mult, ALU.mult,
                    accum_out=stat[0:pp, SXX + cb + _j:SXX + cb + _j + 1]),
                    deps=deps_each[j])
                toks += [t1, t2]
            c0, c1 = cb, cb + nsub
            tm = E.st("v", [
                lambda v: v.tensor_scalar(stat[:, MM + c0:MM + c1],
                                          stat[:, SX + c0:SX + c1], 1.0 / n, 0.0,
                                          ALU.mult, ALU.add),
                lambda v: v.tensor_tensor(stat[:, M2 + c0:M2 + c1],
                                          stat[:, MM + c0:MM + c1],
                                          stat[:, MM + c0:MM + c1], ALU.mult),
                lambda v: v.tensor_scalar(stat[:, RS + c0:RS + c1],
                                          stat[:, SXX + c0:SXX + c1], 1.0 / n, 0.0,
                                          ALU.mult, ALU.add),
                lambda v: v.tensor_tensor(stat[:, RS + c0:RS + c1],
                                          stat[:, RS + c0:RS + c1],
                                          stat[:, M2 + c0:M2 + c1], ALU.subtract),
            ], deps=toks + [war.get("lnrs")])
            trs = E.st("s", [
                lambda s: s.activation(stat[:, RS + c0:RS + c1],
                                       stat[:, RS + c0:RS + c1], AF.Ln,
                                       bias=cbias[:, 0:1]),
                lambda s: s.activation(stat[:, RS + c0:RS + c1],
                                       stat[:, RS + c0:RS + c1], AF.Exp,
                                       scale=-0.5),
            ], deps=[tm])
            war["lnrs"] = trs
            return trs

        ap_scope = ExitStack()
        top.enter_context(ap_scope)
        uk_sb = sb("uk_sb", [128, 4, SKV], BF16, ap_scope)
        vbuf = sb("vbuf", [128, KC, 8, 65], BF16, ap_scope)
        a_scope = ExitStack()
        ap_scope.enter_context(a_scope)
        wk_sb = sb("wk_sb", [128, 8, 1024], BF16, a_scope)
        wv_sb = sb("wv_sb", [128, 8, 1024], BF16, a_scope)
        trigk = sb("trigk", [128, SKV], BF16, a_scope)
        xg = sb("xg", [128, 2, 4, 1024], F32, a_scope)
        h1g = sb("h1g", [128, 2, 4, 1024], BF16, a_scope)
        h1gT = sb("h1gT", [128, 8, 512], BF16, a_scope)
        ukst = sb("ukst", [128, 2, 512], BF16, a_scope)
        vst = sb("vst", [128, 2, 512], BF16, a_scope)

        E = em()
        tok_const = E.st("y", [
            lambda y: y.dma_start(identb[:, :], identb_i),
            lambda y: y.dma_start(identf[:, :], identf_i),
            lambda y: y.dma_start(u0m[:, :], u0m_i),
            lambda y: y.dma_start(mur[:, :], mur_i),
            lambda y: y.dma_start(muk[:, :], muk_i),
            lambda y: y.dma_start(bmask[:, :], bmask_i),
            lambda y: y.dma_start(xq0[:, :], xq[0:1, :]),
        ])
        E.st("s", [
            lambda s: s.activation(omur[:, :], mur[:, :], AF.Identity,
                                   bias=1.0, scale=-1.0),
            lambda s: s.activation(omuk[:, :], muk[:, :], AF.Identity,
                                   bias=1.0, scale=-1.0),
        ], deps=[tok_const])
        _e1 = float(np.float32(EPS))
        _e2 = float(np.float32(EPS - _e1))
        tok_kn0 = E.st("v", [
            lambda v: v.memset(cbias[:, 0:1], _e1),
            lambda v: v.tensor_scalar(cbias[:, 0:1], cbias[:, 0:1], 1.0, _e2,
                                      ALU.mult, ALU.add),
            lambda v: v.memset(cbias[:, 1:2], 1e-30),
            lambda v: v.memset(stat[:, :], 0.0),
        ], deps=[])
        tok_vones = E.st("v", lambda v: v.memset(vbuf[:, :, :, 64:65], 1.0),
                         deps=[])
        tok_wkv = E.st("g", [
            lambda g: g.dma_start(
                wk_sb[:, :, :], wk.rearrange("(c p) o -> p c o", p=128)),
            lambda g: g.dma_start(
                wv_sb[:, :, :], wv.rearrange("(c p) o -> p c o", p=128)),
        ])
        tok_trigk = E.st("g", lambda g: g.dma_start(trigk[:, :], trigk_i))
        E.run(nc)

        E = em()
        a_ps = ExitStack()
        a_scope.enter_context(a_ps)
        ps_tp = [ps(f"ps_tpA{i}", [128, 1024], BF16, a_ps) for i in range(2)]
        ps_ko = [ps(f"ps_koA{i}", [128, 512], F32, a_ps) for i in range(2)]
        ps_v = ps("ps_vA", [128, 1024], F32, a_ps)

        def emit_xg_dma(g, dep):
            sl = g % 2
            return E.st("y", lambda y, _g=g, _sl=sl: y.dma_start(
                xg[:, _sl, :, :],
                xkv[512 * _g:512 * (_g + 1), :].rearrange("(a p) o -> p a o", p=128)),
                deps=[dep])

        tok_dma = {0: emit_xg_dma(0, tok_wkv), 1: emit_xg_dma(1, None)}

        def a_accum(g, idx):
            sl, cb = g % 2, (g % 2) * 4
            j, kind = idx // 2, idx % 2
            if kind == 0:
                return E.st("v", lambda v, _j=j, _sl=sl, _cb=cb: v.tensor_scalar(
                    junk[:, :], xg[:, _sl, _j, :], 1.0, 0.0, ALU.mult, ALU.add,
                    accum_out=stat[:, SX + _cb + _j:SX + _cb + _j + 1]),
                    deps=[tok_dma[g]])
            return E.st("v", lambda v, _j=j, _sl=sl, _cb=cb:
                        v.scalar_tensor_tensor(
                junk[:, :], xg[:, _sl, _j, :], 1.0, xg[:, _sl, _j, :],
                ALU.mult, ALU.mult,
                accum_out=stat[:, SXX + _cb + _j:SXX + _cb + _j + 1]),
                deps=[tok_dma[g]])

        def a_finish(g, toks):
            cb = (g % 2) * 4
            c0, c1 = cb, cb + 4
            tm = E.st("v", [
                lambda v: v.tensor_scalar(stat[:, MM + c0:MM + c1],
                                          stat[:, SX + c0:SX + c1],
                                          1.0 / 1024.0, 0.0, ALU.mult, ALU.add),
                lambda v: v.tensor_tensor(stat[:, M2 + c0:M2 + c1],
                                          stat[:, MM + c0:MM + c1],
                                          stat[:, MM + c0:MM + c1], ALU.mult),
                lambda v: v.tensor_scalar(stat[:, RS + c0:RS + c1],
                                          stat[:, SXX + c0:SXX + c1],
                                          1.0 / 1024.0, 0.0, ALU.mult, ALU.add),
                lambda v: v.tensor_tensor(stat[:, RS + c0:RS + c1],
                                          stat[:, RS + c0:RS + c1],
                                          stat[:, M2 + c0:M2 + c1],
                                          ALU.subtract),
            ], deps=toks + [war.get("lnrs")])
            trs = E.st("s", [
                lambda s: s.activation(stat[:, RS + c0:RS + c1],
                                       stat[:, RS + c0:RS + c1], AF.Ln,
                                       bias=cbias[:, 0:1]),
                lambda s: s.activation(stat[:, RS + c0:RS + c1],
                                       stat[:, RS + c0:RS + c1], AF.Exp,
                                       scale=-0.5),
            ], deps=[tm])
            war["lnrs"] = trs
            return trs

        def a_norm(g, j, extra):
            sl, cb = g % 2, (g % 2) * 4
            return E.st("v", lambda v, _j=j, _sl=sl, _cb=cb: v.tensor_scalar(
                h1g[:, _sl, _j, :], xg[:, _sl, _j, :], mcol(_j, _cb),
                rcol(_j, _cb), ALU.subtract, ALU.mult), deps=extra)

        toks0 = [a_accum(0, i) for i in range(8)]
        trs0a = a_finish(0, toks0)
        tok_norm = {0: [a_norm(0, j, [trs0a]) for j in range(4)]}
        for g in range(NG):
            sl = g % 2
            if g + 2 < NG:
                tok_dma[g + 2] = emit_xg_dma(g + 2, tok_norm[g][-1])
            tok_ev = []
            for j in range(4):
                bk = j % 2
                tt = E.st("t", [
                    (lambda t, _j=j, _c=c, _bk=bk, _sl=sl: t.matmul(
                        ps_tp[_bk][:, 128 * _c:128 * (_c + 1)],
                        h1g[:, _sl, _j, 128 * _c:128 * (_c + 1)], identb[:, :],
                        start=True, stop=True, is_transpose=True))
                    for c in range(8)
                ], deps=[tok_norm[g][j], war.get(("ps_tpA", bk)), tok_const])
                tev = E.st("v", lambda v, _j=j, _bk=bk: v.tensor_copy(
                    h1gT[:, :, 128 * _j:128 * (_j + 1)],
                    ps_tp[_bk][:, :].rearrange("p (c q) -> p c q", q=128)),
                    deps=[tt])
                war[("ps_tpA", bk)] = tev
                tok_ev.append(tev)
            tok_nxt = []
            for o in range(8):
                bk = o % 2
                tmm = E.st("t", [
                    (lambda t, _o=o, _c=c, _bk=bk: t.matmul(
                        ps_ko[_bk][:, :], wk_sb[:, _c, 128 * _o:128 * (_o + 1)],
                        h1gT[:, _c, :], start=(_c == 0), stop=(_c == 7)))
                    for c in range(8)
                ], deps=tok_ev + [war.get(("ps_koA", bk)), tok_trigk])
                if o < 4:
                    dst = uk_sb[:, o, 512 * g:512 * (g + 1)]
                else:
                    dst = ukst[:, o % 2, :]
                tev = E.st("v", lambda v, _bk=bk, _g=g, _dst=dst: v.tensor_tensor(
                    _dst, ps_ko[_bk][:, :], trigk[:, 512 * _g:512 * (_g + 1)],
                    ALU.mult),
                    deps=[tmm, war.get(("ukst", o % 2)) if o >= 4 else None])
                war[("ps_koA", bk)] = tev
                if o >= 4:
                    tst = E.st("y", lambda y, _o=o, _g=g: y.dma_start(
                        ukhi_d[:, _o - 4, 512 * _g:512 * (_g + 1)],
                        ukst[:, _o % 2, :]), deps=[tev])
                    war[("ukst", o % 2)] = tst
                if g + 1 < NG:
                    tok_nxt.append(a_accum(g + 1, o))
            if g + 1 < NG:
                trs_n = a_finish(g + 1, tok_nxt)
            for j in range(4):
                kc = 4 * g + j
                fns = []
                for half in range(2):
                    for c in range(8):
                        fns.append(lambda t, _j=j, _c=c, _h=half: t.matmul(
                            ps_v[:, 512 * _h:512 * (_h + 1)],
                            h1gT[:, _c, 128 * _j:128 * (_j + 1)],
                            wv_sb[:, _c, 512 * _h:512 * (_h + 1)],
                            start=(_c == 0), stop=(_c == 7)))
                tvm = E.st("t", fns, deps=tok_ev + [war.get("ps_vA")])
                tlo = E.st("v", lambda v, _kc=kc: v.tensor_copy(
                    vbuf[:, _kc, :, 0:64],
                    ps_v[:, 0:512].rearrange("p (h d) -> p h d", d=64)),
                    deps=[tvm, tok_vones])
                thi = E.st("v", lambda v, _kc=kc: v.tensor_copy(
                    vst[:, _kc % 2, :], ps_v[:, 512:1024]),
                    deps=[tvm, war.get(("vst", kc % 2))])
                war["ps_vA"] = thi
                tvs = E.st("y", lambda y, _kc=kc: y.dma_start(
                    vhi_d[128 * _kc:128 * (_kc + 1), :], vst[:, _kc % 2, :]),
                    deps=[thi])
                war[("vst", kc % 2)] = tvs
                if g + 1 < NG:
                    tok_norm.setdefault(g + 1, []).append(
                        a_norm(g + 1, j, [trs_n, tok_ev[-1]]))
        tok_A_ukst = (war[("ukst", 0)], war[("ukst", 1)])
        tok_A_vst = (war[("vst", 0)], war[("vst", 1)])
        E.run(nc)
        a_ps.close()
        a_scope.close()
        set_gate()

        p_scope = ExitStack()
        ap_scope.enter_context(p_scope)
        wo_sb = sb("wo_sb", [128, 8, 1024], BF16, p_scope)
        ctxF = sb("ctxF", [128, 8, OWN], BF16, p_scope)
        ctx0T = sb("ctx0T", [128, 16], BF16, p_scope)
        pa_scope = ExitStack()
        p_scope.enter_context(pa_scope)
        wq_sb = sb("wq_sb", [128, 8, 1024], BF16, pa_scope)
        trigq = sb("trigq", [128, W1], BF16, pa_scope)
        uqt = sb("uqt", [128, 4, 512], BF16, pa_scope)
        xqg = sb("xqg", [128, 4, 1024], F32, pa_scope)
        h1q = sb("h1q", [128, 4, 1024], BF16, pa_scope)
        h1qT = sb("h1qT", [128, 8, 512], BF16, pa_scope)
        pT = sb("pT", [128, 4, 1024], BF16, pa_scope)
        rl = sb("rl", [128, 1], F32, pa_scope)
        rden = sb("rden", [128, 4], F32, pa_scope)
        ctx_sb = sb("ctx_sb", [128, 4, 64], BF16, pa_scope)
        h1q0 = sb("h1q0", [1, 1024], BF16, pa_scope)
        h1q0T = sb("h1q0T", [128, 8], BF16, pa_scope)
        U0c = sb("U0c", [128, 4, 8], BF16, pa_scope)
        P0 = sb("P0", [8, SKV], BF16, pa_scope)
        P0T = sb("P0T", [128, KC, 8], BF16, pa_scope)
        vtmp = sb("vtmp", [128, 2, 2, 512], BF16, pa_scope)
        c0row = sb("c0row", [1, 512], BF16, pa_scope)
        rc0 = sb("rc0", [8, 1], F32, pa_scope)

        pp_ps = ExitStack()
        pa_scope.enter_context(pp_ps)
        ps_s = ps("ps_s", [128, 2048], F32, pp_ps)
        ps_av = [ps(f"ps_av{i}", [128, 512], F32, pp_ps) for i in range(4)]

        for ppass in range(2):
            E = em()
            if ppass == 0:
                tok_w = E.st("g", lambda g: g.dma_start(
                    wq_sb[:, :, :], wq.rearrange("(c p) o -> p c o", p=128)))
                tok_trq = E.st("g", lambda g: g.dma_start(trigq[:, :], trigq_i))
                tok_ukrdy = tok_w
                tok_vrdy = tok_vones
            else:
                tok_w = E.st("y", [
                    (lambda y, _j=j: y.dma_start(uk_sb[:, _j, :], ukhi_d[:, _j, :]))
                    for j in range(4)
                ], deps=[tok_A_ukst[0], tok_A_ukst[1], war.get("uk_rd")])
                tok_ukrdy = tok_w
                vt = None
                for b2 in range(KC // 2):
                    sl2 = b2 % 2
                    tvd = E.st("y", lambda y, _b=b2, _s=sl2: y.dma_start(
                        vtmp[:, _s, :, :],
                        vhi_d[256 * _b:256 * (_b + 1), :].rearrange(
                            "(a p) o -> p a o", p=128)),
                        deps=[tok_A_vst[0], tok_A_vst[1],
                              war.get(("vtmp", sl2))])
                    vt = E.st("v", [
                        (lambda v, _b=b2, _s=sl2, _i=i: v.tensor_copy(
                            vbuf[:, 2 * _b + _i, :, 0:64],
                            vtmp[:, _s, _i, :].rearrange(
                                "p (h d) -> p h d", d=64)))
                        for i in range(2)
                    ], deps=[tvd, war.get("vbuf_rd")])
                    war[("vtmp", sl2)] = vt
                tok_vrdy = vt
                tok_trq = None

            for qt in range(NQT):
                r0 = 1 + QT * qt
                tq = E.st("y", lambda y, _r0=r0: y.dma_start(
                    xqg[:, :, :],
                    xq[_r0:_r0 + QT, :].rearrange("(a p) o -> p a o", p=128)),
                    deps=[war.get("xqg"), tok_trq])
                trs = ln_stats(E, lambda j: xqg[:, j, :], 4, [[tq]] * 4)
                tok_norm = []
                for j in range(4):
                    tok_norm.append(E.st("v", lambda v, _j=j: v.tensor_scalar(
                        h1q[:, _j, :], xqg[:, _j, :], mcol(_j), rcol(_j),
                        ALU.subtract, ALU.mult), deps=[trs]))
                war["xqg"] = tok_norm[-1]
                tok_ev = []
                for j in range(4):
                    hfj = j % 2
                    tt = E.st("t", [
                        (lambda t, _j=j, _c=c, _hf=hfj: t.matmul(
                            ps_s[:, 1024 * _hf + 128 * _c:
                                 1024 * _hf + 128 * (_c + 1)],
                            h1q[:, _j, 128 * _c:128 * (_c + 1)], identb[:, :],
                            start=True, stop=True))
                        for c in range(8)
                    ], deps=[tok_norm[j], war.get(("ps_s", hfj))])
                    tev = E.st("v", lambda v, _j=j, _hf=hfj: v.tensor_copy(
                        h1qT[:, :, 128 * _j:128 * (_j + 1)],
                        ps_s[:, 1024 * _hf:1024 * (_hf + 1)].rearrange(
                            "p (c q) -> p c q", q=128)),
                        deps=[tt])
                    war[("ps_s", hfj)] = tev
                    tok_ev.append(tev)
                tok_uq = []
                for oo in range(4):
                    o = 4 * ppass + oo
                    tmm = E.st("t", [
                        (lambda t, _o=o, _c=c, _oo=oo: t.matmul(
                            ps_av[_oo][:, :],
                            wq_sb[:, _c, 128 * _o:128 * (_o + 1)],
                            h1qT[:, _c, :], start=(_c == 0), stop=(_c == 7)))
                        for c in range(8)
                    ], deps=tok_ev + [war.get(("ps_av", oo)), tok_w])
                    tev = E.st("v", lambda v, _oo=oo, _r0=r0: v.tensor_tensor(
                        uqt[:, _oo, :], ps_av[_oo][:, :],
                        trigq[:, _r0:_r0 + QT],
                        ALU.mult), deps=[tmm, war.get("uqt")])
                    war[("ps_av", oo)] = tev
                    tok_uq.append(tev)

                for hh in range(8):
                    h = 8 * ppass + hh
                    oo, m = hh // 2, hh % 2
                    tok_exp = {}
                    tok_pv = {}
                    def _emit_pv(_gg):
                        _sl = _gg % 4
                        tok_pv[_gg] = E.st("t", [
                            (lambda t, _jj=jj, _qq=qq, _g=_gg, _hh=hh, _s=_sl:
                             t.matmul(
                                ps_av[_qq][:, 0:65],
                                pT[:, _s, 512 * _jj + 128 * _qq:
                                   512 * _jj + 128 * _qq + 128],
                                vbuf[:, 2 * _g + _jj, _hh, :],
                                start=(2 * _g + _jj == 0),
                                stop=(2 * _g + _jj == KC - 1)))
                            for jj in range(2) for qq in range(4)
                        ], deps=[tok_exp[_gg], tok_vrdy]
                           + ([war.get(("ps_av", q2)) for q2 in range(4)]
                              if _gg == 0 else []))
                        war[("pT", _sl)] = tok_pv[_gg]

                    for gg in range(16):
                        hf = gg % 2
                        sl = gg % 4
                        tqk = E.st("t", [
                            (lambda t, _jj=jj, _gg=gg, _oo=oo, _m=m, _hf=hf:
                             t.matmul(
                                ps_s[:, 1024 * _hf + 512 * _jj:
                                     1024 * _hf + 512 * (_jj + 1)],
                                uk_sb[64 * _m:64 * _m + 64, _oo,
                                      128 * (2 * _gg + _jj):
                                      128 * (2 * _gg + _jj) + 128],
                                uqt[64 * _m:64 * _m + 64, _oo, :],
                                start=True, stop=True,
                                tile_position=(64 * _m, 0)))
                            for jj in range(2)
                        ], deps=[tok_uq[oo], tok_ukrdy,
                                 war.get(("ps_s", hf))])
                        tok_exp[gg] = E.st("s", lambda s, _sl=sl, _h=h, _hf=hf:
                            s.activation(
                                pT[:, _sl, :],
                                ps_s[:, 1024 * _hf:1024 * (_hf + 1)], AF.Exp,
                                scale=0.125),
                            deps=[tqk, war.get(("pT", sl))])
                        war[("ps_s", hf)] = tok_exp[gg]
                        if gg >= 1:
                            _emit_pv(gg - 1)
                    _emit_pv(15)
                    chunk = 4 * ppass + hh // 2
                    tok_dv = []
                    for qq in range(4):
                        tdv = E.st("v", [
                            (lambda v, _qq=qq: v.reciprocal(
                                rden[:, _qq:_qq + 1], ps_av[_qq][:, 64:65])),
                            (lambda v, _qq=qq: v.tensor_scalar(
                                ctx_sb[:, _qq, :], ps_av[_qq][:, 0:64],
                                rden[:, _qq:_qq + 1], 0.0, ALU.mult, ALU.add)),
                        ], deps=[tok_pv[15]])
                        war[("ps_av", qq)] = tdv
                        tok_dv.append(tdv)
                    ttx = E.st("t", [
                        (lambda t, _qq=qq, _m=m: t.matmul(
                            ps_av[_qq][64 * _m:64 * _m + 64, 128:256],
                            ctx_sb[:, _qq, :], identb[:, :],
                            start=True, stop=True))
                        for qq in range(4)
                    ], deps=tok_dv)
                    tcx = E.st("v", [
                        (lambda v, _ck=chunk, _qt=qt, _m=m, _qq=qq:
                         v.tensor_copy(
                             ctxF[64 * _m:64 * _m + 64, _ck,
                                  QT * _qt + 128 * _qq:
                                  QT * _qt + 128 * (_qq + 1)],
                             ps_av[_qq][64 * _m:64 * _m + 64, 128:256]))
                        for qq in range(4)
                    ], deps=[ttx])
                    for qq in range(4):
                        war[("ps_av", qq)] = tcx
                war["uqt"] = tok_pv[15]

            trs0 = ln_stats(E, lambda j: xq0[0:1, :], 1, [[tok_const]], cb=8, pp=1)
            tn0 = E.st("v", lambda v: v.tensor_scalar(
                h1q0[:, :], xq0[0:1, :], stat[0:1, MM + 8:MM + 9],
                stat[0:1, RS + 8:RS + 9], ALU.subtract, ALU.mult), deps=[trs0])
            tt0 = E.st("t", [
                (lambda t, _c=c: t.matmul(
                    ps_av[0][:, _c:_c + 1], h1q0[0:1, 128 * _c:128 * (_c + 1)],
                    identb[0:1, 0:1], start=True, stop=True))
                for c in range(8)
            ], deps=[tn0, war.get(("ps_av", 0))])
            tev0 = E.st("v", lambda v: v.tensor_copy(
                h1q0T[:, 0:8], ps_av[0][:, 0:8]),
                deps=[tt0])
            war[("ps_av", 0)] = tev0
            tu0m = E.st("v", lambda v: v.memset(U0c[:, :, :], 0.0),
                        deps=[war.get("U0c_rd"), war.get("dbg_g")])
            tok_u0 = []
            for oo in range(4):
                o = 4 * ppass + oo
                tmm = E.st("t", [
                    (lambda t, _o=o, _c=c: t.matmul(
                        ps_av[1][:, 0:1], wq_sb[:, _c, 128 * _o:128 * (_o + 1)],
                        h1q0T[:, _c:_c + 1], start=(_c == 0), stop=(_c == 7)))
                    for c in range(8)
                ], deps=[tev0, war.get(("ps_av", 1))])
                tum = E.st("v", lambda v: v.tensor_tensor(
                    rl[:, 0:1], ps_av[1][:, 0:1], trigq[:, 0:1], ALU.mult),
                    deps=[tmm])
                war[("ps_av", 1)] = tum
                for jj in range(2):
                    tok_u0.append(E.st("v", lambda v, _oo=oo, _jj=jj:
                                       v.tensor_scalar(
                                           U0c[:, _oo, 2 * _oo + _jj:2 * _oo + _jj + 1],
                                           rl[:, 0:1], u0m[:, _jj:_jj + 1], 0.0,
                                           ALU.mult, ALU.add),
                                       deps=[tum, tu0m]))
            if dbg:
                war["dbg_g"] = E.st("g", lambda g, _p=ppass: g.dma_start(
                    dU0c_d[_p], U0c[:, :, :]), deps=tok_u0)
            tok_p0 = []
            tsm = None
            for c8 in range(8):
                tsm = E.st("t", [
                    (lambda t, _oo=oo, _c8=c8: t.matmul(
                        ps_av[2][0:8, :], U0c[:, _oo, :],
                        uk_sb[:, _oo, 512 * _c8:512 * (_c8 + 1)],
                        start=(_oo == 0), stop=(_oo == 3)))
                    for oo in range(4)
                ], deps=tok_u0 + [war.get(("ps_av", 2)), tok_ukrdy])
                texp = E.st("s", lambda s, _c8=c8: s.activation(
                    P0[:, 512 * _c8:512 * (_c8 + 1)], ps_av[2][0:8, :], AF.Exp,
                    scale=0.125),
                    deps=[tsm, war.get("P0_rdv"), war.get("P0_rdt"),
                          war.get("dbg_g")])
                war[("ps_av", 2)] = texp
                tok_p0.append(texp)
            tnp0 = E.st("v", [
                lambda v: v.tensor_reduce(rc0[:, :], P0[:, :], AX.X, ALU.add),
                lambda v: v.reciprocal(rc0[:, :], rc0[:, :]),
                lambda v: v.tensor_scalar(P0[:, :], P0[:, :], rc0[:, :], 0.0,
                                          ALU.mult, ALU.add),
            ], deps=tok_p0)
            if dbg:
                war["dbg_g"] = E.st("g", lambda g, _p=ppass: g.dma_start(
                    dP0_d[_p], P0[:, :]), deps=[tnp0])
            war["P0_rdv"] = tnp0
            war["U0c_rd"] = tsm
            war["uk_rd"] = tsm
            ttr0 = E.st("t", [
                (lambda t, _kc=kc: t.matmul(
                    ps_av[3][:, 8 * _kc:8 * (_kc + 1)],
                    P0[:, 128 * _kc:128 * (_kc + 1)], identb[0:8, 0:8],
                    start=True, stop=True))
                for kc in range(KC)
            ], deps=[tnp0, war.get(("ps_av", 3))])
            war["P0_rdt"] = ttr0
            tp0e = E.st("v", lambda v: v.tensor_copy(
                P0T[:, :, :],
                ps_av[3][:, 0:256].rearrange("p (k e) -> p k e", e=8)),
                deps=[ttr0, war.get("P0T_rd")])
            war[("ps_av", 3)] = tp0e
            tpv = E.st("t", [
                (lambda t, _kc=kc, _j=j: t.matmul(
                    ps_av[0][0:1, 64 * _j:64 * _j + 64],
                    P0T[:, _kc, _j:_j + 1], vbuf[:, _kc, _j, 0:64],
                    start=(_kc == 0), stop=(_kc == KC - 1)))
                for j in range(8) for kc in range(KC)
            ], deps=[tp0e, tok_vrdy, war.get(("ps_av", 0))])
            war["P0T_rd"] = tpv
            war["vbuf_rd"] = tpv
            tc0 = E.st("v", lambda v: v.tensor_copy(
                c0row[0:1, :], ps_av[0][0:1, 0:512]),
                deps=[tpv, war.get("dbg_g")])
            war[("ps_av", 0)] = tc0
            if dbg:
                war["dbg_g"] = E.st("g", lambda g, _p=ppass: g.dma_start(
                    dc0_d[_p], c0row[0:1, :]), deps=[tc0])
            ttc = E.st("t", [
                (lambda t, _h=h, _half=half: t.matmul(
                    ps_av[1][64 * _half:64 * _half + 64,
                             8 * _half + _h:8 * _half + _h + 1],
                    c0row[0:1, 64 * _h:64 * _h + 64], identb[0:1, 0:1],
                    start=True, stop=True))
                for half in range(2) for h in range(8)
            ], deps=[tc0, war.get(("ps_av", 1))])
            tce = E.st("v", [
                lambda v, _p=ppass: v.tensor_copy(
                    ctx0T[0:64, 4 * _p:4 * _p + 4].rearrange(
                        "p (c e) -> p c e", e=1),
                    ps_av[1][0:64, 0:8].rearrange(
                        "p (c e) -> p c e", e=2)[:, :, 0:1]),
                lambda v, _p=ppass: v.tensor_copy(
                    ctx0T[64:128, 4 * _p:4 * _p + 4].rearrange(
                        "p (c e) -> p c e", e=1),
                    ps_av[1][64:128, 8:16].rearrange(
                        "p (c e) -> p c e", e=2)[:, :, 1:2]),
            ], deps=[ttc])
            war[("ps_av", 1)] = tce
            war["ctx0T"] = tce
            E.run(nc)

        pp_ps.close()
        pa_scope.close()
        set_gate()

        e_scope = ExitStack()
        p_scope.enter_context(e_scope)
        atF = sb("atF", [128, 8, 512], F32, e_scope)
        at_sb = sb("at_sb", [128, 1024], F32, e_scope)
        xqE = sb("xqE", [128, 4, 1024], F32, e_scope)
        x2st = sb("x2st", [128, 2, 4, 1024], F32, e_scope)
        at0 = sb("at0", [1, 1024], F32, e_scope)
        e_ps = ExitStack()
        e_scope.enter_context(e_ps)
        ps_wo = [ps(f"ps_wo{i}", [128, 512], F32, e_ps) for i in range(2)]
        ps_tm = [ps(f"ps_tm{i}", [128, 1024], F32, e_ps) for i in range(2)]

        E = em()
        tok_wo = E.st("g", lambda g: g.dma_start(
            wo_sb[:, :, :], wo.rearrange("(c p) o -> p c o", p=128)))
        colb = sb("colb", [128, 8], F32, e_scope)
        tcl = []
        for o in range(8):
            bk = o % 2
            fns = []
            for c in range(8):
                fns.append(lambda t, _o=o, _c=c, _bk=bk: t.matmul(
                    ps_wo[_bk][:, 0:1],
                    wo_sb[:, _c, 128 * _o:128 * (_o + 1)],
                    ctx0T[:, _c:_c + 1],
                    start=(_c == 0), stop=(_c == 7)))
            tmo = E.st("t", fns, deps=[war["ctx0T"], tok_wo,
                                       war.get(("ps_wo", bk))])
            tce2 = E.st("v", lambda v, _o=o, _bk=bk: v.tensor_copy(
                colb[:, _o:_o + 1], ps_wo[_bk][:, 0:1]), deps=[tmo])
            war[("ps_wo", bk)] = tce2
            tcl.append(tce2)
        t0tr = E.st("t", [
            (lambda t, _o=o: t.matmul(
                ps_tm[0][0:1, 128 * _o:128 * (_o + 1)], colb[:, _o:_o + 1],
                identf[:, :], start=True, stop=True,
                is_transpose=True))
            for o in range(8)
        ], deps=tcl + [war.get(("ps_tm", 0))])
        t0e = E.st("v", lambda v: v.tensor_copy(at0[0:1, :], ps_tm[0][0:1, :]),
                   deps=[t0tr])
        war[("ps_tm", 0)] = t0e
        trs0 = ln_stats(E, lambda j: at0[0:1, :], 1, [[t0e]], cb=8, pp=1)
        tx2r0 = E.st("v", [
            lambda v: v.tensor_scalar(x2r0[:, :], at0[0:1, :],
                                      stat[0:1, MM + 8:MM + 9],
                                      stat[0:1, RS + 8:RS + 9],
                                      ALU.subtract, ALU.mult),
            lambda v: v.tensor_tensor(x2r0[:, :], x2r0[:, :], xq0[0:1, :],
                                      ALU.add),
            lambda v: v.tensor_scalar(x2r0[:, :], x2r0[:, :], bmask[0:1, 0:1],
                                      0.0, ALU.mult, ALU.add),
        ], deps=[trs0, tok_const])
        war["x2r0"] = tx2r0

        for qt in range(NQT):
            r0 = 1 + QT * qt
            tok_nrs = []
            tq = E.st("y", lambda y, _r0=r0: y.dma_start(
                xqE[:, :, :],
                xq[_r0:_r0 + QT, :].rearrange("(a p) o -> p a o", p=128)),
                deps=[war.get("xqE")])
            tok_wom = []
            for o in range(8):
                bk = o % 2
                fns = []
                for c in range(8):
                    fns.append(lambda t, _c=c, _o=o, _bk=bk, _qt=qt:
                               t.matmul(
                                   ps_wo[_bk][:, :],
                                   wo_sb[:, _c, 128 * _o:128 * (_o + 1)],
                                   ctxF[:, _c, QT * _qt:QT * (_qt + 1)],
                                   start=(_c == 0), stop=(_c == 7)))
                tm = E.st("t", fns, deps=[tok_wo, war.get(("ps_wo", bk))])
                tev = E.st("v", lambda v, _o=o, _bk=bk: v.tensor_copy(
                    atF[:, _o, :], ps_wo[_bk][:, :]), deps=[tm])
                war[("ps_wo", bk)] = tev
                tok_wom.append(tev)
            for j in range(4):
                bk = j % 2
                ttr = E.st("t", [
                    (lambda t, _j=j, _o=o, _bk=bk: t.matmul(
                        ps_tm[_bk][:, 128 * _o:128 * (_o + 1)],
                        atF[:, _o, 128 * _j:128 * (_j + 1)], identf[:, :],
                        start=True, stop=True, is_transpose=True))
                    for o in range(8)
                ], deps=tok_wom + [war.get(("ps_tm", bk))])
                tst = E.st("v", lambda v, _bk=bk, _j=j: v.tensor_scalar(
                    at_sb[:, :], ps_tm[_bk][:, :], 1.0, 0.0, ALU.mult, ALU.add,
                    accum_out=stat[:, SX + _j:SX + _j + 1]), deps=[ttr])
                tst2 = E.st("v", lambda v, _j=j: v.scalar_tensor_tensor(
                    junk[:, :], at_sb[:, :], 1.0, at_sb[:, :], ALU.mult,
                    ALU.mult, accum_out=stat[:, SXX + _j:SXX + _j + 1]),
                    deps=[tst])
                war[("ps_tm", bk)] = tst
                tfin = E.st("v", [
                    lambda v, _j=j: v.tensor_scalar(
                        mcol(_j), stat[:, SX + _j:SX + _j + 1], 1.0 / 1024.0,
                        0.0, ALU.mult, ALU.add),
                    lambda v, _j=j: v.tensor_tensor(
                        stat[:, M2 + _j:M2 + _j + 1], mcol(_j), mcol(_j), ALU.mult),
                    lambda v, _j=j: v.tensor_scalar(
                        rcol(_j), stat[:, SXX + _j:SXX + _j + 1], 1.0 / 1024.0,
                        0.0, ALU.mult, ALU.add),
                    lambda v, _j=j: v.tensor_tensor(
                        rcol(_j), rcol(_j), stat[:, M2 + _j:M2 + _j + 1], ALU.subtract),
                ], deps=[tst2, war.get("lnrs")])
                trsE = E.st("s", [
                    lambda s, _j=j: s.activation(rcol(_j), rcol(_j), AF.Ln,
                                                 bias=cbias[:, 0:1]),
                    lambda s, _j=j: s.activation(rcol(_j), rcol(_j), AF.Exp,
                                                 scale=-0.5),
                ], deps=[tfin])
                war["lnrs"] = trsE
                sl = qt % 2
                tnr = E.st("v", [
                    lambda v, _j=j, _sl=sl: v.tensor_scalar(
                        x2st[:, _sl, _j, :], at_sb[:, :], mcol(_j), rcol(_j),
                        ALU.subtract, ALU.mult),
                    lambda v, _j=j, _sl=sl: v.tensor_tensor(
                        x2st[:, _sl, _j, :], x2st[:, _sl, _j, :], xqE[:, _j, :],
                        ALU.add),
                ], deps=[trsE, tq, war.get(("x2st", sl))])
                tok_nrs.append(tnr)
                war["at_sb"] = tnr
            tso = E.st("g", lambda g, _qt=qt, _sl=qt % 2: g.dma_start(
                x2_d[QT * _qt:QT * (_qt + 1), :].rearrange(
                    "(a p) o -> p a o", p=128),
                x2st[:, _sl, :, :]), deps=list(tok_nrs))
            war[("x2st", qt % 2)] = tso
            war["xqE"] = war["at_sb"]
        if dbg:
            E.st("g", [
                lambda g: g.dma_start(dctx_d[:, :, :], ctxF[:, :, :]),
                lambda g: g.dma_start(dx2r0_d, x2r0[:, :]),
                lambda g: g.dma_start(dct_d[:, :], ctx0T[:, 0:8]),
                lambda g: g.dma_start(dat0_d, at0[:, :]),
                lambda g: g.dma_start(dcolb_d, colb[:, :]),
            ], deps=[war["x2r0"]])
        tok_x2_done = [war[("x2st", 0)], war[("x2st", 1)]]
        E.run(nc)
        e_ps.close()
        e_scope.close()
        p_scope.close()
        ap_scope.close()
        set_gate()

        f_scope = ExitStack()
        top.enter_context(f_scope)
        wr_sb = sb("wr_sb", [128, 8, 1024], BF16, f_scope)
        wkf_sb = sb("wkf_sb", [128, 8, 1024], BF16, f_scope)
        wvf_sb = sb("wvf_sb", [128, 8, 1024], BF16, f_scope)
        h2T = sb("h2T", [128, 8, W1], BF16, f_scope)
        ffnout = sb("ffnout", [128, 16, 1024], BF16, f_scope)
        x2t = sb("x2t", [128, 2, 2, 1024], F32, f_scope)
        h2b = sb("h2b", [128, 1024], BF16, f_scope)
        xrT = sb("xrT", [128, 2, 8, 512], BF16, f_scope)
        xkT = sb("xkT", [128, 2, 8, 512], BF16, f_scope)
        kactT = sb("kactT", [128, 8, 512], BF16, f_scope)
        krelu = sb("krelu", [128, 512], BF16, f_scope)
        rt = sb("rt", [128, 1024], F32, f_scope)
        tmp8 = sb("tmp8", [128, 512], BF16, f_scope)
        sx3 = sb("sx3", [128, 16], F32, f_scope)
        sxx3 = sb("sxx3", [128, 16], F32, f_scope)
        m3 = sb("m3", [128, 16], F32, f_scope)
        rstd3 = sb("rstd3", [128, 16], F32, f_scope)
        outst = sb("outst", [128, 2, 2, 1024], F32, f_scope)

        fp_ps = ExitStack()
        f_scope.enter_context(fp_ps)
        ps_tpF = [ps(f"ps_tpF{i}", [128, 1024], BF16, fp_ps) for i in range(2)]
        E = em()
        tok_wf = E.st("g", [
            lambda g: g.dma_start(wr_sb[:, :, :],
                                  wr.rearrange("(c p) o -> p c o", p=128)),
            lambda g: g.dma_start(wkf_sb[:, :, :],
                                  wkf.rearrange("(c p) o -> p c o", p=128)),
            lambda g: g.dma_start(wvf_sb[:, :, :],
                                  wvf.rearrange("(c p) o -> p c o", p=128)),
        ])
        trs0 = ln_stats(E, lambda j: x2r0[0:1, :], 1, [[war["x2r0"]]], cb=8, pp=1)
        th20 = E.st("v", lambda v: v.tensor_scalar(
            h2b[0:1, :], x2r0[0:1, :], stat[0:1, MM + 8:MM + 9],
            stat[0:1, RS + 8:RS + 9], ALU.subtract, ALU.mult), deps=[trs0])
        tt0 = E.st("t", [
            (lambda t, _c=c: t.matmul(
                ps_tpF[0][:, 2 * _c:2 * _c + 1], h2b[0:1, 128 * _c:128 * (_c + 1)],
                identb[0:1, 0:1], start=True, stop=True,
                is_transpose=True))
            for c in range(8)
        ], deps=[th20])
        tev0 = E.st("v", lambda v: v.tensor_copy(
            h2T[:, :, 0:1],
            ps_tpF[0][:, 0:16].rearrange("p (c e) -> p c e", e=2)[:, :, 0:1]),
            deps=[tt0])
        war[("ps_tpF", 0)] = tev0
        for tt_i in range(16):
            b2, sub = tt_i // 2, tt_i % 2
            sl = b2 % 2
            if sub == 0:
                tqd = E.st("y", lambda y, _b2=b2, _sl=sl: y.dma_start(
                    x2t[:, _sl, :, :],
                    x2_d[256 * _b2:256 * (_b2 + 1), :].rearrange(
                        "(a p) o -> p a o", p=128)),
                    deps=tok_x2_done + [war.get(("x2t", sl))])
            trs = ln_stats(E, lambda j, _sl=sl, _su=sub: x2t[:, _sl, _su, :],
                           1, [[tqd]], cb=tt_i % 4)
            tn = E.st("v", lambda v, _sl=sl, _su=sub, _tt=tt_i: v.tensor_scalar(
                h2b[:, :], x2t[:, _sl, _su, :], mcol(_tt % 4), rcol(_tt % 4),
                ALU.subtract, ALU.mult), deps=[trs])
            war[("x2t", sl)] = tn
            bk = tt_i % 2
            ttr = E.st("t", [
                (lambda t, _c=c, _bk=bk: t.matmul(
                    ps_tpF[_bk][:, 128 * _c:128 * (_c + 1)],
                    h2b[:, 128 * _c:128 * (_c + 1)], identb[:, :],
                    start=True, stop=True, is_transpose=True))
                for c in range(8)
            ], deps=[tn, war.get(("ps_tpF", bk))])
            tev = E.st("v", lambda v, _tt=tt_i, _bk=bk: v.tensor_copy(
                h2T[:, :, 1 + 128 * _tt:1 + 128 * (_tt + 1)],
                ps_tpF[_bk][:, :].rearrange("p (c q) -> p c q", q=128)),
                deps=[ttr])
            war[("ps_tpF", bk)] = tev
            war["h2T"] = tev
        E.run(nc)
        fp_ps.close()
        set_gate()

        fa_ps = ExitStack()
        f_scope.enter_context(fa_ps)
        ps_r = ps("ps_r", [128, 1024], F32, fa_ps)
        ps_k = [ps(f"ps_kF{i}", [128, 512], F32, fa_ps) for i in range(2)]
        ps_vo = [ps(f"ps_voF{i}", [128, 1024], F32, fa_ps) for i in range(2)]
        E = em()

        def _ffn_build(_ft):
            _c0 = QT * _ft
            _sl = _ft % 2
            toks = []
            for c in range(8):
                for mu_t, omu_t, dstT in ((mur, omur, xrT), (muk, omuk, xkT)):
                    t1 = E.st("v", lambda v, _c=c, _omu=omu_t, _cc=_c0:
                              v.tensor_scalar(
                                  tmp8[:, :], h2T[:, _c, _cc:_cc + QT],
                                  _omu[:, _c:_c + 1], 0.0, ALU.mult, ALU.add),
                              deps=[war["h2T"], war.get(("xT_rd", _sl))])
                    t2 = E.st("v", lambda v, _c=c, _mu=mu_t, _dst=dstT,
                              _cc=_c0, _s=_sl:
                              v.scalar_tensor_tensor(
                                  _dst[:, _s, _c, :],
                                  h2T[:, _c, _cc + 1:_cc + 1 + QT],
                                  _mu[:, _c:_c + 1], tmp8[:, :], ALU.mult,
                                  ALU.add), deps=[t1])
                    toks.append(t2)
            return toks

        tok_builds = {0: _ffn_build(0)}
        for ft in range(4):
            sl = ft % 2
            tok_x = tok_builds[ft]
            tok_k = []
            for o in range(8):
                bk = o % 2
                tmm = E.st("t", [
                    (lambda t, _o=o, _c=c, _bk=bk, _s=sl: t.matmul(
                        ps_k[_bk][:, :], wkf_sb[:, _c, 128 * _o:128 * (_o + 1)],
                        xkT[:, _s, _c, :], start=(_c == 0), stop=(_c == 7)))
                    for c in range(8)
                ], deps=tok_x + [war.get(("ps_kF", bk)), tok_wf])
                trl = E.st("v", lambda v, _bk=bk: v.tensor_scalar_max(
                    krelu[:, :], ps_k[_bk][:, :], 0.0), deps=[tmm])
                war[("ps_kF", bk)] = trl
                tk2 = E.st("v", lambda v, _o=o: v.tensor_tensor(
                    kactT[:, _o, :], krelu[:, :], krelu[:, :], ALU.mult),
                    deps=[trl])
                tok_k.append(tk2)
            if ft + 1 < 4:
                tok_builds[ft + 1] = _ffn_build(ft + 1)
            for j in range(4):
                slot = 4 * ft + j
                fns = []
                for half in range(2):
                    for c in range(8):
                        fns.append(lambda t, _j=j, _c=c, _h=half, _s=sl:
                                   t.matmul(
                            ps_r[:, 512 * _h:512 * (_h + 1)],
                            xrT[:, _s, _c, 128 * _j:128 * (_j + 1)],
                            wr_sb[:, _c, 512 * _h:512 * (_h + 1)],
                            start=(_c == 0), stop=(_c == 7)))
                tmr = E.st("t", fns, deps=tok_x + [war.get("ps_r"), tok_wf])
                tth = E.st("s", lambda s: s.activation(
                    rt[:, :], ps_r[:, :], AF.Tanh, scale=0.5),
                    deps=[tmr, war.get("rt")])
                war["ps_r"] = tth
                tsg = E.st("v", lambda v: v.tensor_scalar(
                    rt[:, :], rt[:, :], 0.5, 0.5, ALU.mult, ALU.add),
                    deps=[tth])
                bk = j % 2
                fns = []
                for half in range(2):
                    for c in range(8):
                        fns.append(lambda t, _j=j, _c=c, _h=half, _bk=bk:
                                   t.matmul(
                                       ps_vo[_bk][:, 512 * _h:512 * (_h + 1)],
                                       kactT[:, _c, 128 * _j:128 * (_j + 1)],
                                       wvf_sb[:, _c, 512 * _h:512 * (_h + 1)],
                                       start=(_c == 0), stop=(_c == 7)))
                tmv = E.st("t", fns, deps=tok_k + [war.get(("ps_voF", bk)),
                                                   tok_wf])
                war[("xT_rd", sl)] = tmv
                tcf = E.st("v", lambda v, _slot=slot, _bk=bk:
                           v.scalar_tensor_tensor(
                               ffnout[:, _slot, :], ps_vo[_bk][:, :], 1.0,
                               rt[:, :], ALU.mult, ALU.mult,
                               accum_out=sx3[:, _slot:_slot + 1]),
                           deps=[tmv, tsg])
                war[("ps_voF", bk)] = tcf
                war["rt"] = tcf
                tcq = E.st("v", lambda v, _slot=slot: v.scalar_tensor_tensor(
                    junk[:, :], ffnout[:, _slot, :], 1.0, ffnout[:, _slot, :],
                    ALU.mult, ALU.mult,
                    accum_out=sxx3[:, _slot:_slot + 1]), deps=[tcf])
                war[("ffn", slot)] = tcq
        tok_ffn_done = [war[("ffn", s_)] for s_ in range(16)]
        E.run(nc)
        fa_ps.close()
        set_gate()

        E = em()
        tfin = E.st("v", [
            lambda v: v.tensor_scalar(m3[:, :], sx3[:, :], 1.0 / 1024.0, 0.0,
                                      ALU.mult, ALU.add),
            lambda v: v.tensor_tensor(stat[:, M2:M2 + 16], m3[:, :],
                                      m3[:, :], ALU.mult),
            lambda v: v.tensor_scalar(rstd3[:, :], sxx3[:, :], 1.0 / 1024.0,
                                      0.0, ALU.mult, ALU.add),
            lambda v: v.tensor_tensor(rstd3[:, :], rstd3[:, :],
                                      stat[:, M2:M2 + 16], ALU.subtract),
        ], deps=tok_ffn_done + [war.get("lnrs")])
        trs3 = E.st("s", [
            lambda s: s.activation(rstd3[:, :], rstd3[:, :], AF.Ln,
                                   bias=cbias[:, 0:1]),
            lambda s: s.activation(rstd3[:, :], rstd3[:, :], AF.Exp,
                                   scale=-0.5),
        ], deps=[tfin])
        for tt_i in range(16):
            b2, sub = tt_i // 2, tt_i % 2
            sl = b2 % 2
            if sub == 0:
                tqd = E.st("y", lambda y, _b2=b2, _sl=sl: y.dma_start(
                    x2t[:, _sl, :, :],
                    x2_d[256 * _b2:256 * (_b2 + 1), :].rearrange(
                        "(a p) o -> p a o", p=128)),
                    deps=[war.get(("x2tF", sl))])
                tok_fin = []
            tn = E.st("v", [
                lambda v, _tt=tt_i, _sl=sl, _su=sub: v.tensor_scalar(
                    outst[:, _sl, _su, :], ffnout[:, _tt, :],
                    m3[:, _tt:_tt + 1],
                    rstd3[:, _tt:_tt + 1], ALU.subtract, ALU.mult),
                lambda v, _tt=tt_i, _sl=sl, _su=sub: v.tensor_tensor(
                    outst[:, _sl, _su, :], outst[:, _sl, _su, :],
                    x2t[:, _sl, _su, :],
                    ALU.add),
            ], deps=[trs3, tqd, war.get(("outst", sl))])
            war[("x2tF", sl)] = tn
            tok_fin.append(tn)
            if sub == 1:
                tso = E.st("g", lambda g, _b2=b2, _sl=sl: g.dma_start(
                    out_d[256 * _b2:256 * (_b2 + 1), :].rearrange(
                        "(a p) o -> p a o", p=128),
                    outst[:, _sl, :, :]), deps=list(tok_fin))
                war[("outst", sl)] = tso
        E.raw("g", [("sG", counters["sG"])], [lambda g: g.engine_nop()], 0)
        E.run(nc)

    return nc



def _trig(positions):
    import jax
    import jax.numpy as jnp
    cpu = jax.devices("cpu")[0]
    with jax.default_device(cpu):
        pos = jnp.asarray(np.asarray(positions, np.float32))
        freqs = jnp.exp(jnp.arange(0, HD, 2, dtype=jnp.float32)
                        * (-math.log(10000.0) / HD))
        ang = pos[:, None] * freqs[None, :]
        c = np.asarray(jnp.cos(ang)).T.astype(np.float32)
        s = np.asarray(jnp.sin(ang)).T.astype(np.float32)
    t = np.concatenate([c, s], axis=0)
    return np.vstack([t, t]).astype(np.float32)


def _dup_rows(W):
    Wd = np.empty_like(W)
    for h in range(H):
        blk = W[h * HD:(h + 1) * HD]
        Wd[h * HD:h * HD + HD // 2] = blk[0::2]
        Wd[h * HD + HD // 2:(h + 1) * HD] = blk[0::2]
    return np.ascontiguousarray(Wd.T)


def make_in_maps(L0_attn_x, L0_attn_Wq, L0_attn_Wk, L0_attn_Wv, L0_attn_Wo,
                 L0_ffn_W_r, L0_ffn_W_k, L0_ffn_W_v, L0_ffn_mu_r, L0_ffn_mu_k):
    f32 = np.float32
    x = np.ascontiguousarray(np.asarray(L0_attn_x, f32))
    wq_h = _dup_rows(np.asarray(L0_attn_Wq, f32))
    wk_h = _dup_rows(np.asarray(L0_attn_Wk, f32))
    wv_h = np.ascontiguousarray(np.asarray(L0_attn_Wv, f32).T)
    wo_h = np.ascontiguousarray(np.asarray(L0_attn_Wo, f32).T)
    wr_h = np.ascontiguousarray(np.asarray(L0_ffn_W_r, f32).T)
    wkf_h = np.ascontiguousarray(np.asarray(L0_ffn_W_k, f32).T)
    wvf_h = np.ascontiguousarray(np.asarray(L0_ffn_W_v, f32).T)
    mur_h = np.ascontiguousarray(np.asarray(L0_ffn_mu_r, f32).reshape(8, 128).T)
    muk_h = np.ascontiguousarray(np.asarray(L0_ffn_mu_k, f32).reshape(8, 128).T)
    trigk_h = _trig(np.arange(SKV))
    identb_h = np.eye(128, dtype=bfdt)
    identf_h = np.eye(128, dtype=np.float32)
    u0m_h = np.zeros((128, 2), np.float32)
    u0m_h[0:64, 0] = 1
    u0m_h[64:128, 1] = 1
    zrow = np.zeros((1, D), f32)
    trigq0 = np.ascontiguousarray(np.concatenate(
        [np.zeros((128, 1), np.float32), _trig(np.arange(OWN))], axis=1))
    trigq1 = np.ascontiguousarray(_trig(np.arange(OWN - 1, SKV)))

    in_maps = []
    for core in range(8):
        b, s = core // 2, core % 2
        if s == 0:
            xq_h = np.ascontiguousarray(np.concatenate([zrow, x[b, 0:OWN]], 0))
            trigq_h, bm = trigq0, np.zeros((1, 1), f32)
        else:
            xq_h = np.ascontiguousarray(x[b, OWN - 1:SKV])
            trigq_h, bm = trigq1, np.ones((1, 1), f32)
        in_maps.append({
            "xkv": np.ascontiguousarray(x[b]), "xq": xq_h,
            "wq": wq_h, "wk": wk_h, "wv": wv_h, "wo": wo_h,
            "wr": wr_h, "wkf": wkf_h, "wvf": wvf_h,
            "mur": mur_h, "muk": muk_h,
            "trigk": trigk_h, "trigq": trigq_h,
            "identb": identb_h, "identf": identf_h,
            "u0m": u0m_h, "bmask": bm,
        })
    return in_maps


def kernel(L0_attn_x, L0_attn_Wq, L0_attn_Wk, L0_attn_Wv, L0_attn_Wo,
           L0_ffn_W_r, L0_ffn_W_k, L0_ffn_W_v, L0_ffn_mu_r, L0_ffn_mu_k):
    in_maps = make_in_maps(L0_attn_x, L0_attn_Wq, L0_attn_Wk, L0_attn_Wv,
                           L0_attn_Wo, L0_ffn_W_r, L0_ffn_W_k, L0_ffn_W_v,
                           L0_ffn_mu_r, L0_ffn_mu_k)
    nc = build_program()
    res = run_bass_kernel_spmd(nc, in_maps, core_ids=list(range(8)))
    out = np.empty((B, SKV, D), np.asarray(res.results[0]["out"]).dtype)
    for core in range(8):
        b, s = core // 2, core % 2
        out[b, OWN * s:OWN * (s + 1)] = res.results[core]["out"]
    return out


if __name__ == "__main__":
    from reference import setup_inputs
    ins = setup_inputs(0)
    o = kernel(**{k: np.asarray(v) for k, v in ins.items()})
    print(o.shape, o.dtype, float(np.abs(o).mean()))


# version 19 (final)
# speedup vs baseline: 1.0773x; 1.0186x over previous
import math
from contextlib import ExitStack

import numpy as np
import ml_dtypes

import concourse.bass as bass
import concourse.mybir as mybir
from concourse.bass_utils import run_bass_kernel_spmd

F32 = mybir.dt.float32
BF16 = mybir.dt.bfloat16
AF = mybir.ActivationFunctionType
ALU = mybir.AluOpType
AX = mybir.AxisListType
bfdt = ml_dtypes.bfloat16

B, SKV, D, H, HD = 4, 4096, 1024, 16, 64
OWN = 2048
W1 = OWN + 1
QT = 512
NQT = OWN // QT
KC = SKV // 128
NG = SKV // 512
EPS = 1e-5

ENG_SEM = {"t": "sT", "v": "sV", "s": "sS", "y": "sY", "g": "sG"}
ENG_ATTR = {"t": "tensor", "v": "vector", "s": "scalar", "y": "sync", "g": "gpsimd"}
DMA_ENG = ("y", "g")


class Em:

    def __init__(self, sems, counters, lastw, gate=()):
        self.sems = sems
        self.c = counters
        self.lastw = lastw
        self.gate = list(gate)
        self.ops = {e: [] for e in ENG_SEM}

    def st(self, eng, fns, deps=()):
        if not isinstance(fns, (list, tuple)):
            fns = [fns]
        sem = ENG_SEM[eng]
        waits = []
        for d in list(deps) + self.gate:
            if d is None:
                continue
            dsem, dval = d
            if dsem == sem and eng not in DMA_ENG:
                continue
            key = (eng, dsem)
            if self.lastw.get(key, -1) >= dval:
                continue
            self.lastw[key] = dval
            waits.append((dsem, dval))
        pre = self.c[sem]
        if eng in DMA_ENG:
            if pre > 0 and self.lastw.get((eng, sem), -1) < pre:
                self.lastw[(eng, sem)] = pre
                waits.append((sem, pre))
            self.c[sem] += 16 * len(fns)
        else:
            self.c[sem] += len(fns)
        self.ops[eng].append((waits, list(fns), eng in DMA_ENG, pre))
        return (sem, self.c[sem])

    def raw(self, eng, waits, fns, inc):
        sem = ENG_SEM[eng]
        self.c[sem] += inc
        self.ops[eng].append((waits, list(fns), False, ("raw", inc)))
        return (sem, self.c[sem])

    def play(self, eng, engobj):
        sem = self.sems[ENG_SEM[eng]]
        own = ENG_SEM[eng]
        serial = eng in ("v", "s")
        for waits, fns, is_dma, pre in self.ops[eng]:
            for (s, v) in waits:
                engobj.wait_ge(self.sems[s], v)
            if is_dma:
                for f in fns:
                    f(engobj).then_inc(sem, 16)
            elif isinstance(pre, tuple):
                insts = [f(engobj) for f in fns]
                if insts and pre[1]:
                    insts[-1].then_inc(sem, pre[1])
            else:
                for i, f in enumerate(fns):
                    if serial and pre + i > 0:
                        engobj.wait_ge(sem, pre + i)
                    f(engobj).then_inc(sem, 1)

    def run(self, nc):
        with nc.Block() as blk:
            for eng in ("y", "g", "t", "v", "s"):
                if self.ops[eng]:
                    getattr(blk, ENG_ATTR[eng])(
                        lambda e, _eng=eng: self.play(_eng, e))


def build_program(dbg=False):
    nc = bass.Bass("TRN2", target_bir_lowering=False)

    def din(name, shape, dt=F32):
        return nc.dram_tensor(name, shape, dt, kind="ExternalInput").ap()

    xkv = din("xkv", [SKV, D])
    xq = din("xq", [W1, D])
    wq = din("wq", [D, D])
    wk = din("wk", [D, D])
    wv = din("wv", [D, D])
    wo = din("wo", [D, D])
    wr = din("wr", [D, D])
    wkf = din("wkf", [D, D])
    wvf = din("wvf", [D, D])
    mur_i = din("mur", [128, 8])
    muk_i = din("muk", [128, 8])
    trigk_i = din("trigk", [128, SKV], F32)
    trigq_i = din("trigq", [128, W1], F32)
    identb_i = din("identb", [128, 128], BF16)
    identf_i = din("identf", [128, 128], F32)
    u0m_i = din("u0m", [128, 2])
    bmask_i = din("bmask", [1, 1])

    out_d = nc.dram_tensor("out", [OWN, D], F32, kind="ExternalOutput").ap()
    skind = "ExternalOutput" if dbg else "Internal"
    x2_d = nc.dram_tensor("x2d", [OWN, D], F32, kind=skind).ap()
    ukhi_d = nc.dram_tensor("ukhi", [128, 4, SKV], BF16, kind="Internal").ap()
    vhi_d = nc.dram_tensor("vhi", [SKV, 512], BF16, kind="Internal").ap()
    if dbg:
        dctx_d = nc.dram_tensor("dctx", [128, 8, OWN], BF16,
                                kind="ExternalOutput").ap()
        dx2r0_d = nc.dram_tensor("dx2r0", [1, D], F32, kind="ExternalOutput").ap()
        dU0c_d = nc.dram_tensor("dU0c", [2, 128, 4, 8], BF16,
                                kind="ExternalOutput").ap()
        dP0_d = nc.dram_tensor("dP0", [2, 8, SKV], BF16,
                               kind="ExternalOutput").ap()
        dc0_d = nc.dram_tensor("dc0", [2, 1, 512], BF16,
                               kind="ExternalOutput").ap()
        dct_d = nc.dram_tensor("dct", [128, 8], BF16,
                               kind="ExternalOutput").ap()
        dat0_d = nc.dram_tensor("dat0", [1, D], F32,
                                kind="ExternalOutput").ap()
        dcolb_d = nc.dram_tensor("dcolb", [128, 8], F32,
                                 kind="ExternalOutput").ap()

    sems = {}
    counters = {s: 0 for s in ENG_SEM.values()}
    lastw = {}
    war = {}

    gate_hold = [[]]

    def em():
        return Em(sems, counters, lastw, gate_hold[0])

    def set_gate():
        gate_hold[0] = [(s, counters[s]) for s in counters if counters[s] > 0]

    with ExitStack() as top:
        for s in ("sT", "sV", "sS", "sY", "sG"):
            sems[s] = top.enter_context(nc.semaphore(s))

        def sb(name, shape, dt, ctx):
            return ctx.enter_context(nc.sbuf_tensor("s_" + name, shape, dt))

        def ps(name, shape, dt, ctx):
            return ctx.enter_context(nc.psum_tensor(name, shape, dt))

        identb = sb("identb", [128, 128], BF16, top)
        identf = sb("identf", [128, 128], F32, top)
        u0m = sb("u0m", [128, 2], F32, top)
        mur = sb("mur", [128, 8], F32, top)
        muk = sb("muk", [128, 8], F32, top)
        omur = sb("omur", [128, 8], F32, top)
        omuk = sb("omuk", [128, 8], F32, top)
        bmask = sb("bmask", [1, 1], F32, top)
        junk = sb("junk", [128, 1024], BF16, top)
        stat = sb("stat", [128, 80], F32, top)
        x2r0 = sb("x2r0", [1, D], F32, top)
        xq0 = sb("xq0", [1, D], F32, top)
        cbias = sb("cbias", [128, 2], F32, top)

        SX, SXX, MM, RS, M2 = 0, 16, 32, 48, 64

        def mcol(j, cb=0):
            return stat[:, MM + cb + j:MM + cb + j + 1]

        def rcol(j, cb=0):
            return stat[:, RS + cb + j:RS + cb + j + 1]

        def ln_stats(E, srcs, nsub, deps_each, n=1024.0, cb=0, pp=128):
            toks = []
            for j in range(nsub):
                t1 = E.st("v", lambda v, _j=j: v.tensor_scalar(
                    junk[0:pp, :], srcs(_j), 1.0, 0.0, ALU.mult, ALU.add,
                    accum_out=stat[0:pp, SX + cb + _j:SX + cb + _j + 1]),
                    deps=deps_each[j])
                t2 = E.st("v", lambda v, _j=j: v.scalar_tensor_tensor(
                    junk[0:pp, :], srcs(_j), 1.0, srcs(_j), ALU.mult, ALU.mult,
                    accum_out=stat[0:pp, SXX + cb + _j:SXX + cb + _j + 1]),
                    deps=deps_each[j])
                toks += [t1, t2]
            c0, c1 = cb, cb + nsub
            tm = E.st("v", [
                lambda v: v.tensor_scalar(stat[:, MM + c0:MM + c1],
                                          stat[:, SX + c0:SX + c1], 1.0 / n, 0.0,
                                          ALU.mult, ALU.add),
                lambda v: v.tensor_tensor(stat[:, M2 + c0:M2 + c1],
                                          stat[:, MM + c0:MM + c1],
                                          stat[:, MM + c0:MM + c1], ALU.mult),
                lambda v: v.tensor_scalar(stat[:, RS + c0:RS + c1],
                                          stat[:, SXX + c0:SXX + c1], 1.0 / n, 0.0,
                                          ALU.mult, ALU.add),
                lambda v: v.tensor_tensor(stat[:, RS + c0:RS + c1],
                                          stat[:, RS + c0:RS + c1],
                                          stat[:, M2 + c0:M2 + c1], ALU.subtract),
            ], deps=toks + [war.get("lnrs")])
            trs = E.st("s", [
                lambda s: s.activation(stat[:, RS + c0:RS + c1],
                                       stat[:, RS + c0:RS + c1], AF.Ln,
                                       bias=cbias[:, 0:1]),
                lambda s: s.activation(stat[:, RS + c0:RS + c1],
                                       stat[:, RS + c0:RS + c1], AF.Exp,
                                       scale=-0.5),
            ], deps=[tm])
            war["lnrs"] = trs
            return trs

        ap_scope = ExitStack()
        top.enter_context(ap_scope)
        uk_sb = sb("uk_sb", [128, 4, SKV], BF16, ap_scope)
        vbuf = sb("vbuf", [128, KC, 8, 65], BF16, ap_scope)
        a_scope = ExitStack()
        ap_scope.enter_context(a_scope)
        wk_sb = sb("wk_sb", [128, 8, 1024], BF16, a_scope)
        wv_sb = sb("wv_sb", [128, 8, 1024], BF16, a_scope)
        trigk = sb("trigk", [128, SKV], BF16, a_scope)
        xg = sb("xg", [128, 2, 4, 1024], F32, a_scope)
        h1g = sb("h1g", [128, 2, 4, 1024], BF16, a_scope)
        h1gT = sb("h1gT", [128, 8, 512], BF16, a_scope)
        ukst = sb("ukst", [128, 2, 512], BF16, a_scope)
        vst = sb("vst", [128, 2, 512], BF16, a_scope)

        E = em()
        tok_const = E.st("y", [
            lambda y: y.dma_start(identb[:, :], identb_i),
            lambda y: y.dma_start(identf[:, :], identf_i),
            lambda y: y.dma_start(u0m[:, :], u0m_i),
            lambda y: y.dma_start(mur[:, :], mur_i),
            lambda y: y.dma_start(muk[:, :], muk_i),
            lambda y: y.dma_start(bmask[:, :], bmask_i),
            lambda y: y.dma_start(xq0[:, :], xq[0:1, :]),
        ])
        E.st("s", [
            lambda s: s.activation(omur[:, :], mur[:, :], AF.Identity,
                                   bias=1.0, scale=-1.0),
            lambda s: s.activation(omuk[:, :], muk[:, :], AF.Identity,
                                   bias=1.0, scale=-1.0),
        ], deps=[tok_const])
        _e1 = float(np.float32(EPS))
        _e2 = float(np.float32(EPS - _e1))
        tok_kn0 = E.st("v", [
            lambda v: v.memset(cbias[:, 0:1], _e1),
            lambda v: v.tensor_scalar(cbias[:, 0:1], cbias[:, 0:1], 1.0, _e2,
                                      ALU.mult, ALU.add),
            lambda v: v.memset(cbias[:, 1:2], 1e-30),
            lambda v: v.memset(stat[:, :], 0.0),
        ], deps=[])
        tok_vones = E.st("v", lambda v: v.memset(vbuf[:, :, :, 64:65], 1.0),
                         deps=[])
        tok_wkv = E.st("g", [
            lambda g: g.dma_start(
                wk_sb[:, :, :], wk.rearrange("(c p) o -> p c o", p=128)),
            lambda g: g.dma_start(
                wv_sb[:, :, :], wv.rearrange("(c p) o -> p c o", p=128)),
        ])
        tok_trigk = E.st("g", lambda g: g.dma_start(trigk[:, :], trigk_i))
        E.run(nc)

        E = em()
        a_ps = ExitStack()
        a_scope.enter_context(a_ps)
        ps_tp = [ps(f"ps_tpA{i}", [128, 1024], BF16, a_ps) for i in range(2)]
        ps_ko = [ps(f"ps_koA{i}", [128, 512], F32, a_ps) for i in range(2)]
        ps_v = ps("ps_vA", [128, 1024], F32, a_ps)

        def emit_xg_dma(g, dep):
            sl = g % 2
            return E.st("y", lambda y, _g=g, _sl=sl: y.dma_start(
                xg[:, _sl, :, :],
                xkv[512 * _g:512 * (_g + 1), :].rearrange("(a p) o -> p a o", p=128)),
                deps=[dep])

        tok_dma = {0: emit_xg_dma(0, tok_wkv), 1: emit_xg_dma(1, None)}

        def a_accum(g, idx):
            sl, cb = g % 2, (g % 2) * 4
            j, kind = idx // 2, idx % 2
            if kind == 0:
                return E.st("v", lambda v, _j=j, _sl=sl, _cb=cb: v.tensor_scalar(
                    junk[:, :], xg[:, _sl, _j, :], 1.0, 0.0, ALU.mult, ALU.add,
                    accum_out=stat[:, SX + _cb + _j:SX + _cb + _j + 1]),
                    deps=[tok_dma[g]])
            return E.st("v", lambda v, _j=j, _sl=sl, _cb=cb:
                        v.scalar_tensor_tensor(
                junk[:, :], xg[:, _sl, _j, :], 1.0, xg[:, _sl, _j, :],
                ALU.mult, ALU.mult,
                accum_out=stat[:, SXX + _cb + _j:SXX + _cb + _j + 1]),
                deps=[tok_dma[g]])

        def a_finish(g, toks):
            cb = (g % 2) * 4
            c0, c1 = cb, cb + 4
            tm = E.st("v", [
                lambda v: v.tensor_scalar(stat[:, MM + c0:MM + c1],
                                          stat[:, SX + c0:SX + c1],
                                          1.0 / 1024.0, 0.0, ALU.mult, ALU.add),
                lambda v: v.tensor_tensor(stat[:, M2 + c0:M2 + c1],
                                          stat[:, MM + c0:MM + c1],
                                          stat[:, MM + c0:MM + c1], ALU.mult),
                lambda v: v.tensor_scalar(stat[:, RS + c0:RS + c1],
                                          stat[:, SXX + c0:SXX + c1],
                                          1.0 / 1024.0, 0.0, ALU.mult, ALU.add),
                lambda v: v.tensor_tensor(stat[:, RS + c0:RS + c1],
                                          stat[:, RS + c0:RS + c1],
                                          stat[:, M2 + c0:M2 + c1],
                                          ALU.subtract),
            ], deps=toks + [war.get("lnrs")])
            trs = E.st("s", [
                lambda s: s.activation(stat[:, RS + c0:RS + c1],
                                       stat[:, RS + c0:RS + c1], AF.Ln,
                                       bias=cbias[:, 0:1]),
                lambda s: s.activation(stat[:, RS + c0:RS + c1],
                                       stat[:, RS + c0:RS + c1], AF.Exp,
                                       scale=-0.5),
            ], deps=[tm])
            war["lnrs"] = trs
            return trs

        def a_norm(g, j, extra):
            sl, cb = g % 2, (g % 2) * 4
            return E.st("v", lambda v, _j=j, _sl=sl, _cb=cb: v.tensor_scalar(
                h1g[:, _sl, _j, :], xg[:, _sl, _j, :], mcol(_j, _cb),
                rcol(_j, _cb), ALU.subtract, ALU.mult), deps=extra)

        toks0 = [a_accum(0, i) for i in range(8)]
        trs0a = a_finish(0, toks0)
        tok_norm = {0: [a_norm(0, j, [trs0a]) for j in range(4)]}
        for g in range(NG):
            sl = g % 2
            if g + 2 < NG:
                tok_dma[g + 2] = emit_xg_dma(g + 2, tok_norm[g][-1])
            tok_ev = []
            for j in range(4):
                bk = j % 2
                tt = E.st("t", [
                    (lambda t, _j=j, _c=c, _bk=bk, _sl=sl: t.matmul(
                        ps_tp[_bk][:, 128 * _c:128 * (_c + 1)],
                        h1g[:, _sl, _j, 128 * _c:128 * (_c + 1)], identb[:, :],
                        start=True, stop=True, is_transpose=True))
                    for c in range(8)
                ], deps=[tok_norm[g][j], war.get(("ps_tpA", bk)), tok_const])
                tev = E.st("v", lambda v, _j=j, _bk=bk: v.tensor_copy(
                    h1gT[:, :, 128 * _j:128 * (_j + 1)],
                    ps_tp[_bk][:, :].rearrange("p (c q) -> p c q", q=128)),
                    deps=[tt])
                war[("ps_tpA", bk)] = tev
                tok_ev.append(tev)
            tok_nxt = []
            for o in range(8):
                bk = o % 2
                tmm = E.st("t", [
                    (lambda t, _o=o, _c=c, _bk=bk: t.matmul(
                        ps_ko[_bk][:, :], wk_sb[:, _c, 128 * _o:128 * (_o + 1)],
                        h1gT[:, _c, :], start=(_c == 0), stop=(_c == 7)))
                    for c in range(8)
                ], deps=tok_ev + [war.get(("ps_koA", bk)), tok_trigk])
                if o < 4:
                    dst = uk_sb[:, o, 512 * g:512 * (g + 1)]
                else:
                    dst = ukst[:, o % 2, :]
                tev = E.st("v", lambda v, _bk=bk, _g=g, _dst=dst: v.tensor_tensor(
                    _dst, ps_ko[_bk][:, :], trigk[:, 512 * _g:512 * (_g + 1)],
                    ALU.mult),
                    deps=[tmm, war.get(("ukst", o % 2)) if o >= 4 else None])
                war[("ps_koA", bk)] = tev
                if o >= 4:
                    tst = E.st("y", lambda y, _o=o, _g=g: y.dma_start(
                        ukhi_d[:, _o - 4, 512 * _g:512 * (_g + 1)],
                        ukst[:, _o % 2, :]), deps=[tev])
                    war[("ukst", o % 2)] = tst
                if g + 1 < NG:
                    tok_nxt.append(a_accum(g + 1, o))
            if g + 1 < NG:
                trs_n = a_finish(g + 1, tok_nxt)
            for j in range(4):
                kc = 4 * g + j
                fns = []
                for half in range(2):
                    for c in range(8):
                        fns.append(lambda t, _j=j, _c=c, _h=half: t.matmul(
                            ps_v[:, 512 * _h:512 * (_h + 1)],
                            h1gT[:, _c, 128 * _j:128 * (_j + 1)],
                            wv_sb[:, _c, 512 * _h:512 * (_h + 1)],
                            start=(_c == 0), stop=(_c == 7)))
                tvm = E.st("t", fns, deps=tok_ev + [war.get("ps_vA")])
                tlo = E.st("v", lambda v, _kc=kc: v.tensor_copy(
                    vbuf[:, _kc, :, 0:64],
                    ps_v[:, 0:512].rearrange("p (h d) -> p h d", d=64)),
                    deps=[tvm, tok_vones])
                thi = E.st("v", lambda v, _kc=kc: v.tensor_copy(
                    vst[:, _kc % 2, :], ps_v[:, 512:1024]),
                    deps=[tvm, war.get(("vst", kc % 2))])
                war["ps_vA"] = thi
                tvs = E.st("y", lambda y, _kc=kc: y.dma_start(
                    vhi_d[128 * _kc:128 * (_kc + 1), :], vst[:, _kc % 2, :]),
                    deps=[thi])
                war[("vst", kc % 2)] = tvs
                if g + 1 < NG:
                    tok_norm.setdefault(g + 1, []).append(
                        a_norm(g + 1, j, [trs_n, tok_ev[-1]]))
        tok_A_ukst = (war[("ukst", 0)], war[("ukst", 1)])
        tok_A_vst = (war[("vst", 0)], war[("vst", 1)])
        E.run(nc)
        a_ps.close()
        a_scope.close()
        set_gate()

        p_scope = ExitStack()
        ap_scope.enter_context(p_scope)
        wo_sb = sb("wo_sb", [128, 8, 1024], BF16, p_scope)
        ctxF = sb("ctxF", [128, 8, OWN], BF16, p_scope)
        ctx0T = sb("ctx0T", [128, 16], BF16, p_scope)
        pa_scope = ExitStack()
        p_scope.enter_context(pa_scope)
        wq_sb = sb("wq_sb", [128, 8, 1024], BF16, pa_scope)
        trigq = sb("trigq", [128, W1], BF16, pa_scope)
        uqt = sb("uqt", [128, 4, 512], BF16, pa_scope)
        xqg = sb("xqg", [128, 4, 1024], F32, pa_scope)
        h1q = sb("h1q", [128, 4, 1024], BF16, pa_scope)
        h1qT = sb("h1qT", [128, 8, 512], BF16, pa_scope)
        pT = sb("pT", [128, 4, 1024], BF16, pa_scope)
        rl = sb("rl", [128, 1], F32, pa_scope)
        rden = sb("rden", [128, 4], F32, pa_scope)
        ctx_sb = sb("ctx_sb", [128, 4, 64], BF16, pa_scope)
        h1q0 = sb("h1q0", [1, 1024], BF16, pa_scope)
        h1q0T = sb("h1q0T", [128, 8], BF16, pa_scope)
        U0c = sb("U0c", [128, 4, 8], BF16, pa_scope)
        P0 = sb("P0", [8, SKV], BF16, pa_scope)
        P0T = sb("P0T", [128, KC, 8], BF16, pa_scope)
        vtmp = sb("vtmp", [128, 2, 2, 512], BF16, pa_scope)
        c0row = sb("c0row", [1, 512], BF16, pa_scope)
        rc0 = sb("rc0", [8, 1], F32, pa_scope)

        pp_ps = ExitStack()
        pa_scope.enter_context(pp_ps)
        ps_s = ps("ps_s", [128, 2048], F32, pp_ps)
        ps_av = [ps(f"ps_av{i}", [128, 512], F32, pp_ps) for i in range(4)]

        for ppass in range(2):
            E = em()
            if ppass == 0:
                tok_w = E.st("g", lambda g: g.dma_start(
                    wq_sb[:, :, :], wq.rearrange("(c p) o -> p c o", p=128)))
                tok_trq = E.st("g", lambda g: g.dma_start(trigq[:, :], trigq_i))
                tok_ukrdy = tok_w
                tok_vrdy = tok_vones
            else:
                tok_w = E.st("y", [
                    (lambda y, _j=j: y.dma_start(uk_sb[:, _j, :], ukhi_d[:, _j, :]))
                    for j in range(4)
                ], deps=[tok_A_ukst[0], tok_A_ukst[1], war.get("uk_rd")])
                tok_ukrdy = tok_w
                vt = None
                for b2 in range(KC // 2):
                    sl2 = b2 % 2
                    tvd = E.st("y", lambda y, _b=b2, _s=sl2: y.dma_start(
                        vtmp[:, _s, :, :],
                        vhi_d[256 * _b:256 * (_b + 1), :].rearrange(
                            "(a p) o -> p a o", p=128)),
                        deps=[tok_A_vst[0], tok_A_vst[1],
                              war.get(("vtmp", sl2))])
                    vt = E.st("v", [
                        (lambda v, _b=b2, _s=sl2, _i=i: v.tensor_copy(
                            vbuf[:, 2 * _b + _i, :, 0:64],
                            vtmp[:, _s, _i, :].rearrange(
                                "p (h d) -> p h d", d=64)))
                        for i in range(2)
                    ], deps=[tvd, war.get("vbuf_rd")])
                    war[("vtmp", sl2)] = vt
                tok_vrdy = vt
                tok_trq = None

            for qt in range(NQT):
                r0 = 1 + QT * qt
                tq = E.st("y", lambda y, _r0=r0: y.dma_start(
                    xqg[:, :, :],
                    xq[_r0:_r0 + QT, :].rearrange("(a p) o -> p a o", p=128)),
                    deps=[war.get("xqg"), tok_trq])
                trs = ln_stats(E, lambda j: xqg[:, j, :], 4, [[tq]] * 4)
                tok_norm = []
                for j in range(4):
                    tok_norm.append(E.st("v", lambda v, _j=j: v.tensor_scalar(
                        h1q[:, _j, :], xqg[:, _j, :], mcol(_j), rcol(_j),
                        ALU.subtract, ALU.mult), deps=[trs]))
                war["xqg"] = tok_norm[-1]
                tok_ev = []
                for j in range(4):
                    hfj = j % 2
                    tt = E.st("t", [
                        (lambda t, _j=j, _c=c, _hf=hfj: t.matmul(
                            ps_s[:, 1024 * _hf + 128 * _c:
                                 1024 * _hf + 128 * (_c + 1)],
                            h1q[:, _j, 128 * _c:128 * (_c + 1)], identb[:, :],
                            start=True, stop=True))
                        for c in range(8)
                    ], deps=[tok_norm[j], war.get(("ps_s", hfj))])
                    tev = E.st("v", lambda v, _j=j, _hf=hfj: v.tensor_copy(
                        h1qT[:, :, 128 * _j:128 * (_j + 1)],
                        ps_s[:, 1024 * _hf:1024 * (_hf + 1)].rearrange(
                            "p (c q) -> p c q", q=128)),
                        deps=[tt])
                    war[("ps_s", hfj)] = tev
                    tok_ev.append(tev)
                tok_uq = []
                for oo in range(4):
                    o = 4 * ppass + oo
                    tmm = E.st("t", [
                        (lambda t, _o=o, _c=c, _oo=oo: t.matmul(
                            ps_av[_oo][:, :],
                            wq_sb[:, _c, 128 * _o:128 * (_o + 1)],
                            h1qT[:, _c, :], start=(_c == 0), stop=(_c == 7)))
                        for c in range(8)
                    ], deps=tok_ev + [war.get(("ps_av", oo)), tok_w])
                    tev = E.st("v", lambda v, _oo=oo, _r0=r0: v.tensor_tensor(
                        uqt[:, _oo, :], ps_av[_oo][:, :],
                        trigq[:, _r0:_r0 + QT],
                        ALU.mult), deps=[tmm, war.get("uqt")])
                    war[("ps_av", oo)] = tev
                    tok_uq.append(tev)

                for hh in range(8):
                    h = 8 * ppass + hh
                    oo, m = hh // 2, hh % 2
                    tok_exp = {}
                    tok_pv = {}
                    def _emit_pv(_gg):
                        _sl = _gg % 4
                        tok_pv[_gg] = E.st("t", [
                            (lambda t, _jj=jj, _qq=qq, _g=_gg, _hh=hh, _s=_sl:
                             t.matmul(
                                ps_av[_qq][:, 0:65],
                                pT[:, _s, 512 * _jj + 128 * _qq:
                                   512 * _jj + 128 * _qq + 128],
                                vbuf[:, 2 * _g + _jj, _hh, :],
                                start=(2 * _g + _jj == 0),
                                stop=(2 * _g + _jj == KC - 1)))
                            for jj in range(2) for qq in range(4)
                        ], deps=[tok_exp[_gg], tok_vrdy]
                           + ([war.get(("ps_av", q2)) for q2 in range(4)]
                              if _gg == 0 else []))
                        war[("pT", _sl)] = tok_pv[_gg]

                    def _emit_qk_exp(_gg):
                        _hf = _gg % 2
                        _sl = _gg % 4
                        tqk = E.st("t", [
                            (lambda t, _jj=jj, _g=_gg, _oo=oo, _m=m, _h2=_hf:
                             t.matmul(
                                ps_s[:, 1024 * _h2 + 512 * _jj:
                                     1024 * _h2 + 512 * (_jj + 1)],
                                uk_sb[64 * _m:64 * _m + 64, _oo,
                                      128 * (2 * _g + _jj):
                                      128 * (2 * _g + _jj) + 128],
                                uqt[64 * _m:64 * _m + 64, _oo, :],
                                start=True, stop=True,
                                tile_position=(64 * _m, 0)))
                            for jj in range(2)
                        ], deps=[tok_uq[oo], tok_ukrdy,
                                 war.get(("ps_s", _hf))])
                        tok_exp[_gg] = E.st("s", lambda s, _s2=_sl, _h=h,
                                            _h2=_hf:
                            s.activation(
                                pT[:, _s2, :],
                                ps_s[:, 1024 * _h2:1024 * (_h2 + 1)], AF.Exp,
                                scale=0.125),
                            deps=[tqk, war.get(("pT", _sl))])
                        war[("ps_s", _hf)] = tok_exp[_gg]

                    _emit_qk_exp(0)
                    _emit_qk_exp(1)
                    for gg in range(2, 16):
                        _emit_qk_exp(gg)
                        _emit_pv(gg - 2)
                    _emit_pv(14)
                    _emit_pv(15)
                    chunk = 4 * ppass + hh // 2
                    tok_dv = []
                    for qq in range(4):
                        tdv = E.st("v", [
                            (lambda v, _qq=qq: v.reciprocal(
                                rden[:, _qq:_qq + 1], ps_av[_qq][:, 64:65])),
                            (lambda v, _qq=qq: v.tensor_scalar(
                                ctx_sb[:, _qq, :], ps_av[_qq][:, 0:64],
                                rden[:, _qq:_qq + 1], 0.0, ALU.mult, ALU.add)),
                        ], deps=[tok_pv[15]])
                        war[("ps_av", qq)] = tdv
                        tok_dv.append(tdv)
                    ttx = E.st("t", [
                        (lambda t, _qq=qq, _m=m: t.matmul(
                            ps_av[_qq][64 * _m:64 * _m + 64, 128:256],
                            ctx_sb[:, _qq, :], identb[:, :],
                            start=True, stop=True))
                        for qq in range(4)
                    ], deps=tok_dv)
                    tcx = E.st("v", [
                        (lambda v, _ck=chunk, _qt=qt, _m=m, _qq=qq:
                         v.tensor_copy(
                             ctxF[64 * _m:64 * _m + 64, _ck,
                                  QT * _qt + 128 * _qq:
                                  QT * _qt + 128 * (_qq + 1)],
                             ps_av[_qq][64 * _m:64 * _m + 64, 128:256]))
                        for qq in range(4)
                    ], deps=[ttx])
                    for qq in range(4):
                        war[("ps_av", qq)] = tcx
                war["uqt"] = tok_pv[15]

            trs0 = ln_stats(E, lambda j: xq0[0:1, :], 1, [[tok_const]], cb=8, pp=1)
            tn0 = E.st("v", lambda v: v.tensor_scalar(
                h1q0[:, :], xq0[0:1, :], stat[0:1, MM + 8:MM + 9],
                stat[0:1, RS + 8:RS + 9], ALU.subtract, ALU.mult), deps=[trs0])
            tt0 = E.st("t", [
                (lambda t, _c=c: t.matmul(
                    ps_av[0][:, _c:_c + 1], h1q0[0:1, 128 * _c:128 * (_c + 1)],
                    identb[0:1, 0:1], start=True, stop=True))
                for c in range(8)
            ], deps=[tn0, war.get(("ps_av", 0))])
            tev0 = E.st("v", lambda v: v.tensor_copy(
                h1q0T[:, 0:8], ps_av[0][:, 0:8]),
                deps=[tt0])
            war[("ps_av", 0)] = tev0
            tu0m = E.st("v", lambda v: v.memset(U0c[:, :, :], 0.0),
                        deps=[war.get("U0c_rd"), war.get("dbg_g")])
            tok_u0 = []
            for oo in range(4):
                o = 4 * ppass + oo
                tmm = E.st("t", [
                    (lambda t, _o=o, _c=c: t.matmul(
                        ps_av[1][:, 0:1], wq_sb[:, _c, 128 * _o:128 * (_o + 1)],
                        h1q0T[:, _c:_c + 1], start=(_c == 0), stop=(_c == 7)))
                    for c in range(8)
                ], deps=[tev0, war.get(("ps_av", 1))])
                tum = E.st("v", lambda v: v.tensor_tensor(
                    rl[:, 0:1], ps_av[1][:, 0:1], trigq[:, 0:1], ALU.mult),
                    deps=[tmm])
                war[("ps_av", 1)] = tum
                for jj in range(2):
                    tok_u0.append(E.st("v", lambda v, _oo=oo, _jj=jj:
                                       v.tensor_scalar(
                                           U0c[:, _oo, 2 * _oo + _jj:2 * _oo + _jj + 1],
                                           rl[:, 0:1], u0m[:, _jj:_jj + 1], 0.0,
                                           ALU.mult, ALU.add),
                                       deps=[tum, tu0m]))
            if dbg:
                war["dbg_g"] = E.st("g", lambda g, _p=ppass: g.dma_start(
                    dU0c_d[_p], U0c[:, :, :]), deps=tok_u0)
            tok_p0 = []
            tsm = None
            for c8 in range(8):
                tsm = E.st("t", [
                    (lambda t, _oo=oo, _c8=c8: t.matmul(
                        ps_av[2][0:8, :], U0c[:, _oo, :],
                        uk_sb[:, _oo, 512 * _c8:512 * (_c8 + 1)],
                        start=(_oo == 0), stop=(_oo == 3)))
                    for oo in range(4)
                ], deps=tok_u0 + [war.get(("ps_av", 2)), tok_ukrdy])
                texp = E.st("s", lambda s, _c8=c8: s.activation(
                    P0[:, 512 * _c8:512 * (_c8 + 1)], ps_av[2][0:8, :], AF.Exp,
                    scale=0.125),
                    deps=[tsm, war.get("P0_rdv"), war.get("P0_rdt"),
                          war.get("dbg_g")])
                war[("ps_av", 2)] = texp
                tok_p0.append(texp)
            tnp0 = E.st("v", [
                lambda v: v.tensor_reduce(rc0[:, :], P0[:, :], AX.X, ALU.add),
                lambda v: v.reciprocal(rc0[:, :], rc0[:, :]),
                lambda v: v.tensor_scalar(P0[:, :], P0[:, :], rc0[:, :], 0.0,
                                          ALU.mult, ALU.add),
            ], deps=tok_p0)
            if dbg:
                war["dbg_g"] = E.st("g", lambda g, _p=ppass: g.dma_start(
                    dP0_d[_p], P0[:, :]), deps=[tnp0])
            war["P0_rdv"] = tnp0
            war["U0c_rd"] = tsm
            war["uk_rd"] = tsm
            ttr0 = E.st("t", [
                (lambda t, _kc=kc: t.matmul(
                    ps_av[3][:, 8 * _kc:8 * (_kc + 1)],
                    P0[:, 128 * _kc:128 * (_kc + 1)], identb[0:8, 0:8],
                    start=True, stop=True))
                for kc in range(KC)
            ], deps=[tnp0, war.get(("ps_av", 3))])
            war["P0_rdt"] = ttr0
            tp0e = E.st("v", lambda v: v.tensor_copy(
                P0T[:, :, :],
                ps_av[3][:, 0:256].rearrange("p (k e) -> p k e", e=8)),
                deps=[ttr0, war.get("P0T_rd")])
            war[("ps_av", 3)] = tp0e
            tpv = E.st("t", [
                (lambda t, _kc=kc, _j=j: t.matmul(
                    ps_av[0][0:1, 64 * _j:64 * _j + 64],
                    P0T[:, _kc, _j:_j + 1], vbuf[:, _kc, _j, 0:64],
                    start=(_kc == 0), stop=(_kc == KC - 1)))
                for j in range(8) for kc in range(KC)
            ], deps=[tp0e, tok_vrdy, war.get(("ps_av", 0))])
            war["P0T_rd"] = tpv
            war["vbuf_rd"] = tpv
            tc0 = E.st("v", lambda v: v.tensor_copy(
                c0row[0:1, :], ps_av[0][0:1, 0:512]),
                deps=[tpv, war.get("dbg_g")])
            war[("ps_av", 0)] = tc0
            if dbg:
                war["dbg_g"] = E.st("g", lambda g, _p=ppass: g.dma_start(
                    dc0_d[_p], c0row[0:1, :]), deps=[tc0])
            ttc = E.st("t", [
                (lambda t, _h=h, _half=half: t.matmul(
                    ps_av[1][64 * _half:64 * _half + 64,
                             8 * _half + _h:8 * _half + _h + 1],
                    c0row[0:1, 64 * _h:64 * _h + 64], identb[0:1, 0:1],
                    start=True, stop=True))
                for half in range(2) for h in range(8)
            ], deps=[tc0, war.get(("ps_av", 1))])
            tce = E.st("v", [
                lambda v, _p=ppass: v.tensor_copy(
                    ctx0T[0:64, 4 * _p:4 * _p + 4].rearrange(
                        "p (c e) -> p c e", e=1),
                    ps_av[1][0:64, 0:8].rearrange(
                        "p (c e) -> p c e", e=2)[:, :, 0:1]),
                lambda v, _p=ppass: v.tensor_copy(
                    ctx0T[64:128, 4 * _p:4 * _p + 4].rearrange(
                        "p (c e) -> p c e", e=1),
                    ps_av[1][64:128, 8:16].rearrange(
                        "p (c e) -> p c e", e=2)[:, :, 1:2]),
            ], deps=[ttc])
            war[("ps_av", 1)] = tce
            war["ctx0T"] = tce
            E.run(nc)

        pp_ps.close()
        pa_scope.close()
        set_gate()

        e_scope = ExitStack()
        p_scope.enter_context(e_scope)
        atF = sb("atF", [128, 8, 512], F32, e_scope)
        at_sb = sb("at_sb", [128, 1024], F32, e_scope)
        xqE = sb("xqE", [128, 4, 1024], F32, e_scope)
        x2st = sb("x2st", [128, 2, 4, 1024], F32, e_scope)
        at0 = sb("at0", [1, 1024], F32, e_scope)
        e_ps = ExitStack()
        e_scope.enter_context(e_ps)
        ps_wo = [ps(f"ps_wo{i}", [128, 512], F32, e_ps) for i in range(2)]
        ps_tm = [ps(f"ps_tm{i}", [128, 1024], F32, e_ps) for i in range(2)]

        E = em()
        tok_wo = E.st("g", lambda g: g.dma_start(
            wo_sb[:, :, :], wo.rearrange("(c p) o -> p c o", p=128)))
        colb = sb("colb", [128, 8], F32, e_scope)
        tcl = []
        for o in range(8):
            bk = o % 2
            fns = []
            for c in range(8):
                fns.append(lambda t, _o=o, _c=c, _bk=bk: t.matmul(
                    ps_wo[_bk][:, 0:1],
                    wo_sb[:, _c, 128 * _o:128 * (_o + 1)],
                    ctx0T[:, _c:_c + 1],
                    start=(_c == 0), stop=(_c == 7)))
            tmo = E.st("t", fns, deps=[war["ctx0T"], tok_wo,
                                       war.get(("ps_wo", bk))])
            tce2 = E.st("v", lambda v, _o=o, _bk=bk: v.tensor_copy(
                colb[:, _o:_o + 1], ps_wo[_bk][:, 0:1]), deps=[tmo])
            war[("ps_wo", bk)] = tce2
            tcl.append(tce2)
        t0tr = E.st("t", [
            (lambda t, _o=o: t.matmul(
                ps_tm[0][0:1, 128 * _o:128 * (_o + 1)], colb[:, _o:_o + 1],
                identf[:, :], start=True, stop=True,
                is_transpose=True))
            for o in range(8)
        ], deps=tcl + [war.get(("ps_tm", 0))])
        t0e = E.st("v", lambda v: v.tensor_copy(at0[0:1, :], ps_tm[0][0:1, :]),
                   deps=[t0tr])
        war[("ps_tm", 0)] = t0e
        trs0 = ln_stats(E, lambda j: at0[0:1, :], 1, [[t0e]], cb=8, pp=1)
        tx2r0 = E.st("v", [
            lambda v: v.tensor_scalar(x2r0[:, :], at0[0:1, :],
                                      stat[0:1, MM + 8:MM + 9],
                                      stat[0:1, RS + 8:RS + 9],
                                      ALU.subtract, ALU.mult),
            lambda v: v.tensor_tensor(x2r0[:, :], x2r0[:, :], xq0[0:1, :],
                                      ALU.add),
            lambda v: v.tensor_scalar(x2r0[:, :], x2r0[:, :], bmask[0:1, 0:1],
                                      0.0, ALU.mult, ALU.add),
        ], deps=[trs0, tok_const])
        war["x2r0"] = tx2r0

        for qt in range(NQT):
            r0 = 1 + QT * qt
            tok_nrs = []
            tq = E.st("y", lambda y, _r0=r0: y.dma_start(
                xqE[:, :, :],
                xq[_r0:_r0 + QT, :].rearrange("(a p) o -> p a o", p=128)),
                deps=[war.get("xqE")])
            tok_wom = []
            for o in range(8):
                bk = o % 2
                fns = []
                for c in range(8):
                    fns.append(lambda t, _c=c, _o=o, _bk=bk, _qt=qt:
                               t.matmul(
                                   ps_wo[_bk][:, :],
                                   wo_sb[:, _c, 128 * _o:128 * (_o + 1)],
                                   ctxF[:, _c, QT * _qt:QT * (_qt + 1)],
                                   start=(_c == 0), stop=(_c == 7)))
                tm = E.st("t", fns, deps=[tok_wo, war.get(("ps_wo", bk))])
                tev = E.st("v", lambda v, _o=o, _bk=bk: v.tensor_copy(
                    atF[:, _o, :], ps_wo[_bk][:, :]), deps=[tm])
                war[("ps_wo", bk)] = tev
                tok_wom.append(tev)
            for j in range(4):
                bk = j % 2
                ttr = E.st("t", [
                    (lambda t, _j=j, _o=o, _bk=bk: t.matmul(
                        ps_tm[_bk][:, 128 * _o:128 * (_o + 1)],
                        atF[:, _o, 128 * _j:128 * (_j + 1)], identf[:, :],
                        start=True, stop=True, is_transpose=True))
                    for o in range(8)
                ], deps=tok_wom + [war.get(("ps_tm", bk))])
                tst = E.st("v", lambda v, _bk=bk, _j=j: v.tensor_scalar(
                    at_sb[:, :], ps_tm[_bk][:, :], 1.0, 0.0, ALU.mult, ALU.add,
                    accum_out=stat[:, SX + _j:SX + _j + 1]), deps=[ttr])
                tst2 = E.st("v", lambda v, _j=j: v.scalar_tensor_tensor(
                    junk[:, :], at_sb[:, :], 1.0, at_sb[:, :], ALU.mult,
                    ALU.mult, accum_out=stat[:, SXX + _j:SXX + _j + 1]),
                    deps=[tst])
                war[("ps_tm", bk)] = tst
                tfin = E.st("v", [
                    lambda v, _j=j: v.tensor_scalar(
                        mcol(_j), stat[:, SX + _j:SX + _j + 1], 1.0 / 1024.0,
                        0.0, ALU.mult, ALU.add),
                    lambda v, _j=j: v.tensor_tensor(
                        stat[:, M2 + _j:M2 + _j + 1], mcol(_j), mcol(_j), ALU.mult),
                    lambda v, _j=j: v.tensor_scalar(
                        rcol(_j), stat[:, SXX + _j:SXX + _j + 1], 1.0 / 1024.0,
                        0.0, ALU.mult, ALU.add),
                    lambda v, _j=j: v.tensor_tensor(
                        rcol(_j), rcol(_j), stat[:, M2 + _j:M2 + _j + 1], ALU.subtract),
                ], deps=[tst2, war.get("lnrs")])
                trsE = E.st("s", [
                    lambda s, _j=j: s.activation(rcol(_j), rcol(_j), AF.Ln,
                                                 bias=cbias[:, 0:1]),
                    lambda s, _j=j: s.activation(rcol(_j), rcol(_j), AF.Exp,
                                                 scale=-0.5),
                ], deps=[tfin])
                war["lnrs"] = trsE
                sl = qt % 2
                tnr = E.st("v", [
                    lambda v, _j=j, _sl=sl: v.tensor_scalar(
                        x2st[:, _sl, _j, :], at_sb[:, :], mcol(_j), rcol(_j),
                        ALU.subtract, ALU.mult),
                    lambda v, _j=j, _sl=sl: v.tensor_tensor(
                        x2st[:, _sl, _j, :], x2st[:, _sl, _j, :], xqE[:, _j, :],
                        ALU.add),
                ], deps=[trsE, tq, war.get(("x2st", sl))])
                tok_nrs.append(tnr)
                war["at_sb"] = tnr
            tso = E.st("g", lambda g, _qt=qt, _sl=qt % 2: g.dma_start(
                x2_d[QT * _qt:QT * (_qt + 1), :].rearrange(
                    "(a p) o -> p a o", p=128),
                x2st[:, _sl, :, :]), deps=list(tok_nrs))
            war[("x2st", qt % 2)] = tso
            war["xqE"] = war["at_sb"]
        if dbg:
            E.st("g", [
                lambda g: g.dma_start(dctx_d[:, :, :], ctxF[:, :, :]),
                lambda g: g.dma_start(dx2r0_d, x2r0[:, :]),
                lambda g: g.dma_start(dct_d[:, :], ctx0T[:, 0:8]),
                lambda g: g.dma_start(dat0_d, at0[:, :]),
                lambda g: g.dma_start(dcolb_d, colb[:, :]),
            ], deps=[war["x2r0"]])
        tok_x2_done = [war[("x2st", 0)], war[("x2st", 1)]]
        E.run(nc)
        e_ps.close()
        e_scope.close()
        p_scope.close()
        ap_scope.close()
        set_gate()

        f_scope = ExitStack()
        top.enter_context(f_scope)
        wr_sb = sb("wr_sb", [128, 8, 1024], BF16, f_scope)
        wkf_sb = sb("wkf_sb", [128, 8, 1024], BF16, f_scope)
        wvf_sb = sb("wvf_sb", [128, 8, 1024], BF16, f_scope)
        h2T = sb("h2T", [128, 8, W1], BF16, f_scope)
        ffnout = sb("ffnout", [128, 16, 1024], BF16, f_scope)
        x2t = sb("x2t", [128, 2, 2, 1024], F32, f_scope)
        h2b = sb("h2b", [128, 1024], BF16, f_scope)
        xrT = sb("xrT", [128, 2, 8, 512], BF16, f_scope)
        xkT = sb("xkT", [128, 2, 8, 512], BF16, f_scope)
        kactT = sb("kactT", [128, 8, 512], BF16, f_scope)
        krelu = sb("krelu", [128, 512], BF16, f_scope)
        rt = sb("rt", [128, 1024], F32, f_scope)
        tmp8 = sb("tmp8", [128, 512], BF16, f_scope)
        sx3 = sb("sx3", [128, 16], F32, f_scope)
        sxx3 = sb("sxx3", [128, 16], F32, f_scope)
        m3 = sb("m3", [128, 16], F32, f_scope)
        rstd3 = sb("rstd3", [128, 16], F32, f_scope)
        outst = sb("outst", [128, 2, 2, 1024], F32, f_scope)

        fp_ps = ExitStack()
        f_scope.enter_context(fp_ps)
        ps_tpF = [ps(f"ps_tpF{i}", [128, 1024], BF16, fp_ps) for i in range(2)]
        E = em()
        tok_wf = E.st("g", [
            lambda g: g.dma_start(wr_sb[:, :, :],
                                  wr.rearrange("(c p) o -> p c o", p=128)),
            lambda g: g.dma_start(wkf_sb[:, :, :],
                                  wkf.rearrange("(c p) o -> p c o", p=128)),
            lambda g: g.dma_start(wvf_sb[:, :, :],
                                  wvf.rearrange("(c p) o -> p c o", p=128)),
        ])
        trs0 = ln_stats(E, lambda j: x2r0[0:1, :], 1, [[war["x2r0"]]], cb=8, pp=1)
        th20 = E.st("v", lambda v: v.tensor_scalar(
            h2b[0:1, :], x2r0[0:1, :], stat[0:1, MM + 8:MM + 9],
            stat[0:1, RS + 8:RS + 9], ALU.subtract, ALU.mult), deps=[trs0])
        tt0 = E.st("t", [
            (lambda t, _c=c: t.matmul(
                ps_tpF[0][:, 2 * _c:2 * _c + 1], h2b[0:1, 128 * _c:128 * (_c + 1)],
                identb[0:1, 0:1], start=True, stop=True,
                is_transpose=True))
            for c in range(8)
        ], deps=[th20])
        tev0 = E.st("v", lambda v: v.tensor_copy(
            h2T[:, :, 0:1],
            ps_tpF[0][:, 0:16].rearrange("p (c e) -> p c e", e=2)[:, :, 0:1]),
            deps=[tt0])
        war[("ps_tpF", 0)] = tev0
        for tt_i in range(16):
            b2, sub = tt_i // 2, tt_i % 2
            sl = b2 % 2
            if sub == 0:
                tqd = E.st("y", lambda y, _b2=b2, _sl=sl: y.dma_start(
                    x2t[:, _sl, :, :],
                    x2_d[256 * _b2:256 * (_b2 + 1), :].rearrange(
                        "(a p) o -> p a o", p=128)),
                    deps=tok_x2_done + [war.get(("x2t", sl))])
            trs = ln_stats(E, lambda j, _sl=sl, _su=sub: x2t[:, _sl, _su, :],
                           1, [[tqd]], cb=tt_i % 4)
            tn = E.st("v", lambda v, _sl=sl, _su=sub, _tt=tt_i: v.tensor_scalar(
                h2b[:, :], x2t[:, _sl, _su, :], mcol(_tt % 4), rcol(_tt % 4),
                ALU.subtract, ALU.mult), deps=[trs])
            war[("x2t", sl)] = tn
            bk = tt_i % 2
            ttr = E.st("t", [
                (lambda t, _c=c, _bk=bk: t.matmul(
                    ps_tpF[_bk][:, 128 * _c:128 * (_c + 1)],
                    h2b[:, 128 * _c:128 * (_c + 1)], identb[:, :],
                    start=True, stop=True, is_transpose=True))
                for c in range(8)
            ], deps=[tn, war.get(("ps_tpF", bk))])
            tev = E.st("v", lambda v, _tt=tt_i, _bk=bk: v.tensor_copy(
                h2T[:, :, 1 + 128 * _tt:1 + 128 * (_tt + 1)],
                ps_tpF[_bk][:, :].rearrange("p (c q) -> p c q", q=128)),
                deps=[ttr])
            war[("ps_tpF", bk)] = tev
            war["h2T"] = tev
        E.run(nc)
        fp_ps.close()
        set_gate()

        fa_ps = ExitStack()
        f_scope.enter_context(fa_ps)
        ps_r = ps("ps_r", [128, 1024], F32, fa_ps)
        ps_k = [ps(f"ps_kF{i}", [128, 512], F32, fa_ps) for i in range(2)]
        ps_vo = [ps(f"ps_voF{i}", [128, 1024], F32, fa_ps) for i in range(2)]
        E = em()

        def _ffn_build(_ft):
            _c0 = QT * _ft
            _sl = _ft % 2
            toks = []
            for c in range(8):
                for mu_t, omu_t, dstT in ((mur, omur, xrT), (muk, omuk, xkT)):
                    t1 = E.st("v", lambda v, _c=c, _omu=omu_t, _cc=_c0:
                              v.tensor_scalar(
                                  tmp8[:, :], h2T[:, _c, _cc:_cc + QT],
                                  _omu[:, _c:_c + 1], 0.0, ALU.mult, ALU.add),
                              deps=[war["h2T"], war.get(("xT_rd", _sl))])
                    t2 = E.st("v", lambda v, _c=c, _mu=mu_t, _dst=dstT,
                              _cc=_c0, _s=_sl:
                              v.scalar_tensor_tensor(
                                  _dst[:, _s, _c, :],
                                  h2T[:, _c, _cc + 1:_cc + 1 + QT],
                                  _mu[:, _c:_c + 1], tmp8[:, :], ALU.mult,
                                  ALU.add), deps=[t1])
                    toks.append(t2)
            return toks

        tok_builds = {0: _ffn_build(0)}
        for ft in range(4):
            sl = ft % 2
            tok_x = tok_builds[ft]
            tok_k = []
            for o in range(8):
                bk = o % 2
                tmm = E.st("t", [
                    (lambda t, _o=o, _c=c, _bk=bk, _s=sl: t.matmul(
                        ps_k[_bk][:, :], wkf_sb[:, _c, 128 * _o:128 * (_o + 1)],
                        xkT[:, _s, _c, :], start=(_c == 0), stop=(_c == 7)))
                    for c in range(8)
                ], deps=tok_x + [war.get(("ps_kF", bk)), tok_wf])
                trl = E.st("v", lambda v, _bk=bk: v.tensor_scalar_max(
                    krelu[:, :], ps_k[_bk][:, :], 0.0), deps=[tmm])
                war[("ps_kF", bk)] = trl
                tk2 = E.st("v", lambda v, _o=o: v.tensor_tensor(
                    kactT[:, _o, :], krelu[:, :], krelu[:, :], ALU.mult),
                    deps=[trl])
                tok_k.append(tk2)
            if ft + 1 < 4:
                tok_builds[ft + 1] = _ffn_build(ft + 1)
            for j in range(4):
                slot = 4 * ft + j
                fns = []
                for half in range(2):
                    for c in range(8):
                        fns.append(lambda t, _j=j, _c=c, _h=half, _s=sl:
                                   t.matmul(
                            ps_r[:, 512 * _h:512 * (_h + 1)],
                            xrT[:, _s, _c, 128 * _j:128 * (_j + 1)],
                            wr_sb[:, _c, 512 * _h:512 * (_h + 1)],
                            start=(_c == 0), stop=(_c == 7)))
                tmr = E.st("t", fns, deps=tok_x + [war.get("ps_r"), tok_wf])
                tth = E.st("s", lambda s: s.activation(
                    rt[:, :], ps_r[:, :], AF.Tanh, scale=0.5),
                    deps=[tmr, war.get("rt")])
                war["ps_r"] = tth
                tsg = E.st("v", lambda v: v.tensor_scalar(
                    rt[:, :], rt[:, :], 0.5, 0.5, ALU.mult, ALU.add),
                    deps=[tth])
                bk = j % 2
                fns = []
                for half in range(2):
                    for c in range(8):
                        fns.append(lambda t, _j=j, _c=c, _h=half, _bk=bk:
                                   t.matmul(
                                       ps_vo[_bk][:, 512 * _h:512 * (_h + 1)],
                                       kactT[:, _c, 128 * _j:128 * (_j + 1)],
                                       wvf_sb[:, _c, 512 * _h:512 * (_h + 1)],
                                       start=(_c == 0), stop=(_c == 7)))
                tmv = E.st("t", fns, deps=tok_k + [war.get(("ps_voF", bk)),
                                                   tok_wf])
                war[("xT_rd", sl)] = tmv
                tcf = E.st("v", lambda v, _slot=slot, _bk=bk:
                           v.scalar_tensor_tensor(
                               ffnout[:, _slot, :], ps_vo[_bk][:, :], 1.0,
                               rt[:, :], ALU.mult, ALU.mult,
                               accum_out=sx3[:, _slot:_slot + 1]),
                           deps=[tmv, tsg])
                war[("ps_voF", bk)] = tcf
                war["rt"] = tcf
                tcq = E.st("v", lambda v, _slot=slot: v.scalar_tensor_tensor(
                    junk[:, :], ffnout[:, _slot, :], 1.0, ffnout[:, _slot, :],
                    ALU.mult, ALU.mult,
                    accum_out=sxx3[:, _slot:_slot + 1]), deps=[tcf])
                war[("ffn", slot)] = tcq
        tok_ffn_done = [war[("ffn", s_)] for s_ in range(16)]
        E.run(nc)
        fa_ps.close()
        set_gate()

        E = em()
        tfin = E.st("v", [
            lambda v: v.tensor_scalar(m3[:, :], sx3[:, :], 1.0 / 1024.0, 0.0,
                                      ALU.mult, ALU.add),
            lambda v: v.tensor_tensor(stat[:, M2:M2 + 16], m3[:, :],
                                      m3[:, :], ALU.mult),
            lambda v: v.tensor_scalar(rstd3[:, :], sxx3[:, :], 1.0 / 1024.0,
                                      0.0, ALU.mult, ALU.add),
            lambda v: v.tensor_tensor(rstd3[:, :], rstd3[:, :],
                                      stat[:, M2:M2 + 16], ALU.subtract),
        ], deps=tok_ffn_done + [war.get("lnrs")])
        trs3 = E.st("s", [
            lambda s: s.activation(rstd3[:, :], rstd3[:, :], AF.Ln,
                                   bias=cbias[:, 0:1]),
            lambda s: s.activation(rstd3[:, :], rstd3[:, :], AF.Exp,
                                   scale=-0.5),
        ], deps=[tfin])
        for tt_i in range(16):
            b2, sub = tt_i // 2, tt_i % 2
            sl = b2 % 2
            if sub == 0:
                tqd = E.st("y", lambda y, _b2=b2, _sl=sl: y.dma_start(
                    x2t[:, _sl, :, :],
                    x2_d[256 * _b2:256 * (_b2 + 1), :].rearrange(
                        "(a p) o -> p a o", p=128)),
                    deps=[war.get(("x2tF", sl))])
                tok_fin = []
            tn = E.st("v", [
                lambda v, _tt=tt_i, _sl=sl, _su=sub: v.tensor_scalar(
                    outst[:, _sl, _su, :], ffnout[:, _tt, :],
                    m3[:, _tt:_tt + 1],
                    rstd3[:, _tt:_tt + 1], ALU.subtract, ALU.mult),
                lambda v, _tt=tt_i, _sl=sl, _su=sub: v.tensor_tensor(
                    outst[:, _sl, _su, :], outst[:, _sl, _su, :],
                    x2t[:, _sl, _su, :],
                    ALU.add),
            ], deps=[trs3, tqd, war.get(("outst", sl))])
            war[("x2tF", sl)] = tn
            tok_fin.append(tn)
            if sub == 1:
                tso = E.st("g", lambda g, _b2=b2, _sl=sl: g.dma_start(
                    out_d[256 * _b2:256 * (_b2 + 1), :].rearrange(
                        "(a p) o -> p a o", p=128),
                    outst[:, _sl, :, :]), deps=list(tok_fin))
                war[("outst", sl)] = tso
        E.raw("g", [("sG", counters["sG"])], [lambda g: g.engine_nop()], 0)
        E.run(nc)

    return nc



def _trig(positions):
    import jax
    import jax.numpy as jnp
    cpu = jax.devices("cpu")[0]
    with jax.default_device(cpu):
        pos = jnp.asarray(np.asarray(positions, np.float32))
        freqs = jnp.exp(jnp.arange(0, HD, 2, dtype=jnp.float32)
                        * (-math.log(10000.0) / HD))
        ang = pos[:, None] * freqs[None, :]
        c = np.asarray(jnp.cos(ang)).T.astype(np.float32)
        s = np.asarray(jnp.sin(ang)).T.astype(np.float32)
    t = np.concatenate([c, s], axis=0)
    return np.vstack([t, t]).astype(np.float32)


def _dup_rows(W):
    Wd = np.empty_like(W)
    for h in range(H):
        blk = W[h * HD:(h + 1) * HD]
        Wd[h * HD:h * HD + HD // 2] = blk[0::2]
        Wd[h * HD + HD // 2:(h + 1) * HD] = blk[0::2]
    return np.ascontiguousarray(Wd.T)


def make_in_maps(L0_attn_x, L0_attn_Wq, L0_attn_Wk, L0_attn_Wv, L0_attn_Wo,
                 L0_ffn_W_r, L0_ffn_W_k, L0_ffn_W_v, L0_ffn_mu_r, L0_ffn_mu_k):
    f32 = np.float32
    x = np.ascontiguousarray(np.asarray(L0_attn_x, f32))
    wq_h = _dup_rows(np.asarray(L0_attn_Wq, f32))
    wk_h = _dup_rows(np.asarray(L0_attn_Wk, f32))
    wv_h = np.ascontiguousarray(np.asarray(L0_attn_Wv, f32).T)
    wo_h = np.ascontiguousarray(np.asarray(L0_attn_Wo, f32).T)
    wr_h = np.ascontiguousarray(np.asarray(L0_ffn_W_r, f32).T)
    wkf_h = np.ascontiguousarray(np.asarray(L0_ffn_W_k, f32).T)
    wvf_h = np.ascontiguousarray(np.asarray(L0_ffn_W_v, f32).T)
    mur_h = np.ascontiguousarray(np.asarray(L0_ffn_mu_r, f32).reshape(8, 128).T)
    muk_h = np.ascontiguousarray(np.asarray(L0_ffn_mu_k, f32).reshape(8, 128).T)
    trigk_h = _trig(np.arange(SKV))
    identb_h = np.eye(128, dtype=bfdt)
    identf_h = np.eye(128, dtype=np.float32)
    u0m_h = np.zeros((128, 2), np.float32)
    u0m_h[0:64, 0] = 1
    u0m_h[64:128, 1] = 1
    zrow = np.zeros((1, D), f32)
    trigq0 = np.ascontiguousarray(np.concatenate(
        [np.zeros((128, 1), np.float32), _trig(np.arange(OWN))], axis=1))
    trigq1 = np.ascontiguousarray(_trig(np.arange(OWN - 1, SKV)))

    in_maps = []
    for core in range(8):
        b, s = core // 2, core % 2
        if s == 0:
            xq_h = np.ascontiguousarray(np.concatenate([zrow, x[b, 0:OWN]], 0))
            trigq_h, bm = trigq0, np.zeros((1, 1), f32)
        else:
            xq_h = np.ascontiguousarray(x[b, OWN - 1:SKV])
            trigq_h, bm = trigq1, np.ones((1, 1), f32)
        in_maps.append({
            "xkv": np.ascontiguousarray(x[b]), "xq": xq_h,
            "wq": wq_h, "wk": wk_h, "wv": wv_h, "wo": wo_h,
            "wr": wr_h, "wkf": wkf_h, "wvf": wvf_h,
            "mur": mur_h, "muk": muk_h,
            "trigk": trigk_h, "trigq": trigq_h,
            "identb": identb_h, "identf": identf_h,
            "u0m": u0m_h, "bmask": bm,
        })
    return in_maps


def kernel(L0_attn_x, L0_attn_Wq, L0_attn_Wk, L0_attn_Wv, L0_attn_Wo,
           L0_ffn_W_r, L0_ffn_W_k, L0_ffn_W_v, L0_ffn_mu_r, L0_ffn_mu_k):
    in_maps = make_in_maps(L0_attn_x, L0_attn_Wq, L0_attn_Wk, L0_attn_Wv,
                           L0_attn_Wo, L0_ffn_W_r, L0_ffn_W_k, L0_ffn_W_v,
                           L0_ffn_mu_r, L0_ffn_mu_k)
    nc = build_program()
    res = run_bass_kernel_spmd(nc, in_maps, core_ids=list(range(8)))
    out = np.empty((B, SKV, D), np.asarray(res.results[0]["out"]).dtype)
    for core in range(8):
        b, s = core // 2, core % 2
        out[b, OWN * s:OWN * (s + 1)] = res.results[core]["out"]
    return out


if __name__ == "__main__":
    from reference import setup_inputs
    ins = setup_inputs(0)
    o = kernel(**{k: np.asarray(v) for k, v in ins.items()})
    print(o.shape, o.dtype, float(np.abs(o).mean()))
